# version 12
# baseline (speedup 1.0000x reference)
import numpy as np
import ml_dtypes
import contextlib
import concourse.bass as bass
import concourse.mybir as mybir
from concourse.bass_utils import run_bass_kernel_spmd

F32 = mybir.dt.float32
BF16 = mybir.dt.bfloat16
AF = mybir.ActivationFunctionType
OP = mybir.AluOpType
AX = mybir.AxisListType

D = 1024
NT = 16
TPT = 128
NCH = 8
GSLOT = 17

(C_EPS, C_NHALF, C_QSC, C_NSH, C_HALF, C_NEG1, C_RD, C_ONE, C_QTR,
 C_NQTR, C_QK, C_SH1, C_15, C_X0, C_X1, C_X2) = range(16)

MVNAME = ["sa", "ca", "ga", "sb", "cb", "gb"]

DEBUG_DUMP = []


class _Trk:

    _OPS = {"tensor_copy", "tensor_tensor", "tensor_scalar", "memset",
            "scalar_tensor_tensor", "reciprocal", "reduce_sum", "reduce_max",
            "tensor_reduce", "bn_stats", "bn_aggr", "activation", "copy",
            "add", "mul", "matmul", "transpose"}

    def __init__(self, eng):
        object.__setattr__(self, "_e", eng)
        object.__setattr__(self, "last", None)
        object.__setattr__(self, "_finned", None)
        object.__setattr__(self, "_pend", set())

    @staticmethod
    def _apname(x):
        t = getattr(x, "tensor", None)
        nm = getattr(t, "name", None)
        return nm

    def __getattr__(self, n):
        eng = object.__getattribute__(self, "_e")
        a = getattr(eng, n)
        if not callable(a) or n in ("wait_ge",):
            return a
        trk = self

        def w(*args, **kw):
            if n in _Trk._OPS:
                pend = object.__getattribute__(trk, "_pend")
                names = []
                for x in list(args) + list(kw.values()):
                    nm = _Trk._apname(x)
                    if nm is not None:
                        names.append(nm)
                outs = names[:1]
                if "accum_out" in kw and kw["accum_out"] is not None:
                    nm = _Trk._apname(kw["accum_out"])
                    if nm:
                        outs.append(nm)
                ins = names[1:]
                if any(nm in pend for nm in ins):
                    eng.drain()
                    pend.clear()
                r = a(*args, **kw)
                for nm in outs:
                    pend.add(nm)
            else:
                r = a(*args, **kw)
            if r is not None and hasattr(r, "then_inc"):
                object.__setattr__(trk, "last", r)
            return r
        return w

    def fin(self, sem_, n_=1):
        li = object.__getattribute__(self, "last")
        if li is object.__getattribute__(self, "_finned"):
            object.__getattribute__(self, "_e").drain().then_inc(sem_, n_)
        else:
            li.then_inc(sem_, n_)
            object.__setattr__(self, "_finned", li)



def nbatch(i):
    return 5 if i == 0 else 4


def scb(i):
    return sum(nbatch(j) for j in range(i))


def batch_groups(i, b):
    if i == 0 and b == 0:
        return [0]
    bb = b - 1 if i == 0 else b
    return [1 + 4 * bb + k for k in range(4)]



def _emit_rsqrt(v, consts, out_col, var_ap, eps_bias, scr):
    I32 = mybir.dt.int32
    P = var_ap.shape[0]
    ve = scr[0:P, 0:1]
    t_ = scr[0:P, 1:2]
    v.tensor_scalar(ve, var_ap, eps_bias, None, OP.add)
    v.tensor_tensor(out_col.bitcast(I32), ve.bitcast(I32),
                    consts[0:P, C_SH1:C_SH1 + 1].bitcast(I32), OP.logical_shift_right)
    v.tensor_tensor(out_col.bitcast(I32), consts[0:P, C_QK:C_QK + 1].bitcast(I32),
                    out_col.bitcast(I32), OP.subtract)
    for _ in range(2):
        v.tensor_tensor(t_, out_col, out_col, OP.mult)
        v.tensor_tensor(t_, t_, ve, OP.mult)
        v.tensor_scalar(t_, t_, consts[0:P, C_NHALF:C_NHALF + 1], consts[0:P, C_15:C_15 + 1],
                        OP.mult, OP.add)
        v.tensor_tensor(out_col, out_col, t_, OP.mult)


def build_nc():
    nc = bass.Bass("TRN2", target_bir_lowering=False, detect_race_conditions=False)
    x_h = nc.dram_tensor("x", [NT * TPT, D], F32, kind="ExternalInput").ap()
    xb_h = nc.dram_tensor("xbnd", [1, D], F32, kind="ExternalInput").ap()
    flag_h = nc.dram_tensor("flag", [128, 1], F32, kind="ExternalInput").ap()
    w_h = {}
    for w in ["wq", "wk", "wv", "wr", "wkf", "wvf"]:
        w_h[w] = nc.dram_tensor(w, [D, D], F32, kind="ExternalInput").ap()
    wmod_h = nc.dram_tensor("wmod", [D, 6 * D], F32, kind="ExternalInput").ap()
    crow_h = nc.dram_tensor("crow", [1, D], F32, kind="ExternalInput").ap()
    consts_h = nc.dram_tensor("consts", [128, 16], F32, kind="ExternalInput").ap()
    idn_h = nc.dram_tensor("idn", [128, 128], BF16, kind="ExternalInput").ap()
    qstat_h = nc.dram_tensor("qstat", [64, GSLOT * 128], BF16, kind="ExternalInput").ap()
    kstat_h = nc.dram_tensor("kstat", [64, GSLOT * 128], BF16, kind="ExternalInput").ap()
    bqp_h = nc.dram_tensor("bqp", [64, 16], F32, kind="ExternalInput").ap()
    bkp_h = nc.dram_tensor("bkp", [64, 16], F32, kind="ExternalInput").ap()
    bvp_h = nc.dram_tensor("bvp", [64, 16], F32, kind="ExternalInput").ap()
    bvt_h = nc.dram_tensor("bvt", [128, 64], F32, kind="ExternalInput").ap()
    mur_h = nc.dram_tensor("mur", [128, 8], F32, kind="ExternalInput").ap()
    muk_h = nc.dram_tensor("muk", [128, 8], F32, kind="ExternalInput").ap()
    ones1_h = nc.dram_tensor("ones1", [1, 128], F32, kind="ExternalInput").ap()
    ones1b_h = nc.dram_tensor("ones1b", [1, 128], BF16, kind="ExternalInput").ap()
    onescol_h = nc.dram_tensor("onescol", [128, 1], F32, kind="ExternalInput").ap()
    y_h = nc.dram_tensor("y", [NT * TPT, D], F32, kind="ExternalOutput").ap()

    es = contextlib.ExitStack()
    sbuf = lambda n, s, d: es.enter_context(nc.sbuf_tensor(n, s, d))
    psum = lambda n, s, d: es.enter_context(nc.psum_tensor(n, s, d))
    sem = lambda n: es.enter_context(nc.semaphore(n))
    with es:
        W = {w: sbuf(w + "s", [128, 8 * D], BF16) for w in ["wq", "wk", "wv", "wr", "wkf", "wvf"]}
        wmb = [sbuf(f"wmb{i}", [128, 4 * 512], BF16) for i in range(2)]
        xin = [sbuf(f"xin{i}", [128, D], F32) for i in range(3)]
        x2b_ = [sbuf(f"x2b{i}", [128, D], F32) for i in range(2)]
        yb_ = [sbuf(f"yb{i}", [128, D], F32) for i in range(2)]
        u1 = sbuf("u1", [128, D], BF16)
        u2 = sbuf("u2", [128, D], BF16)
        h1f = sbuf("h1f", [128, D], BF16)
        h2f = sbuf("h2f", [128, 8 * 132], BF16)
        dt_ = sbuf("dt", [128, D], BF16)
        xr = sbuf("xr", [128, D], BF16)
        xk = sbuf("xk", [128, D], BF16)
        rt = sbuf("rt", [128, D], BF16)
        krel = sbuf("krel", [128, D], BF16)
        ksq = sbuf("ksq", [128, D], BF16)
        ffnf = sbuf("ffnf", [128, D], BF16)
        qg = sbuf("qg", [128, GSLOT * 128], BF16)
        kg = sbuf("kg", [128, GSLOT * 128], BF16)
        vg = sbuf("vg", [128, GSLOT * 128], BF16)
        a2u = sbuf("a2u", [128, GSLOT * 128], BF16)
        vbs = sbuf("vbs", [128, GSLOT * 72], BF16)
        on_ = sbuf("on", [128, GSLOT * 64], BF16)
        fma = sbuf("fma", [128, D], BF16)
        rinv = sbuf("rinv", [128, GSLOT], F32)
        idn = sbuf("idn_s", [128, 128], BF16)
        consts = sbuf("consts_s", [128, 16], F32)
        bqp = sbuf("bqp_s", [64, 16], F32)
        bkp = sbuf("bkp_s", [64, 16], F32)
        bvp = sbuf("bvp_s", [64, 16], F32)
        bvt = sbuf("bvt_s", [128, 64], F32)
        murf = sbuf("murf", [128, 8], F32)
        mukf = sbuf("mukf", [128, 8], F32)
        modf = sbuf("modf", [128, 48], F32)
        clsc = sbuf("clsc", [128, 24], F32)
        dga = sbuf("dga", [128, 8 * 128], BF16)
        gbT = sbuf("gbT", [128, D], BF16)
        gbrow = sbuf("gbrow", [1, D], F32)
        gbc = sbuf("gbc", [128, 8], BF16)
        gbT8 = sbuf("gbT8", [8, 128], F32)
        ln_s = sbuf("ln_s", [128, 28], F32)
        ln_c = sbuf("ln_c", [128, 4], F32)
        crow = sbuf("crow_s", [1, D], F32)
        silcb = sbuf("silcb", [1, D], BF16)
        sT = sbuf("sT", [128, 8], BF16)
        modrow = sbuf("modrow", [1, D], F32)
        xbnd = sbuf("xbnd_s", [1, D], F32)
        u1b = sbuf("u1b", [1, D], BF16)
        lb_s = sbuf("lb_s", [1, 14], F32)
        lb_c = sbuf("lb_c", [1, 6], F32)
        h1bf = sbuf("h1bf", [128, 8], BF16)
        xbf = sbuf("xbf", [128, 8], F32)
        atnb = sbuf("atnb", [128, 8], F32)
        x2bf = sbuf("x2bf", [128, 8], F32)
        x2sq = sbuf("x2sq", [128, 8], F32)
        red_b = sbuf("red_b", [128, 2], F32)
        rstdb = sbuf("rstdb", [128, 2], F32)
        h2bf = sbuf("h2bf", [128, 8], BF16)
        flagc = sbuf("flagc", [128, 1], F32)
        ones1 = sbuf("ones1_s", [1, 128], F32)
        ones1b = sbuf("ones1b_s", [1, 128], BF16)
        onescol = sbuf("onescol_s", [128, 1], F32)

        pa = psum("pa", [128, 1024], F32)
        pb = psum("pb", [128, 1024], F32)
        pc = psum("pc", [128, 1024], F32)
        pd0 = psum("pd0", [128, 512], F32)
        pd1 = psum("pd1", [128, 512], F32)

        ld_s, ld_x, ld_w, st_y = sem("ld_s"), sem("ld_x"), sem("ld_w"), sem("st_y")
        p_sT, p_xbf, p_mvh, p_mtr = sem("p_sT"), sem("p_xbf"), sem("p_mvh"), sem("p_mtr")
        p_q, p_k, p_v, p_bqkv = sem("p_q"), sem("p_k"), sem("p_v"), sem("p_bqkv")
        p_sc, p_vt, p_av, p_ont = sem("p_sc"), sem("p_vt"), sem("p_av"), sem("p_ont")
        p_bont, p_bst, p_bbc = sem("p_bont"), sem("p_bst"), sem("p_bbc")
        p_atm, p_u2t, p_u1t = sem("p_atm"), sem("p_u2t"), sem("p_u1t")
        p_r, p_kf, p_vf = sem("p_r"), sem("p_kf"), sem("p_vf")
        g_gbr, p_gbt, v_gbt = sem("g_gbr"), sem("p_gbt"), sem("v_gbt")
        p_gb1, v_gb8 = sem("p_gb1"), sem("v_gb8")
        v_lnb, v_u1b, v_silu, v_sT = sem("v_lnb"), sem("v_u1b"), sem("v_silu"), sem("v_sT")
        v_mve, v_mod, v_dga = sem("v_mve"), sem("v_mod"), sem("v_dga")
        v_xbf, v_h1b = sem("v_xbf"), sem("v_h1b")
        v_ln1, v_u1, v_qe, v_ke, v_ve = sem("v_ln1"), sem("v_u1"), sem("v_qe"), sem("v_ke"), sem("v_ve")
        v_vbe, v_rin, v_asm, v_x2 = sem("v_vbe"), sem("v_rin"), sem("v_asm"), sem("v_x2")
        v_ln2, v_u2, v_xr, v_ffn, v_y = sem("v_ln2"), sem("v_u2"), sem("v_xr"), sem("v_ffn"), sem("v_y")
        v_bev, v_ext, v_h2b = sem("v_bev"), sem("v_ext"), sem("v_h2b")
        s_silu, s_lb1, s_lb2, s_ln = sem("s_silu"), sem("s_lb1"), sem("s_lb2"), sem("s_ln")
        s_h1, s_exp, s_non, s_h2 = sem("s_h1"), sem("s_exp"), sem("s_non"), sem("s_h2")
        s_rt, s_kr, s_asm = sem("s_rt"), sem("s_kr"), sem("s_asm")

        block = es.enter_context(nc.Block())

        def score_pd(i, b):
            if i < 6 or (i == 0 and b == 0):
                return pd0
            return pd0 if b % 2 == 0 else pd1

        def mv_loads(nm):
            return [nm + jh + h for jh in "01" for h in "01"]
        dma_order = (mv_loads("sa") + mv_loads("ca") + ["wq"] + mv_loads("ga")
                     + ["wk", "wv"] + mv_loads("sb") + mv_loads("cb")
                     + ["wr", "wkf", "wvf"] + mv_loads("gb"))
        LW = {nm: 16 * (k + 1) for k, nm in enumerate(dma_order)}

        @block.scalar
        def _(s):
            for dst, src in [(consts[:], consts_h), (flagc[:], flag_h),
                             (crow[:], crow_h), (xbnd[:], xb_h),
                             (bqp[:], bqp_h), (bkp[:], bkp_h), (bvp[:], bvp_h), (bvt[:], bvt_h),
                             (murf[:], mur_h), (mukf[:], muk_h),
                             (idn[:], idn_h), (ones1[:], ones1_h),
                             (ones1b[:], ones1b_h), (onescol[:], onescol_h),
                             (qg[64:128, :], qstat_h), (kg[64:128, :], kstat_h)]:
                s.dma_start(out=dst, in_=src).then_inc(ld_s, 16)
        NLD_S = 16

        @block.sync
        def _(s):
            for i in range(min(3, NT)):
                s.dma_start(out=xin[i % 3][:], in_=x_h[i * TPT:(i + 1) * TPT, :]).then_inc(ld_x, 16)
            for i in range(NT):
                s.wait_ge(v_y, i + 1)
                s.dma_start(out=y_h[i * TPT:(i + 1) * TPT, :], in_=yb_[i % 2][:]).then_inc(st_y, 16)
                j = i + 3
                if j < NT:
                    s.dma_start(out=xin[j % 3][:], in_=x_h[j * TPT:(j + 1) * TPT, :]).then_inc(ld_x, 16)
            s.wait_ge(st_y, 16 * NT)

        @block.gpsimd
        def _(g):
            for nm in dma_order:
                if nm in W:
                    g.dma_start(out=W[nm][:, :].rearrange("p (kc d) -> p kc d", d=D),
                                in_=w_h[nm].rearrange("(kc p) d -> p kc d", p=128)).then_inc(ld_w, 16)
                else:
                    v = MVNAME.index(nm[:2])
                    jh, h = int(nm[2]), int(nm[3])
                    ell = 4 * v + 2 * jh + h
                    if ell >= 2:
                        g.wait_ge(p_mvh, 2 * v + jh)
                    src = wmod_h[h * 512:(h + 1) * 512, v * D + jh * 512: v * D + jh * 512 + 512]
                    g.dma_start(out=wmb[h][:, :].rearrange("p (ic j) -> p ic j", j=512),
                                in_=src.rearrange("(ic p) j -> p ic j", p=128)).then_inc(ld_w, 16)
            g.wait_ge(v_gb8, 1)
            for c in range(NCH):
                g.dma_start(out=gbrow[0:1, c * 128:(c + 1) * 128],
                            in_=gbT8[c:c + 1, :]).then_inc(g_gbr, 16)

        @block.scalar
        def _(s):
            s = _Trk(s)
            s.wait_ge(ld_s, 16 * NLD_S)
            s.activation(modrow[:], crow[:], AF.Silu).then_inc(s_silu, 1)

            def emit_h1(s, i):
                s.wait_ge(p_u1t, i + 1)
                s.wait_ge(v_mod, 2)
                for c in range(NCH):
                    srcp = (pa if i == 0 else pb)[:, c * 128:(c + 1) * 128]
                    s.activation(h1f[:, c * 128:(c + 1) * 128], srcp,
                                 AF.Identity, scale=modf[:, 8 + c:9 + c], bias=modf[:, c:c + 1])
                s.fin(s_h1, 1)

            emit_h1(s, 0)
            for i in range(NT):
                for b in range(nbatch(i)):
                    gl = batch_groups(i, b)
                    s.wait_ge(p_sc, scb(i) + b + 1)
                    pdx = score_pd(i, b)
                    ncols = 128 * len(gl)
                    s.activation(a2u[:, gl[0] * 128: gl[0] * 128 + ncols], pdx[:, 0:ncols],
                                 AF.Exp, scale=consts[:, C_QSC:C_QSC + 1],
                                 bias=consts[:, C_NSH:C_NSH + 1]).then_inc(s_exp, 1)
                    s.wait_ge(p_av, scb(i) + b + 1)
                    s.wait_ge(v_rin, scb(i) + b + 1)
                    for gidx in gl:
                        if gidx == 0:
                            src = pd0[:, 256:320]
                        else:
                            pb_ = (gidx - 1) // 4
                            k = (gidx - 1) % 4
                            cb = (pb_ % 2) * 512 + k * 65
                            src = pb[:, cb:cb + 64]
                        s.activation(on_[:, gidx * 64:(gidx + 1) * 64], src,
                                     AF.Identity, scale=rinv[:, gidx:gidx + 1])
                    s.fin(s_non, 1)
                if i + 1 < NT:
                    emit_h1(s, i + 1)
                s.wait_ge(p_u2t, i + 1)
                s.wait_ge(v_mod, 5)
                for c in range(NCH):
                    if i >= 4:
                        srcp = (pd0 if c < 4 else pd1)[:, (c % 4) * 128:(c % 4 + 1) * 128]
                    else:
                        srcp = pa[:, c * 128:(c + 1) * 128]
                    s.activation(h2f[:, c * 132 + 1: c * 132 + 129], srcp,
                                 AF.Identity, scale=modf[:, 32 + c:33 + c], bias=modf[:, 24 + c:25 + c])
                s.fin(s_h2, 1)
                s.wait_ge(p_r, i + 1)
                s.activation(rt[:], pa[:, :], AF.Tanh, scale=consts[:, C_HALF:C_HALF + 1]).then_inc(s_rt, 1)
                s.wait_ge(p_kf, i + 1)
                s.activation(krel[:], pc[:, :], AF.Relu)
                s.activation(ksq[:], krel[:], AF.Square).then_inc(s_kr, 1)

        @block.vector
        def _(v):
            v = _Trk(v)
            v.wait_ge(ld_s, 16 * NLD_S)
            v.memset(qg[0:64, 0:128], 0.0)
            v.memset(kg[0:64, 0:128], 0.0)
            v.memset(vg[0:64, 0:128], 0.0)
            v.memset(vbs[:, :], 0.0)
            v.memset(vbs[:, :].rearrange("p (s c) -> p s c", c=72)[:, :, 64:65], 1.0)
            v.memset(h2f[:, :].rearrange("p (s c) -> p s c", c=132)[:, :, 0:1], 0.0)
            v.bn_stats(lb_s[:, 0:6], xbnd[0:1, 0:512])
            v.bn_stats(lb_s[:, 6:12], xbnd[0:1, 512:1024])
            v.bn_aggr(lb_s[:, 12:14], lb_s[:, 0:12]).then_inc(v_lnb, 1)
            _emit_rsqrt(v, consts, lb_c[0:1, 0:1], lb_s[0:1, 13:14],
                        consts[0:1, C_EPS:C_EPS + 1], clsc)
            v.tensor_scalar(lb_c[0:1, 5:6], lb_s[0:1, 12:13], lb_c[0:1, 0:1],
                            consts[0:1, C_NEG1:C_NEG1 + 1], OP.mult, OP.mult)
            v.tensor_scalar(u1b[:], xbnd[:], lb_c[0:1, 0:1], lb_c[0:1, 5:6],
                            OP.mult, OP.add).then_inc(v_u1b, 1)
            v.wait_ge(s_silu, 1)
            v.tensor_copy(silcb[:], modrow[:]).then_inc(v_silu, 1)
            v.wait_ge(p_sT, 1)
            v.tensor_copy(sT[:, :], pd0[:, 480:488]).then_inc(v_sT, 1)
            v.wait_ge(p_xbf, 1)
            v.tensor_copy(xbf[:, :], pa[:, 1000:1008]).then_inc(v_xbf, 1)
            for mv in range(6):
                for jh in range(2):
                    v.wait_ge(p_mvh, 2 * mv + jh + 1)
                    v.tensor_copy(modrow[0:1, jh * 512:(jh + 1) * 512], pd1[0:1, 0:512]).then_inc(v_mve, 1)
                v.wait_ge(p_mtr, mv + 1)
                v.tensor_copy(modf[:, 8 * mv:8 * mv + 8], pd1[:, 384:392])
                if mv in (1, 4):
                    v.tensor_scalar(modf[:, 8 * mv:8 * mv + 8], modf[:, 8 * mv:8 * mv + 8],
                                    consts[:, C_ONE:C_ONE + 1], None, OP.add)
                if mv in (2, 5):
                    cf = 8 * mv
                    gsl = modf[:, cf:cf + 8]
                    v.tensor_scalar(clsc[:, 0:8], gsl, 0.0, None, OP.is_ge)
                    v.tensor_scalar(clsc[:, 8:16], gsl, consts[:, C_QTR:C_QTR + 1], None, OP.max)
                    v.tensor_scalar(clsc[:, 16:24], gsl, consts[:, C_NQTR:C_NQTR + 1], None, OP.min)
                    v.tensor_tensor(clsc[:, 8:16], clsc[:, 8:16], clsc[:, 16:24], OP.subtract)
                    v.tensor_tensor(clsc[:, 8:16], clsc[:, 8:16], clsc[:, 0:8], OP.mult)
                    v.tensor_tensor(gsl, clsc[:, 8:16], clsc[:, 16:24], OP.add)
                    if mv == 5:
                        v.tensor_scalar(gsl, gsl, consts[:, C_HALF:C_HALF + 1], None, OP.mult)
                    if mv == 2:
                        for c in range(NCH):
                            v.tensor_scalar(dga[:, c * 128:(c + 1) * 128], idn[:, :],
                                            modf[:, cf + c:cf + c + 1], None, OP.mult)
                    else:
                        v.tensor_copy(gbc[:, :], gsl)
                    v.fin(v_dga, 1)
                v.fin(v_mod, 1)
                if mv == 1:
                    v.tensor_tensor(x2sq[:, 0:8], pa[:, 1008:1016], modf[:, 8:16], OP.mult)
                    v.tensor_tensor(x2sq[:, 0:8], x2sq[:, 0:8], modf[:, 0:8], OP.add)
                    v.tensor_copy(h1bf[:, :], x2sq[:, 0:8]).then_inc(v_h1b, 1)
            def emit_ln1(v, i):
                v.wait_ge(ld_x, 16 * (i + 1))
                v.wait_ge(p_u1t, i)
                v.bn_stats(ln_s[:, 0:6], xin[i % 3][:, 0:512])
                v.bn_stats(ln_s[:, 6:12], xin[i % 3][:, 512:1024])
                v.bn_aggr(ln_s[:, 24:26], ln_s[:, 0:12]).then_inc(v_ln1, 1)
                _emit_rsqrt(v, consts, ln_c[:, 0:1], ln_s[:, 25:26],
                            consts[:, C_EPS:C_EPS + 1], clsc)
                v.tensor_scalar(ln_c[:, 2:3], ln_s[:, 24:25], ln_c[:, 0:1],
                                consts[:, C_NEG1:C_NEG1 + 1], OP.mult, OP.mult)
                v.tensor_scalar(u1[:], xin[i % 3][:], ln_c[:, 0:1], ln_c[:, 2:3],
                                OP.mult, OP.add).then_inc(v_u1, 1)

            v.wait_ge(p_gb1, 1)
            v.tensor_copy(gbT8[:, :], pd1[0:8, 384:512])
            v.fin(v_gb8, 1)
            v.wait_ge(p_gbt, 1)
            v.tensor_copy(gbT[:, 0:512], pd0[:, 0:512])
            v.tensor_copy(gbT[:, 512:1024], pd1[:, 0:512])
            v.fin(v_gbt, 1)

            for i in range(NT):
                if i == 0:
                    emit_ln1(v, 0)
                for (psrc, gdst, bpat, vsem, pwait) in [
                        (pb, qg, bqp, v_qe, p_q), (pc, kg, bkp, v_ke, p_k), (pb, vg, None, v_ve, p_v)]:
                    v.wait_ge(pwait, i + 1)
                    for par in range(2):
                        in3 = psrc[64 * par:64 * par + 64, :].rearrange(
                            "p (mc g t) -> p mc g t", g=16, t=8)
                        out3 = gdst[0:64, :].rearrange("p (g jj t) -> p g jj t", jj=16, t=8)[
                            :, 1:GSLOT, 8 * par:8 * par + 8, :].rearrange("p g j t -> p j g t")
                        if bpat is None:
                            v.tensor_copy(out3, in3)
                        else:
                            bias = bpat[:, 8 * par:8 * par + 8].unsqueeze(2).unsqueeze(3).broadcast_to([64, 8, 16, 8])
                            v.tensor_tensor(out3, in3, bias, OP.add)
                    v.fin(vsem, 1)
                if i == 0:
                    v.wait_ge(p_bqkv, 1)
                    for (tix, gdst, bpat) in [(0, qg, bqp), (1, kg, bkp)]:
                        for par in range(2):
                            in2 = pd0[64 * par:64 * par + 64, 8 * tix:8 * tix + 8]
                            out2 = gdst[0:64, :].rearrange("p (g jj t) -> p g jj t", jj=16, t=8)[
                                :, 0, 8 * par:8 * par + 8, 0]
                            v.tensor_tensor(out2, in2, bpat[:, 8 * par:8 * par + 8], OP.add)
                    for par in range(2):
                        in2 = pd0[64 * par:64 * par + 64, 16:24]
                        out2 = vg[0:64, :].rearrange("p (g jj t) -> p g jj t", jj=16, t=8)[
                            :, 0, 8 * par:8 * par + 8, 0]
                        v.tensor_copy(out2, in2)
                    v.fin(v_bev, 1)
                for b in range(nbatch(i)):
                    gl = batch_groups(i, b)
                    v.wait_ge(p_vt, scb(i) + b + 1)
                    if len(gl) == 1:
                        v.tensor_tensor(vbs[:, 0:64], pa[:, 256:320], bvt[:, :], OP.add)
                    else:
                        pb_ = b - 1 if i == 0 else b
                        base = (pb_ % 2) * 512
                        dstv = vbs[:, :].rearrange("p (s c) -> p s c", c=72)[:, gl[0]:gl[0] + 4, 0:64]
                        srcv = pa[:, base:base + 256].rearrange("p (kk c) -> p kk c", c=64)
                        bias4 = bvt[:, :].unsqueeze(1).broadcast_to([128, 4, 64])
                        v.tensor_tensor(dstv, srcv, bias4, OP.add)
                    v.fin(v_vbe, 1)
                    v.wait_ge(p_av, scb(i) + b + 1)
                    if len(gl) == 1:
                        v.reciprocal(rinv[:, 0:1], pd0[:, 320:321])
                    else:
                        pb_ = b - 1 if i == 0 else b
                        base = (pb_ % 2) * 512 + 64
                        src_s = pb[:, base - 64:base - 64 + 260].rearrange("p (s c) -> p s c", c=65)[:, :, 64]
                        v.reciprocal(rinv[:, gl[0]:gl[0] + 4], src_s)
                    v.fin(v_rin, 1)
                    pb_h = b - 1 if i == 0 else b
                    if pb_h in (1, 3) and len(gl) == 4:
                        hh = pb_h // 2
                        bofs = 1 if i == 0 else 0
                        v.wait_ge(p_ont, scb(i) + bofs + 2 * (hh + 1))
                        for par in range(2):
                            srca = pc[0:64, :].rearrange(
                                "p (g c2 pr t) -> p c2 pr g t", t=8, pr=2, c2=8)[:, :, par, :, :]
                            dsta = fma[64 * par:64 * par + 64, :].rearrange(
                                "p (c h2 g t) -> p c h2 g t", h2=2, g=8, t=8)[:, :, hh, :, :]
                            v.tensor_copy(dsta, srca)
                        v.fin(v_asm, 1)
                        if pb_h == 1 and i + 1 < NT:
                            emit_ln1(v, i + 1)
                    if i == 0 and b == 0:
                        v.wait_ge(p_bont, 1)
                        for par in range(2):
                            src2 = pd0[0:64, 128:256].rearrange("p (c2 pr t) -> p c2 pr t", pr=2, t=8)[:, :, par, 0]
                            v.tensor_copy(atnb[64 * par:64 * par + 64, 0:8], src2)
                        v.wait_ge(v_dga, 1)
                        v.tensor_tensor(atnb[:], atnb[:], modf[:, 16:24], OP.mult)
                        v.wait_ge(v_xbf, 1)
                        v.tensor_tensor(x2bf[:], xbf[:], atnb[:], OP.add)
                        v.tensor_tensor(x2sq[:], x2bf[:], x2bf[:], OP.mult)
                        v.reduce_sum(red_b[:, 0:1], x2bf[:], axis=AX.X)
                        v.reduce_sum(red_b[:, 1:2], x2sq[:], axis=AX.X)
                        v.fin(v_ext, 1)
                        v.wait_ge(p_bst, 1)
                        v.tensor_scalar(lb_c[0:1, 2:3], pa[0:1, 992:993], consts[0:1, C_RD:C_RD + 1],
                                        None, OP.mult)
                        v.tensor_tensor(lb_c[0:1, 3:4], lb_c[0:1, 2:3], lb_c[0:1, 2:3], OP.mult)
                        v.tensor_scalar(lb_c[0:1, 3:4], lb_c[0:1, 3:4], consts[0:1, C_NEG1:C_NEG1 + 1],
                                        consts[0:1, C_EPS:C_EPS + 1], OP.mult, OP.add)
                        v.tensor_scalar(lb_c[0:1, 4:5], pa[0:1, 993:994],
                                        consts[0:1, C_RD:C_RD + 1], None, OP.mult)
                        v.tensor_tensor(lb_c[0:1, 4:5], lb_c[0:1, 4:5], lb_c[0:1, 3:4], OP.add)
                        _emit_rsqrt(v, consts, lb_c[0:1, 1:2], lb_c[0:1, 4:5], 0.0, clsc)
                        v.fin(v_ext, 1)
                        v.tensor_scalar(lb_c[0:1, 4:5], lb_c[0:1, 2:3], lb_c[0:1, 1:2],
                                        consts[0:1, C_NEG1:C_NEG1 + 1], OP.mult, OP.mult)
                        v.fin(v_ext, 1)
                        v.wait_ge(p_bbc, 1)
                        v.tensor_copy(rstdb[:, 0:2], pa[:, 994:996])
                        v.tensor_scalar(x2bf[:], x2bf[:], rstdb[:, 0:1], rstdb[:, 1:2], OP.mult, OP.add)
                        v.tensor_tensor(x2bf[:], x2bf[:], modf[:, 32:40], OP.mult)
                        v.tensor_tensor(x2bf[:], x2bf[:], modf[:, 24:32], OP.add)
                        v.tensor_scalar(h2bf[:], x2bf[:], flagc[:, 0:1], None, OP.mult)
                        v.tensor_copy(h2f[:, :].rearrange("p (c w) -> p c w", w=132)[:, :, 0:1],
                                      h2bf[:, :].unsqueeze(2))
                        v.fin(v_h2b, 1)
                v.wait_ge(p_atm, i + 1)
                v.tensor_tensor(x2b_[i % 2][:], xin[i % 3][:], pa[:, :], OP.add).then_inc(v_x2, 1)
                v.bn_stats(ln_s[:, 12:18], x2b_[i % 2][:, 0:512])
                v.bn_stats(ln_s[:, 18:24], x2b_[i % 2][:, 512:1024])
                v.bn_aggr(ln_s[:, 26:28], ln_s[:, 12:24]).then_inc(v_ln2, 1)
                _emit_rsqrt(v, consts, ln_c[:, 1:2], ln_s[:, 27:28],
                            consts[:, C_EPS:C_EPS + 1], clsc)
                v.tensor_scalar(ln_c[:, 2:3], ln_s[:, 26:27], ln_c[:, 1:2],
                                consts[:, C_NEG1:C_NEG1 + 1], OP.mult, OP.mult)
                v.tensor_scalar(u2[:], x2b_[i % 2][:], ln_c[:, 1:2], ln_c[:, 2:3],
                                OP.mult, OP.add).then_inc(v_u2, 1)
                v.wait_ge(s_h2, i + 1)
                if i == 0:
                    v.wait_ge(v_h2b, 1)
                v.tensor_tensor(dt_[:, :].rearrange("p (c t) -> p c t", t=128),
                                h2f[:, :].rearrange("p (c w) -> p c w", w=132)[:, :, 1:129],
                                h2f[:, :].rearrange("p (c w) -> p c w", w=132)[:, :, 0:128],
                                OP.subtract)
                for c in range(NCH):
                    v.scalar_tensor_tensor(xr[:, c * 128:(c + 1) * 128], dt_[:, c * 128:(c + 1) * 128],
                                           murf[:, c:c + 1], h2f[:, c * 132:c * 132 + 128],
                                           OP.mult, OP.add)
                for c in range(NCH):
                    v.scalar_tensor_tensor(xk[:, c * 128:(c + 1) * 128], dt_[:, c * 128:(c + 1) * 128],
                                           mukf[:, c:c + 1], h2f[:, c * 132:c * 132 + 128],
                                           OP.mult, OP.add)
                if i < NT - 1:
                    v.tensor_copy(h2f[:, :].rearrange("p (c w) -> p c w", w=132)[:, :, 0:1],
                                  h2f[:, :].rearrange("p (c w) -> p c w", w=132)[:, :, 128:129])
                v.fin(v_xr, 1)
                v.wait_ge(p_vf, i + 1)
                v.scalar_tensor_tensor(ffnf[:], rt[:], 1.0, pa[:, :], OP.add, OP.mult).then_inc(v_ffn, 1)
                if i >= 2:
                    v.wait_ge(st_y, 16 * (i - 1))
                v.tensor_tensor(yb_[i % 2][:], ffnf[:], gbT[:, :], OP.mult)
                v.tensor_tensor(yb_[i % 2][:], yb_[i % 2][:], x2b_[i % 2][:], OP.add).then_inc(v_y, 1)

        @block.tensor
        def _(t):
            t = _Trk(t)
            t.wait_ge(v_silu, 1)
            for c in range(NCH):
                t.matmul(pd0[:, 480 + c:481 + c], silcb[0:1, c * 128:(c + 1) * 128],
                         ones1b[0:1, 0:1], start=True, stop=True)
            t.fin(p_sT, 1)
            for c in range(NCH):
                t.matmul(pa[:, 1000 + c:1001 + c], xbnd[0:1, c * 128:(c + 1) * 128],
                         ones1[0:1, 0:1], start=True, stop=True)
            t.wait_ge(v_u1b, 1)
            for c in range(NCH):
                t.matmul(pa[:, 1008 + c:1009 + c], u1b[0:1, c * 128:(c + 1) * 128],
                         ones1b[0:1, 0:1], start=True, stop=True)
            t.fin(p_xbf, 1)
            t.wait_ge(v_sT, 1)
            for mv in range(6):
                if mv > 0:
                    t.wait_ge(v_mod, mv)
                for jh in range(2):
                    t.wait_ge(ld_w, LW[MVNAME[mv] + str(jh) + "1"])
                    if jh == 1:
                        t.wait_ge(v_mve, 2 * mv + 1)
                    for ic in range(8):
                        t.matmul(pd1[0:1, 0:512], sT[:, ic:ic + 1],
                                 wmb[ic // 4][:, (ic % 4) * 512:(ic % 4) * 512 + 512],
                                 start=(ic == 0), stop=(ic == 7))
                    t.fin(p_mvh, 1)
                t.wait_ge(v_mve, 2 * mv + 2)
                for c in range(NCH):
                    t.matmul(pd1[:, 384 + c:385 + c], modrow[0:1, c * 128:(c + 1) * 128],
                             ones1[0:1, 0:1], start=True, stop=True)
                t.fin(p_mtr, 1)
            t.wait_ge(v_dga, 2)
            t.matmul(pd1[0:8, 384:512], gbc[:, :], idn[:, :], start=True, stop=True)
            t.fin(p_gb1, 1)
            t.wait_ge(g_gbr, 16 * 8)
            t.matmul(pd0[:, 0:512], ones1[0:1, :], gbrow[0:1, 0:512], start=True, stop=True)
            t.matmul(pd1[:, 0:512], ones1[0:1, :], gbrow[0:1, 512:1024], start=True, stop=True)
            t.fin(p_gbt, 1)
            t.wait_ge(v_gbt, 1)
            def emit_u1t(t, i):
                t.wait_ge(v_u1, i + 1)
                if i == 0:
                    t.wait_ge(v_h1b, 1)
                    t.wait_ge(v_xbf, 1)
                    for c in range(NCH):
                        t.matmul(pa[:, c * 128:(c + 1) * 128], u1[:, c * 128:(c + 1) * 128],
                                 idn[:, :], start=True, stop=True)
                else:
                    t.wait_ge(v_rin, scb(i))
                    t.wait_ge(s_non, scb(i))
                    t.wait_ge(v_vbe, scb(i))
                    for c in range(NCH):
                        t.matmul(pb[:, c * 128:(c + 1) * 128], u1[:, c * 128:(c + 1) * 128],
                                 idn[:, :], start=True, stop=True)
                t.fin(p_u1t, 1)

            def emit_q(t, i):
                t.wait_ge(s_h1, i + 1)
                t.wait_ge(ld_w, LW["wq"])
                for mc in range(NCH):
                    for kc in range(NCH):
                        t.matmul(pb[:, mc * 128:(mc + 1) * 128],
                                 W["wq"][:, kc * D + mc * 128: kc * D + (mc + 1) * 128],
                                 h1f[:, kc * 128:(kc + 1) * 128],
                                 start=(kc == 0), stop=(kc == 7))
                        if i == 0:
                            t.matmul(pd0[:, 0 + mc:1 + mc],
                                     W["wq"][:, kc * D + mc * 128: kc * D + (mc + 1) * 128],
                                     h1bf[:, kc:kc + 1], start=(kc == 0), stop=(kc == 7))
                t.fin(p_q, 1)

            def emit_k(t, i):
                t.wait_ge(ld_w, LW["wk"])
                if i > 0:
                    t.wait_ge(s_kr, i)
                for mc in range(NCH):
                    for kc in range(NCH):
                        t.matmul(pc[:, mc * 128:(mc + 1) * 128],
                                 W["wk"][:, kc * D + mc * 128: kc * D + (mc + 1) * 128],
                                 h1f[:, kc * 128:(kc + 1) * 128],
                                 start=(kc == 0), stop=(kc == 7))
                        if i == 0:
                            t.matmul(pd0[:, 8 + mc:9 + mc],
                                     W["wk"][:, kc * D + mc * 128: kc * D + (mc + 1) * 128],
                                     h1bf[:, kc:kc + 1], start=(kc == 0), stop=(kc == 7))
                t.fin(p_k, 1)

            def emit_vp(t, i):
                t.wait_ge(ld_w, LW["wv"])
                t.wait_ge(v_qe, i + 1)
                for mc in range(NCH):
                    for kc in range(NCH):
                        t.matmul(pb[:, mc * 128:(mc + 1) * 128],
                                 W["wv"][:, kc * D + mc * 128: kc * D + (mc + 1) * 128],
                                 h1f[:, kc * 128:(kc + 1) * 128],
                                 start=(kc == 0), stop=(kc == 7))
                        if i == 0:
                            t.matmul(pd0[:, 16 + mc:17 + mc],
                                     W["wv"][:, kc * D + mc * 128: kc * D + (mc + 1) * 128],
                                     h1bf[:, kc:kc + 1], start=(kc == 0), stop=(kc == 7))
                t.fin(p_v, 1)
                if i == 0:
                    t.fin(p_bqkv, 1)

            emit_u1t(t, 0)
            emit_q(t, 0)
            emit_k(t, 0)
            emit_vp(t, 0)
            for i in range(NT):
                t.wait_ge(v_ve, i + 1)
                t.wait_ge(v_ke, i + 1)
                if i == 0:
                    t.wait_ge(v_bev, 1)
                if i == 6:
                    t.wait_ge(v_mod, 6)
                for b in range(nbatch(i)):
                    gl = batch_groups(i, b)
                    pdx = score_pd(i, b)
                    sb_ = scb(i) + b
                    is_bnd = (len(gl) == 1)
                    pb_ = (b - 1 if i == 0 else b)
                    if b == 0 and i >= 5:
                        t.wait_ge(s_h2, i)
                    if b == 0 and i >= 4:
                        t.wait_ge(s_h1, i + 1)
                    if pdx is pd0:
                        if i == 0 and b == 1:
                            t.wait_ge(v_ext, 1)
                            t.wait_ge(v_rin, 1)
                            t.wait_ge(s_non, 1)
                        if sb_ > 0:
                            t.wait_ge(s_exp, sb_ if i < 6 else sb_ - 1)
                    else:
                        if sb_ >= 2:
                            t.wait_ge(s_exp, sb_ - 1)
                    for kk, gidx in enumerate(gl):
                        t.matmul(pdx[:, kk * 128:(kk + 1) * 128], kg[:, gidx * 128:(gidx + 1) * 128],
                                 qg[:, gidx * 128:(gidx + 1) * 128], start=True, stop=True)
                    t.fin(p_sc, 1)
                    if sb_ >= 2:
                        t.wait_ge(v_vbe, sb_ - 1)
                    if b == 0 and i > 0:
                        t.wait_ge(v_ffn, i)
                    if i == 0 and not is_bnd:
                        t.wait_ge(v_h2b, 1)
                    for kk, gidx in enumerate(gl):
                        if is_bnd:
                            vt_out = pa[:, 256:320]
                        else:
                            vt_out = pa[:, (pb_ % 2) * 512 + 64 * kk:(pb_ % 2) * 512 + 64 * kk + 64]
                        t.matmul(vt_out, vg[0:64, gidx * 128:(gidx + 1) * 128],
                                 idn[0:64, 0:64], start=True, stop=True)
                    t.fin(p_vt, 1)
                    t.wait_ge(s_exp, sb_ + 1)
                    t.wait_ge(v_vbe, sb_ + 1)
                    if sb_ >= 2:
                        t.wait_ge(s_non, sb_ - 1)
                        t.wait_ge(v_rin, sb_ - 1)
                    for kk, gidx in enumerate(gl):
                        if is_bnd:
                            oa_out = pd0[:, 256:321]
                        else:
                            cb_ = (pb_ % 2) * 512 + 65 * kk
                            oa_out = pb[:, cb_:cb_ + 65]
                        t.matmul(oa_out, a2u[:, gidx * 128:(gidx + 1) * 128],
                                 vbs[:, gidx * 72:gidx * 72 + 65], start=True, stop=True)
                    t.fin(p_av, 1)
                    t.wait_ge(s_non, sb_ + 1)
                    if is_bnd:
                        t.matmul(pd0[0:64, 128:256], on_[:, 0:64], idn[:, :], start=True, stop=True)
                    else:
                        if pb_ % 2 == 0:
                            if pb_ == 0:
                                t.wait_ge(v_ke, i + 1)
                                if i > 0:
                                    t.wait_ge(s_kr, i)
                                    t.wait_ge(v_asm, 2 * i)
                            else:
                                t.wait_ge(v_asm, 2 * i + 1)
                        for gidx in gl:
                            s_ = (gidx - 1) % 8
                            t.matmul(pc[0:64, s_ * 128:(s_ + 1) * 128],
                                     on_[:, gidx * 64:(gidx + 1) * 64], idn[:, :],
                                     start=True, stop=True)
                    t.fin(p_ont, 1)
                    if i == 0 and b == 0:
                        t.fin(p_bont, 1)
                        t.wait_ge(v_ext, 1)
                        t.matmul(pa[0:1, 992:993], red_b[:, 0:1], onescol[:, 0:1], start=True, stop=True)
                        t.matmul(pa[0:1, 993:994], red_b[:, 1:2], onescol[:, 0:1], start=True, stop=True)
                        t.fin(p_bst, 1)
                        t.wait_ge(v_ext, 3)
                        t.matmul(pa[:, 994:995], ones1[0:1, :], lb_c[0:1, 1:2], start=True, stop=True)
                        t.matmul(pa[:, 995:996], ones1[0:1, :], lb_c[0:1, 4:5], start=True, stop=True)
                        t.fin(p_bbc, 1)
                t.wait_ge(v_asm, 2 * (i + 1))
                t.wait_ge(v_dga, 1)
                t.wait_ge(s_h1, i + 1)
                if i == 0:
                    t.wait_ge(v_h2b, 1)
                for c in range(NCH):
                    t.matmul(pa[:, c * 128:(c + 1) * 128], fma[:, c * 128:(c + 1) * 128],
                             dga[:, c * 128:(c + 1) * 128], start=True, stop=True)
                t.fin(p_atm, 1)
                if i + 1 < NT:
                    emit_u1t(t, i + 1)
                t.wait_ge(v_u2, i + 1)
                if i >= 4:
                    t.wait_ge(s_h1, i + 1)
                    if i == 4:
                        t.wait_ge(v_mod, 6)
                        t.wait_ge(s_exp, scb(4))
                    else:
                        t.wait_ge(s_h2, i)
                    for c in range(NCH):
                        dstp = pd0 if c < 4 else pd1
                        t.matmul(dstp[:, (c % 4) * 128:(c % 4 + 1) * 128],
                                 u2[:, c * 128:(c + 1) * 128], idn[:, :], start=True, stop=True)
                else:
                    t.wait_ge(v_x2, i + 1)
                    for c in range(NCH):
                        t.matmul(pa[:, c * 128:(c + 1) * 128], u2[:, c * 128:(c + 1) * 128],
                                 idn[:, :], start=True, stop=True)
                t.fin(p_u2t, 1)
                if i + 1 < NT:
                    emit_q(t, i + 1)
                t.wait_ge(v_xr, i + 1)
                t.wait_ge(ld_w, LW["wr"])
                for hh2 in range(2):
                    for kc in range(NCH):
                        t.matmul(pa[:, hh2 * 512:hh2 * 512 + 512],
                                 xr[:, kc * 128:(kc + 1) * 128],
                                 W["wr"][:, kc * D + hh2 * 512: kc * D + hh2 * 512 + 512],
                                 start=(kc == 0), stop=(kc == 7))
                t.fin(p_r, 1)
                t.wait_ge(ld_w, LW["wkf"])
                t.wait_ge(v_asm, 2 * (i + 1))
                for mc in range(NCH):
                    for kc in range(NCH):
                        t.matmul(pc[:, mc * 128:(mc + 1) * 128],
                                 W["wkf"][:, kc * D + mc * 128: kc * D + (mc + 1) * 128],
                                 xk[:, kc * 128:(kc + 1) * 128],
                                 start=(kc == 0), stop=(kc == 7))
                t.fin(p_kf, 1)
                if i + 1 < NT:
                    emit_k(t, i + 1)
                t.wait_ge(ld_w, LW["wvf"])
                t.wait_ge(s_kr, i + 1)
                t.wait_ge(s_rt, i + 1)
                for hh2 in range(2):
                    for kc in range(NCH):
                        t.matmul(pa[:, hh2 * 512:hh2 * 512 + 512],
                                 ksq[:, kc * 128:(kc + 1) * 128],
                                 W["wvf"][:, kc * D + hh2 * 512: kc * D + hh2 * 512 + 512],
                                 start=(kc == 0), stop=(kc == 7))
                t.fin(p_vf, 1)
                if i + 1 < NT:
                    emit_vp(t, i + 1)

        if DEBUG_DUMP:
            dbg_sem = sem("dbg_sem")
            name2sb = dict(vg=vg[0:64, :],
                           h2fv=h2f[:, :].rearrange("p (c w) -> p c w", w=132)[:, :, 0:129],
                           xin0=xin[0], xin1=xin[1], xin2=xin[2], u1=u1, u2=u2,
                           h1f=h1f, dt=dt_, xr=xr, xk=xk, rt=rt,
                           krel=krel, ksq=ksq, ffnf=ffnf, qg=qg, kg=kg,
                           a2u=a2u, vbs=vbs, on_=on_, fma=fma, rinv=rinv,
                           modf=modf, dga=dga, gbT=gbT, sT=sT, h1bf=h1bf,
                           x2b0=x2b_[0], x2b1=x2b_[1], yb0=yb_[0], yb1=yb_[1],
                           rstdb=rstdb, h2bf=h2bf, xbf=xbf, atnb=atnb,
                           x2bf=x2bf, lb_c=lb_c, ln_c=ln_c, ln_s=ln_s)
            dbg_outs = {}
            def _ap(x):
                return x if isinstance(x, bass.AP) else x[:, :]
            for nm in DEBUG_DUMP:
                ap_t = _ap(name2sb[nm])
                dbg_outs[nm] = nc.dram_tensor("dbg_" + nm, list(ap_t.shape), ap_t.dtype,
                                              kind="ExternalOutput").ap()

            @block.sync
            def _(s):
                s.wait_ge(st_y, 16 * NT)
                nd = 0
                for nm in DEBUG_DUMP:
                    s.dma_start(out=dbg_outs[nm], in_=_ap(name2sb[nm])).then_inc(dbg_sem, 16)
                    nd += 1
                s.wait_ge(dbg_sem, 16 * nd)

    return nc


def _host_prep(inputs):
    bf = ml_dtypes.bfloat16
    x = np.ascontiguousarray(np.asarray(inputs["L0_attn_x"], dtype=np.float32))
    B, S, _ = x.shape
    perm = np.concatenate([np.r_[mc * 64:(mc + 1) * 64, (mc + 8) * 64:(mc + 9) * 64]
                           for mc in range(8)])
    wq = np.ascontiguousarray(np.asarray(inputs["L0_attn_Wq"], np.float32)[:, perm])
    wk = np.ascontiguousarray(np.asarray(inputs["L0_attn_Wk"], np.float32)[:, perm])
    wv = np.ascontiguousarray(np.asarray(inputs["L0_attn_Wv"], np.float32)[:, perm])
    bq = np.asarray(inputs["L0_attn_bq"], np.float32)[perm]
    bk = np.asarray(inputs["L0_attn_bk"], np.float32)[perm]
    bv = np.asarray(inputs["L0_attn_bv"], np.float32)[perm]
    wr = np.ascontiguousarray(np.asarray(inputs["L0_ffn_W_r"], np.float32).T)
    wkf = np.ascontiguousarray(np.asarray(inputs["L0_ffn_W_k"], np.float32).T)
    wvf = np.ascontiguousarray(np.asarray(inputs["L0_ffn_W_v"], np.float32).T)
    wmod = np.ascontiguousarray(np.asarray(inputs["L0_wmod"], np.float32))
    cc = np.asarray(inputs["L0_c"], np.float32)
    mur = np.ascontiguousarray(np.asarray(inputs["L0_ffn_mu_r"], np.float32).reshape(8, 128).T)
    muk = np.ascontiguousarray(np.asarray(inputs["L0_ffn_mu_k"], np.float32).reshape(8, 128).T)

    def bias_pat(b):
        p = np.zeros((64, 16), np.float32)
        for mc in range(8):
            p[:, mc] = b[mc * 128: mc * 128 + 64]
            p[:, 8 + mc] = b[mc * 128 + 64: (mc + 1) * 128]
        return p

    consts = np.zeros((128, 16), np.float32)
    for col, val in [(C_EPS, 1e-5), (C_NHALF, -0.5), (C_QSC, 0.125), (C_NSH, -20.0),
                     (C_HALF, 0.5), (C_NEG1, -1.0), (C_RD, 1.0 / 1024.0), (C_ONE, 1.0),
                     (C_QTR, 0.25), (C_NQTR, -0.25), (C_15, 1.5)]:
        consts[:, col] = val
    consts[:, C_QK] = np.full(128, 0x5F3759DF, np.int32).view(np.float32)
    consts[:, C_SH1] = np.full(128, 1, np.int32).view(np.float32)
    idn = np.eye(128, dtype=np.float32).astype(bf)
    qs = np.zeros((64, GSLOT * 128), np.float32)
    ks = np.zeros((64, GSLOT * 128), np.float32)
    col_t = np.arange(GSLOT * 128) % 8
    for tt in range(8):
        qs[tt, col_t == tt] = 1.0
        ks[tt, col_t == tt] = 1024.0
    qs[8, :] = 1.0
    ks[8, :] = -1024.0
    qs = qs.astype(bf)
    ks = ks.astype(bf)
    ones1 = np.ones((1, 128), np.float32)
    ones1b = np.ones((1, 128), np.float32).astype(bf)
    onescol = np.ones((128, 1), np.float32)

    bv_nat = np.asarray(inputs["L0_attn_bv"], np.float32)
    bvt = np.ascontiguousarray(np.repeat(bv_nat.reshape(16, 1, 64), 8, axis=1).reshape(128, 64))
    shared = dict(wq=wq, wk=wk, wv=wv, wr=wr, wkf=wkf, wvf=wvf, wmod=wmod,
                  consts=consts, idn=idn, qstat=qs, kstat=ks, bvt=bvt,
                  bqp=bias_pat(bq), bkp=bias_pat(bk), bvp=bias_pat(bv),
                  mur=mur, muk=muk, ones1=ones1, ones1b=ones1b, onescol=onescol)
    in_maps = []
    for ci in range(8):
        bidx, half = ci // 2, ci % 2
        t0 = half * 2048
        m = dict(shared)
        m["x"] = np.ascontiguousarray(x[bidx, t0:t0 + 2048])
        m["xbnd"] = (np.zeros((1, D), np.float32) if half == 0
                     else np.ascontiguousarray(x[bidx, 2047:2048]))
        m["flag"] = np.full((128, 1), float(half), np.float32)
        m["crow"] = np.ascontiguousarray(cc[bidx:bidx + 1])
        in_maps.append(m)
    return in_maps


def kernel(**inputs):
    in_maps = _host_prep(inputs)
    nc = build_nc()
    res = run_bass_kernel_spmd(nc, in_maps, core_ids=list(range(8)))
    x = np.asarray(inputs["L0_attn_x"])
    B, S, _ = x.shape
    out = np.empty((B, S, D), np.float32)
    for ci in range(8):
        bidx, half = ci // 2, ci % 2
        out[bidx, half * 2048:(half + 1) * 2048] = res.results[ci]["y"]
    return out


# version 13
# speedup vs baseline: 1.0833x; 1.0833x over previous
import numpy as np
import ml_dtypes
import contextlib
import concourse.bass as bass
import concourse.mybir as mybir
from concourse.bass_utils import run_bass_kernel_spmd

F32 = mybir.dt.float32
BF16 = mybir.dt.bfloat16
AF = mybir.ActivationFunctionType
OP = mybir.AluOpType
AX = mybir.AxisListType

D = 1024
NT = 16
TPT = 128
NCH = 8
GSLOT = 17

(C_EPS, C_NHALF, C_QSC, C_NSH, C_HALF, C_NEG1, C_RD, C_ONE, C_QTR,
 C_NQTR, C_QK, C_SH1, C_15, C_X0, C_X1, C_X2) = range(16)

MVNAME = ["sa", "ca", "ga", "sb", "cb", "gb"]

DEBUG_DUMP = []


class _Trk:

    _OPS = {"tensor_copy", "tensor_tensor", "tensor_scalar", "memset",
            "scalar_tensor_tensor", "reciprocal", "reduce_sum", "reduce_max",
            "tensor_reduce", "bn_stats", "bn_aggr", "activation", "copy",
            "add", "mul", "matmul", "transpose"}

    def __init__(self, eng):
        object.__setattr__(self, "_e", eng)
        object.__setattr__(self, "last", None)
        object.__setattr__(self, "_finned", None)
        object.__setattr__(self, "_pend", set())

    @staticmethod
    def _apname(x):
        t = getattr(x, "tensor", None)
        nm = getattr(t, "name", None)
        return nm

    def __getattr__(self, n):
        eng = object.__getattribute__(self, "_e")
        a = getattr(eng, n)
        if not callable(a) or n in ("wait_ge",):
            return a
        trk = self

        def w(*args, **kw):
            if n in _Trk._OPS:
                pend = object.__getattribute__(trk, "_pend")
                names = []
                for x in list(args) + list(kw.values()):
                    nm = _Trk._apname(x)
                    if nm is not None:
                        names.append(nm)
                outs = names[:1]
                if "accum_out" in kw and kw["accum_out"] is not None:
                    nm = _Trk._apname(kw["accum_out"])
                    if nm:
                        outs.append(nm)
                ins = names[1:]
                if any(nm in pend for nm in ins):
                    eng.drain()
                    pend.clear()
                r = a(*args, **kw)
                for nm in outs:
                    pend.add(nm)
            else:
                r = a(*args, **kw)
            if r is not None and hasattr(r, "then_inc"):
                object.__setattr__(trk, "last", r)
            return r
        return w

    def fin(self, sem_, n_=1):
        li = object.__getattribute__(self, "last")
        if li is object.__getattribute__(self, "_finned"):
            object.__getattribute__(self, "_e").drain().then_inc(sem_, n_)
        else:
            li.then_inc(sem_, n_)
            object.__setattr__(self, "_finned", li)



def nbatch(i):
    return 5 if i == 0 else 4


def scb(i):
    return sum(nbatch(j) for j in range(i))


def batch_groups(i, b):
    if i == 0 and b == 0:
        return [0]
    bb = b - 1 if i == 0 else b
    return [1 + 4 * bb + k for k in range(4)]



def _emit_rsqrt(v, consts, out_col, var_ap, eps_bias, scr):
    I32 = mybir.dt.int32
    P = var_ap.shape[0]
    ve = scr[0:P, 0:1]
    t_ = scr[0:P, 1:2]
    v.tensor_scalar(ve, var_ap, eps_bias, None, OP.add)
    v.tensor_tensor(out_col.bitcast(I32), ve.bitcast(I32),
                    consts[0:P, C_SH1:C_SH1 + 1].bitcast(I32), OP.logical_shift_right)
    v.tensor_tensor(out_col.bitcast(I32), consts[0:P, C_QK:C_QK + 1].bitcast(I32),
                    out_col.bitcast(I32), OP.subtract)
    for _ in range(2):
        v.tensor_tensor(t_, out_col, out_col, OP.mult)
        v.tensor_tensor(t_, t_, ve, OP.mult)
        v.tensor_scalar(t_, t_, consts[0:P, C_NHALF:C_NHALF + 1], consts[0:P, C_15:C_15 + 1],
                        OP.mult, OP.add)
        v.tensor_tensor(out_col, out_col, t_, OP.mult)


def build_nc():
    nc = bass.Bass("TRN2", target_bir_lowering=False, detect_race_conditions=False)
    x_h = nc.dram_tensor("x", [NT * TPT, D], F32, kind="ExternalInput").ap()
    xb_h = nc.dram_tensor("xbnd", [1, D], F32, kind="ExternalInput").ap()
    flag_h = nc.dram_tensor("flag", [128, 1], F32, kind="ExternalInput").ap()
    w_h = {}
    for w in ["wq", "wk", "wv", "wr", "wkf", "wvf"]:
        w_h[w] = nc.dram_tensor(w, [D, D], F32, kind="ExternalInput").ap()
    wmod_h = nc.dram_tensor("wmod", [D, 6 * D], F32, kind="ExternalInput").ap()
    crow_h = nc.dram_tensor("crow", [1, D], F32, kind="ExternalInput").ap()
    consts_h = nc.dram_tensor("consts", [128, 16], F32, kind="ExternalInput").ap()
    idn_h = nc.dram_tensor("idn", [128, 128], BF16, kind="ExternalInput").ap()
    qstat_h = nc.dram_tensor("qstat", [64, GSLOT * 128], BF16, kind="ExternalInput").ap()
    kstat_h = nc.dram_tensor("kstat", [64, GSLOT * 128], BF16, kind="ExternalInput").ap()
    bqp_h = nc.dram_tensor("bqp", [64, 16], F32, kind="ExternalInput").ap()
    bkp_h = nc.dram_tensor("bkp", [64, 16], F32, kind="ExternalInput").ap()
    bvp_h = nc.dram_tensor("bvp", [64, 16], F32, kind="ExternalInput").ap()
    bvt_h = nc.dram_tensor("bvt", [128, 64], F32, kind="ExternalInput").ap()
    mur_h = nc.dram_tensor("mur", [128, 8], F32, kind="ExternalInput").ap()
    muk_h = nc.dram_tensor("muk", [128, 8], F32, kind="ExternalInput").ap()
    ones1_h = nc.dram_tensor("ones1", [1, 128], F32, kind="ExternalInput").ap()
    ones1b_h = nc.dram_tensor("ones1b", [1, 128], BF16, kind="ExternalInput").ap()
    onescol_h = nc.dram_tensor("onescol", [128, 1], F32, kind="ExternalInput").ap()
    y_h = nc.dram_tensor("y", [NT * TPT, D], F32, kind="ExternalOutput").ap()

    es = contextlib.ExitStack()
    sbuf = lambda n, s, d: es.enter_context(nc.sbuf_tensor(n, s, d))
    psum = lambda n, s, d: es.enter_context(nc.psum_tensor(n, s, d))
    sem = lambda n: es.enter_context(nc.semaphore(n))
    with es:
        W = {w: sbuf(w + "s", [128, 8 * D], BF16) for w in ["wq", "wk", "wv", "wr", "wkf", "wvf"]}
        wmb = [sbuf(f"wmb{i}", [128, 4 * 512], BF16) for i in range(2)]
        xin = [sbuf(f"xin{i}", [128, D], F32) for i in range(3)]
        x2b_ = [sbuf(f"x2b{i}", [128, D], F32) for i in range(2)]
        yb_ = [sbuf(f"yb{i}", [128, D], F32) for i in range(2)]
        u1 = sbuf("u1", [128, D], BF16)
        u2 = sbuf("u2", [128, D], BF16)
        h1f = sbuf("h1f", [128, D], BF16)
        h2f = sbuf("h2f", [128, 8 * 132], BF16)
        dt_ = sbuf("dt", [128, D], BF16)
        xr = sbuf("xr", [128, D], BF16)
        xk = sbuf("xk", [128, D], BF16)
        rt = sbuf("rt", [128, D], BF16)
        krel = sbuf("krel", [128, D], BF16)
        ksq = sbuf("ksq", [128, D], BF16)
        ffnf = sbuf("ffnf", [128, D], BF16)
        qg = sbuf("qg", [128, GSLOT * 128], BF16)
        kg = sbuf("kg", [128, GSLOT * 128], BF16)
        vg = sbuf("vg", [128, GSLOT * 128], BF16)
        a2u = sbuf("a2u", [128, GSLOT * 128], BF16)
        vbs = sbuf("vbs", [128, GSLOT * 72], BF16)
        on_ = sbuf("on", [128, GSLOT * 64], BF16)
        fma = sbuf("fma", [128, D], BF16)
        rinv = sbuf("rinv", [128, GSLOT], F32)
        idn = sbuf("idn_s", [128, 128], BF16)
        consts = sbuf("consts_s", [128, 16], F32)
        bqp = sbuf("bqp_s", [64, 16], F32)
        bkp = sbuf("bkp_s", [64, 16], F32)
        bvp = sbuf("bvp_s", [64, 16], F32)
        bvt = sbuf("bvt_s", [128, 64], F32)
        murf = sbuf("murf", [128, 8], F32)
        mukf = sbuf("mukf", [128, 8], F32)
        modf = sbuf("modf", [128, 48], F32)
        clsc = sbuf("clsc", [128, 24], F32)
        dga = sbuf("dga", [128, 8 * 128], BF16)
        gbT = sbuf("gbT", [128, D], BF16)
        gbrow = sbuf("gbrow", [1, D], F32)
        gbc = sbuf("gbc", [128, 8], BF16)
        gbT8 = sbuf("gbT8", [8, 128], F32)
        ln_s = sbuf("ln_s", [128, 28], F32)
        ln_c = sbuf("ln_c", [128, 4], F32)
        crow = sbuf("crow_s", [1, D], F32)
        silcb = sbuf("silcb", [1, D], BF16)
        sT = sbuf("sT", [128, 8], BF16)
        modrow = sbuf("modrow", [1, D], F32)
        xbnd = sbuf("xbnd_s", [1, D], F32)
        u1b = sbuf("u1b", [1, D], BF16)
        lb_s = sbuf("lb_s", [1, 14], F32)
        lb_c = sbuf("lb_c", [1, 6], F32)
        h1bf = sbuf("h1bf", [128, 8], BF16)
        xbf = sbuf("xbf", [128, 8], F32)
        atnb = sbuf("atnb", [128, 8], F32)
        x2bf = sbuf("x2bf", [128, 8], F32)
        x2sq = sbuf("x2sq", [128, 8], F32)
        red_b = sbuf("red_b", [128, 2], F32)
        rstdb = sbuf("rstdb", [128, 2], F32)
        h2bf = sbuf("h2bf", [128, 8], BF16)
        flagc = sbuf("flagc", [128, 1], F32)
        ones1 = sbuf("ones1_s", [1, 128], F32)
        ones1b = sbuf("ones1b_s", [1, 128], BF16)
        onescol = sbuf("onescol_s", [128, 1], F32)

        pa = psum("pa", [128, 1024], F32)
        pb = psum("pb", [128, 1024], F32)
        pc = psum("pc", [128, 1024], F32)
        pd0 = psum("pd0", [128, 512], F32)
        pd1 = psum("pd1", [128, 512], F32)

        ld_s, ld_x, ld_w, st_y = sem("ld_s"), sem("ld_x"), sem("ld_w"), sem("st_y")
        p_sT, p_xbf, p_mvh, p_mtr = sem("p_sT"), sem("p_xbf"), sem("p_mvh"), sem("p_mtr")
        p_q, p_k, p_v, p_bqkv = sem("p_q"), sem("p_k"), sem("p_v"), sem("p_bqkv")
        p_sc, p_vt, p_av, p_ont = sem("p_sc"), sem("p_vt"), sem("p_av"), sem("p_ont")
        p_bont, p_bst, p_bbc = sem("p_bont"), sem("p_bst"), sem("p_bbc")
        p_atm, p_u2t, p_u1t = sem("p_atm"), sem("p_u2t"), sem("p_u1t")
        p_r, p_kf, p_vf = sem("p_r"), sem("p_kf"), sem("p_vf")
        g_gbr, p_gbt, v_gbt = sem("g_gbr"), sem("p_gbt"), sem("v_gbt")
        p_gb1, v_gb8 = sem("p_gb1"), sem("v_gb8")
        v_lnb, v_u1b, v_silu, v_sT = sem("v_lnb"), sem("v_u1b"), sem("v_silu"), sem("v_sT")
        v_mve, v_mod, v_dga = sem("v_mve"), sem("v_mod"), sem("v_dga")
        v_xbf, v_h1b = sem("v_xbf"), sem("v_h1b")
        v_ln1, v_u1, v_qe, v_ke, v_ve = sem("v_ln1"), sem("v_u1"), sem("v_qe"), sem("v_ke"), sem("v_ve")
        v_vbe, v_rin, v_asm, v_x2 = sem("v_vbe"), sem("v_rin"), sem("v_asm"), sem("v_x2")
        v_ln2, v_u2, v_xr, v_ffn, v_y = sem("v_ln2"), sem("v_u2"), sem("v_xr"), sem("v_ffn"), sem("v_y")
        v_bev, v_ext, v_h2b = sem("v_bev"), sem("v_ext"), sem("v_h2b")
        s_silu, s_lb1, s_lb2, s_ln = sem("s_silu"), sem("s_lb1"), sem("s_lb2"), sem("s_ln")
        s_h1, s_exp, s_non, s_h2 = sem("s_h1"), sem("s_exp"), sem("s_non"), sem("s_h2")
        s_rt, s_kr, s_asm = sem("s_rt"), sem("s_kr"), sem("s_asm")

        block = es.enter_context(nc.Block())

        def score_pd(i, b):
            if i < 6 or (i == 0 and b == 0):
                return pd0
            return pd0 if b % 2 == 0 else pd1

        def mv_loads(nm):
            return [nm + jh + h for jh in "01" for h in "01"]
        dma_order = (mv_loads("sa") + mv_loads("ca") + ["wq"] + mv_loads("ga")
                     + ["wk", "wv"] + mv_loads("sb") + mv_loads("cb")
                     + ["wr", "wkf", "wvf"] + mv_loads("gb"))
        LW = {nm: 16 * (k + 1) for k, nm in enumerate(dma_order)}

        @block.scalar
        def _(s):
            for dst, src in [(consts[:], consts_h), (flagc[:], flag_h),
                             (crow[:], crow_h), (xbnd[:], xb_h),
                             (bqp[:], bqp_h), (bkp[:], bkp_h), (bvp[:], bvp_h), (bvt[:], bvt_h),
                             (murf[:], mur_h), (mukf[:], muk_h),
                             (idn[:], idn_h), (ones1[:], ones1_h),
                             (ones1b[:], ones1b_h), (onescol[:], onescol_h),
                             (qg[64:128, :], qstat_h), (kg[64:128, :], kstat_h)]:
                s.dma_start(out=dst, in_=src).then_inc(ld_s, 16)
        NLD_S = 16

        @block.sync
        def _(s):
            for i in range(min(3, NT)):
                s.dma_start(out=xin[i % 3][:], in_=x_h[i * TPT:(i + 1) * TPT, :]).then_inc(ld_x, 16)
            for i in range(NT):
                s.wait_ge(v_y, i + 1)
                s.dma_start(out=y_h[i * TPT:(i + 1) * TPT, :], in_=yb_[i % 2][:]).then_inc(st_y, 16)
                j = i + 3
                if j < NT:
                    s.dma_start(out=xin[j % 3][:], in_=x_h[j * TPT:(j + 1) * TPT, :]).then_inc(ld_x, 16)
            s.wait_ge(st_y, 16 * NT)

        @block.gpsimd
        def _(g):
            for nm in dma_order:
                if nm in W:
                    g.dma_start(out=W[nm][:, :].rearrange("p (kc d) -> p kc d", d=D),
                                in_=w_h[nm].rearrange("(kc p) d -> p kc d", p=128)).then_inc(ld_w, 16)
                else:
                    v = MVNAME.index(nm[:2])
                    jh, h = int(nm[2]), int(nm[3])
                    ell = 4 * v + 2 * jh + h
                    if ell >= 2:
                        g.wait_ge(p_mvh, 2 * v + jh)
                    src = wmod_h[h * 512:(h + 1) * 512, v * D + jh * 512: v * D + jh * 512 + 512]
                    g.dma_start(out=wmb[h][:, :].rearrange("p (ic j) -> p ic j", j=512),
                                in_=src.rearrange("(ic p) j -> p ic j", p=128)).then_inc(ld_w, 16)
            g.wait_ge(v_gb8, 1)
            for c in range(NCH):
                g.dma_start(out=gbrow[0:1, c * 128:(c + 1) * 128],
                            in_=gbT8[c:c + 1, :]).then_inc(g_gbr, 16)

        @block.scalar
        def _(s):
            s = _Trk(s)
            s.wait_ge(ld_s, 16 * NLD_S)
            s.activation(modrow[:], crow[:], AF.Silu).then_inc(s_silu, 1)

            def emit_h1(s, i):
                s.wait_ge(p_u1t, i + 1)
                s.wait_ge(v_mod, 2)
                for c in range(NCH):
                    srcp = (pa if i == 0 else pb)[:, c * 128:(c + 1) * 128]
                    s.activation(h1f[:, c * 128:(c + 1) * 128], srcp,
                                 AF.Identity, scale=modf[:, 8 + c:9 + c], bias=modf[:, c:c + 1])
                s.fin(s_h1, 1)

            emit_h1(s, 0)
            for i in range(NT):
                for b in range(nbatch(i)):
                    gl = batch_groups(i, b)
                    s.wait_ge(p_sc, scb(i) + b + 1)
                    pdx = score_pd(i, b)
                    ncols = 128 * len(gl)
                    s.activation(a2u[:, gl[0] * 128: gl[0] * 128 + ncols], pdx[:, 0:ncols],
                                 AF.Exp, scale=consts[:, C_QSC:C_QSC + 1],
                                 bias=consts[:, C_NSH:C_NSH + 1]).then_inc(s_exp, 1)
                    s.wait_ge(p_av, scb(i) + b + 1)
                    s.wait_ge(v_rin, scb(i) + b + 1)
                    for gidx in gl:
                        if gidx == 0:
                            src = pd0[:, 256:320]
                        else:
                            pb_ = (gidx - 1) // 4
                            k = (gidx - 1) % 4
                            cb = (pb_ % 2) * 512 + k * 65
                            src = pb[:, cb:cb + 64]
                        s.activation(on_[:, gidx * 64:(gidx + 1) * 64], src,
                                     AF.Identity, scale=rinv[:, gidx:gidx + 1])
                    s.fin(s_non, 1)
                s.wait_ge(p_u2t, i + 1)
                s.wait_ge(v_mod, 5)
                for c in range(NCH):
                    if i >= 4:
                        srcp = (pd0 if c < 4 else pd1)[:, (c % 4) * 128:(c % 4 + 1) * 128]
                    else:
                        srcp = pa[:, c * 128:(c + 1) * 128]
                    s.activation(h2f[:, c * 132 + 1: c * 132 + 129], srcp,
                                 AF.Identity, scale=modf[:, 32 + c:33 + c], bias=modf[:, 24 + c:25 + c])
                s.fin(s_h2, 1)
                if i + 1 < NT:
                    emit_h1(s, i + 1)
                s.wait_ge(p_r, i + 1)
                s.activation(rt[:], pa[:, :], AF.Tanh, scale=consts[:, C_HALF:C_HALF + 1]).then_inc(s_rt, 1)
                s.wait_ge(p_kf, i + 1)
                s.activation(krel[:], pc[:, :], AF.Relu)
                s.activation(ksq[:], krel[:], AF.Square).then_inc(s_kr, 1)

        @block.vector
        def _(v):
            v = _Trk(v)
            v.wait_ge(ld_s, 16 * NLD_S)
            v.memset(qg[0:64, 0:128], 0.0)
            v.memset(kg[0:64, 0:128], 0.0)
            v.memset(vg[0:64, 0:128], 0.0)
            v.memset(vbs[:, :], 0.0)
            v.memset(vbs[:, :].rearrange("p (s c) -> p s c", c=72)[:, :, 64:65], 1.0)
            v.memset(h2f[:, :].rearrange("p (s c) -> p s c", c=132)[:, :, 0:1], 0.0)
            v.bn_stats(lb_s[:, 0:6], xbnd[0:1, 0:512])
            v.bn_stats(lb_s[:, 6:12], xbnd[0:1, 512:1024])
            v.bn_aggr(lb_s[:, 12:14], lb_s[:, 0:12]).then_inc(v_lnb, 1)
            _emit_rsqrt(v, consts, lb_c[0:1, 0:1], lb_s[0:1, 13:14],
                        consts[0:1, C_EPS:C_EPS + 1], clsc)
            v.tensor_scalar(lb_c[0:1, 5:6], lb_s[0:1, 12:13], lb_c[0:1, 0:1],
                            consts[0:1, C_NEG1:C_NEG1 + 1], OP.mult, OP.mult)
            v.tensor_scalar(u1b[:], xbnd[:], lb_c[0:1, 0:1], lb_c[0:1, 5:6],
                            OP.mult, OP.add).then_inc(v_u1b, 1)
            v.wait_ge(s_silu, 1)
            v.tensor_copy(silcb[:], modrow[:]).then_inc(v_silu, 1)
            v.wait_ge(p_sT, 1)
            v.tensor_copy(sT[:, :], pd0[:, 480:488]).then_inc(v_sT, 1)
            v.wait_ge(p_xbf, 1)
            v.tensor_copy(xbf[:, :], pa[:, 1000:1008]).then_inc(v_xbf, 1)
            for mv in range(6):
                for jh in range(2):
                    v.wait_ge(p_mvh, 2 * mv + jh + 1)
                    v.tensor_copy(modrow[0:1, jh * 512:(jh + 1) * 512], pd1[0:1, 0:512]).then_inc(v_mve, 1)
                v.wait_ge(p_mtr, mv + 1)
                v.tensor_copy(modf[:, 8 * mv:8 * mv + 8], pd1[:, 384:392])
                if mv in (1, 4):
                    v.tensor_scalar(modf[:, 8 * mv:8 * mv + 8], modf[:, 8 * mv:8 * mv + 8],
                                    consts[:, C_ONE:C_ONE + 1], None, OP.add)
                if mv in (2, 5):
                    cf = 8 * mv
                    gsl = modf[:, cf:cf + 8]
                    v.tensor_scalar(clsc[:, 0:8], gsl, 0.0, None, OP.is_ge)
                    v.tensor_scalar(clsc[:, 8:16], gsl, consts[:, C_QTR:C_QTR + 1], None, OP.max)
                    v.tensor_scalar(clsc[:, 16:24], gsl, consts[:, C_NQTR:C_NQTR + 1], None, OP.min)
                    v.tensor_tensor(clsc[:, 8:16], clsc[:, 8:16], clsc[:, 16:24], OP.subtract)
                    v.tensor_tensor(clsc[:, 8:16], clsc[:, 8:16], clsc[:, 0:8], OP.mult)
                    v.tensor_tensor(gsl, clsc[:, 8:16], clsc[:, 16:24], OP.add)
                    if mv == 5:
                        v.tensor_scalar(gsl, gsl, consts[:, C_HALF:C_HALF + 1], None, OP.mult)
                    if mv == 2:
                        for c in range(NCH):
                            v.tensor_scalar(dga[:, c * 128:(c + 1) * 128], idn[:, :],
                                            modf[:, cf + c:cf + c + 1], None, OP.mult)
                    else:
                        v.tensor_copy(gbc[:, :], gsl)
                    v.fin(v_dga, 1)
                v.fin(v_mod, 1)
                if mv == 1:
                    v.tensor_tensor(x2sq[:, 0:8], pa[:, 1008:1016], modf[:, 8:16], OP.mult)
                    v.tensor_tensor(x2sq[:, 0:8], x2sq[:, 0:8], modf[:, 0:8], OP.add)
                    v.tensor_copy(h1bf[:, :], x2sq[:, 0:8]).then_inc(v_h1b, 1)
            def emit_ln1(v, i):
                v.wait_ge(ld_x, 16 * (i + 1))
                v.wait_ge(p_u1t, i)
                v.bn_stats(ln_s[:, 0:6], xin[i % 3][:, 0:512])
                v.bn_stats(ln_s[:, 6:12], xin[i % 3][:, 512:1024])
                v.bn_aggr(ln_s[:, 24:26], ln_s[:, 0:12]).then_inc(v_ln1, 1)
                _emit_rsqrt(v, consts, ln_c[:, 0:1], ln_s[:, 25:26],
                            consts[:, C_EPS:C_EPS + 1], clsc)
                v.tensor_scalar(ln_c[:, 2:3], ln_s[:, 24:25], ln_c[:, 0:1],
                                consts[:, C_NEG1:C_NEG1 + 1], OP.mult, OP.mult)
                v.tensor_scalar(u1[:], xin[i % 3][:], ln_c[:, 0:1], ln_c[:, 2:3],
                                OP.mult, OP.add).then_inc(v_u1, 1)

            v.wait_ge(p_gb1, 1)
            v.tensor_copy(gbT8[:, :], pd1[0:8, 384:512])
            v.fin(v_gb8, 1)
            v.wait_ge(p_gbt, 1)
            v.tensor_copy(gbT[:, 0:512], pd0[:, 0:512])
            v.tensor_copy(gbT[:, 512:1024], pd1[:, 0:512])
            v.fin(v_gbt, 1)

            def emit_evac(v, i, which):
                psrc, gdst, bpat, vsem, pwait = {
                    "q": (pb, qg, bqp, v_qe, p_q), "k": (pc, kg, bkp, v_ke, p_k),
                    "v": (pb, vg, None, v_ve, p_v)}[which]
                v.wait_ge(pwait, i + 1)
                for par in range(2):
                    in3 = psrc[64 * par:64 * par + 64, :].rearrange(
                        "p (mc g t) -> p mc g t", g=16, t=8)
                    out3 = gdst[0:64, :].rearrange("p (g jj t) -> p g jj t", jj=16, t=8)[
                        :, 1:GSLOT, 8 * par:8 * par + 8, :].rearrange("p g j t -> p j g t")
                    if bpat is None:
                        v.tensor_copy(out3, in3)
                    else:
                        bias = bpat[:, 8 * par:8 * par + 8].unsqueeze(2).unsqueeze(3).broadcast_to([64, 8, 16, 8])
                        v.tensor_tensor(out3, in3, bias, OP.add)
                v.fin(vsem, 1)

            emit_ln1(v, 0)
            emit_evac(v, 0, "q")
            emit_evac(v, 0, "k")
            for i in range(NT):
                emit_evac(v, i, "v")
                if i == 0:
                    v.wait_ge(p_bqkv, 1)
                    for (tix, gdst, bpat) in [(0, qg, bqp), (1, kg, bkp)]:
                        for par in range(2):
                            in2 = pd0[64 * par:64 * par + 64, 8 * tix:8 * tix + 8]
                            out2 = gdst[0:64, :].rearrange("p (g jj t) -> p g jj t", jj=16, t=8)[
                                :, 0, 8 * par:8 * par + 8, 0]
                            v.tensor_tensor(out2, in2, bpat[:, 8 * par:8 * par + 8], OP.add)
                    for par in range(2):
                        in2 = pd0[64 * par:64 * par + 64, 16:24]
                        out2 = vg[0:64, :].rearrange("p (g jj t) -> p g jj t", jj=16, t=8)[
                            :, 0, 8 * par:8 * par + 8, 0]
                        v.tensor_copy(out2, in2)
                    v.fin(v_bev, 1)
                for b in range(nbatch(i)):
                    gl = batch_groups(i, b)
                    v.wait_ge(p_vt, scb(i) + b + 1)
                    if len(gl) == 1:
                        v.tensor_tensor(vbs[:, 0:64], pa[:, 256:320], bvt[:, :], OP.add)
                    else:
                        pb_ = b - 1 if i == 0 else b
                        base = (pb_ % 2) * 512
                        dstv = vbs[:, :].rearrange("p (s c) -> p s c", c=72)[:, gl[0]:gl[0] + 4, 0:64]
                        srcv = pa[:, base:base + 256].rearrange("p (kk c) -> p kk c", c=64)
                        bias4 = bvt[:, :].unsqueeze(1).broadcast_to([128, 4, 64])
                        v.tensor_tensor(dstv, srcv, bias4, OP.add)
                    v.fin(v_vbe, 1)
                    v.wait_ge(p_av, scb(i) + b + 1)
                    if len(gl) == 1:
                        v.reciprocal(rinv[:, 0:1], pd0[:, 320:321])
                    else:
                        pb_ = b - 1 if i == 0 else b
                        base = (pb_ % 2) * 512 + 64
                        src_s = pb[:, base - 64:base - 64 + 260].rearrange("p (s c) -> p s c", c=65)[:, :, 64]
                        v.reciprocal(rinv[:, gl[0]:gl[0] + 4], src_s)
                    v.fin(v_rin, 1)
                    pb_h = b - 1 if i == 0 else b
                    if pb_h in (1, 3) and len(gl) == 4:
                        hh = pb_h // 2
                        bofs = 1 if i == 0 else 0
                        v.wait_ge(p_ont, scb(i) + bofs + 2 * (hh + 1))
                        for par in range(2):
                            srca = pc[0:64, :].rearrange(
                                "p (g c2 pr t) -> p c2 pr g t", t=8, pr=2, c2=8)[:, :, par, :, :]
                            dsta = fma[64 * par:64 * par + 64, :].rearrange(
                                "p (c h2 g t) -> p c h2 g t", h2=2, g=8, t=8)[:, :, hh, :, :]
                            v.tensor_copy(dsta, srca)
                        v.fin(v_asm, 1)
                        if pb_h == 1 and i + 1 < NT:
                            emit_ln1(v, i + 1)
                    if i == 0 and b == 0:
                        v.wait_ge(p_bont, 1)
                        for par in range(2):
                            src2 = pd0[0:64, 128:256].rearrange("p (c2 pr t) -> p c2 pr t", pr=2, t=8)[:, :, par, 0]
                            v.tensor_copy(atnb[64 * par:64 * par + 64, 0:8], src2)
                        v.wait_ge(v_dga, 1)
                        v.tensor_tensor(atnb[:], atnb[:], modf[:, 16:24], OP.mult)
                        v.wait_ge(v_xbf, 1)
                        v.tensor_tensor(x2bf[:], xbf[:], atnb[:], OP.add)
                        v.tensor_tensor(x2sq[:], x2bf[:], x2bf[:], OP.mult)
                        v.reduce_sum(red_b[:, 0:1], x2bf[:], axis=AX.X)
                        v.reduce_sum(red_b[:, 1:2], x2sq[:], axis=AX.X)
                        v.fin(v_ext, 1)
                        v.wait_ge(p_bst, 1)
                        v.tensor_scalar(lb_c[0:1, 2:3], pa[0:1, 992:993], consts[0:1, C_RD:C_RD + 1],
                                        None, OP.mult)
                        v.tensor_tensor(lb_c[0:1, 3:4], lb_c[0:1, 2:3], lb_c[0:1, 2:3], OP.mult)
                        v.tensor_scalar(lb_c[0:1, 3:4], lb_c[0:1, 3:4], consts[0:1, C_NEG1:C_NEG1 + 1],
                                        consts[0:1, C_EPS:C_EPS + 1], OP.mult, OP.add)
                        v.tensor_scalar(lb_c[0:1, 4:5], pa[0:1, 993:994],
                                        consts[0:1, C_RD:C_RD + 1], None, OP.mult)
                        v.tensor_tensor(lb_c[0:1, 4:5], lb_c[0:1, 4:5], lb_c[0:1, 3:4], OP.add)
                        _emit_rsqrt(v, consts, lb_c[0:1, 1:2], lb_c[0:1, 4:5], 0.0, clsc)
                        v.fin(v_ext, 1)
                        v.tensor_scalar(lb_c[0:1, 4:5], lb_c[0:1, 2:3], lb_c[0:1, 1:2],
                                        consts[0:1, C_NEG1:C_NEG1 + 1], OP.mult, OP.mult)
                        v.fin(v_ext, 1)
                        v.wait_ge(p_bbc, 1)
                        v.tensor_copy(rstdb[:, 0:2], pa[:, 994:996])
                        v.tensor_scalar(x2bf[:], x2bf[:], rstdb[:, 0:1], rstdb[:, 1:2], OP.mult, OP.add)
                        v.tensor_tensor(x2bf[:], x2bf[:], modf[:, 32:40], OP.mult)
                        v.tensor_tensor(x2bf[:], x2bf[:], modf[:, 24:32], OP.add)
                        v.tensor_scalar(h2bf[:], x2bf[:], flagc[:, 0:1], None, OP.mult)
                        v.tensor_copy(h2f[:, :].rearrange("p (c w) -> p c w", w=132)[:, :, 0:1],
                                      h2bf[:, :].unsqueeze(2))
                        v.fin(v_h2b, 1)
                v.wait_ge(p_atm, i + 1)
                v.tensor_tensor(x2b_[i % 2][:], xin[i % 3][:], pa[:, :], OP.add).then_inc(v_x2, 1)
                v.bn_stats(ln_s[:, 12:18], x2b_[i % 2][:, 0:512])
                v.bn_stats(ln_s[:, 18:24], x2b_[i % 2][:, 512:1024])
                v.bn_aggr(ln_s[:, 26:28], ln_s[:, 12:24]).then_inc(v_ln2, 1)
                _emit_rsqrt(v, consts, ln_c[:, 1:2], ln_s[:, 27:28],
                            consts[:, C_EPS:C_EPS + 1], clsc)
                v.tensor_scalar(ln_c[:, 2:3], ln_s[:, 26:27], ln_c[:, 1:2],
                                consts[:, C_NEG1:C_NEG1 + 1], OP.mult, OP.mult)
                v.tensor_scalar(u2[:], x2b_[i % 2][:], ln_c[:, 1:2], ln_c[:, 2:3],
                                OP.mult, OP.add).then_inc(v_u2, 1)
                v.wait_ge(s_h2, i + 1)
                if i == 0:
                    v.wait_ge(v_h2b, 1)
                v.tensor_tensor(dt_[:, :].rearrange("p (c t) -> p c t", t=128),
                                h2f[:, :].rearrange("p (c w) -> p c w", w=132)[:, :, 1:129],
                                h2f[:, :].rearrange("p (c w) -> p c w", w=132)[:, :, 0:128],
                                OP.subtract)
                for c in range(NCH):
                    v.scalar_tensor_tensor(xr[:, c * 128:(c + 1) * 128], dt_[:, c * 128:(c + 1) * 128],
                                           murf[:, c:c + 1], h2f[:, c * 132:c * 132 + 128],
                                           OP.mult, OP.add)
                for c in range(NCH):
                    v.scalar_tensor_tensor(xk[:, c * 128:(c + 1) * 128], dt_[:, c * 128:(c + 1) * 128],
                                           mukf[:, c:c + 1], h2f[:, c * 132:c * 132 + 128],
                                           OP.mult, OP.add)
                if i < NT - 1:
                    v.tensor_copy(h2f[:, :].rearrange("p (c w) -> p c w", w=132)[:, :, 0:1],
                                  h2f[:, :].rearrange("p (c w) -> p c w", w=132)[:, :, 128:129])
                v.fin(v_xr, 1)
                if i + 1 < NT:
                    emit_evac(v, i + 1, "q")
                    emit_evac(v, i + 1, "k")
                v.wait_ge(p_vf, i + 1)
                v.scalar_tensor_tensor(ffnf[:], rt[:], 1.0, pa[:, :], OP.add, OP.mult).then_inc(v_ffn, 1)
                if i >= 2:
                    v.wait_ge(st_y, 16 * (i - 1))
                v.tensor_tensor(yb_[i % 2][:], ffnf[:], gbT[:, :], OP.mult)
                v.tensor_tensor(yb_[i % 2][:], yb_[i % 2][:], x2b_[i % 2][:], OP.add).then_inc(v_y, 1)

        @block.tensor
        def _(t):
            t = _Trk(t)
            t.wait_ge(v_silu, 1)
            for c in range(NCH):
                t.matmul(pd0[:, 480 + c:481 + c], silcb[0:1, c * 128:(c + 1) * 128],
                         ones1b[0:1, 0:1], start=True, stop=True)
            t.fin(p_sT, 1)
            for c in range(NCH):
                t.matmul(pa[:, 1000 + c:1001 + c], xbnd[0:1, c * 128:(c + 1) * 128],
                         ones1[0:1, 0:1], start=True, stop=True)
            t.wait_ge(v_u1b, 1)
            for c in range(NCH):
                t.matmul(pa[:, 1008 + c:1009 + c], u1b[0:1, c * 128:(c + 1) * 128],
                         ones1b[0:1, 0:1], start=True, stop=True)
            t.fin(p_xbf, 1)
            t.wait_ge(v_sT, 1)
            for mv in range(6):
                if mv > 0:
                    t.wait_ge(v_mod, mv)
                for jh in range(2):
                    t.wait_ge(ld_w, LW[MVNAME[mv] + str(jh) + "1"])
                    if jh == 1:
                        t.wait_ge(v_mve, 2 * mv + 1)
                    for ic in range(8):
                        t.matmul(pd1[0:1, 0:512], sT[:, ic:ic + 1],
                                 wmb[ic // 4][:, (ic % 4) * 512:(ic % 4) * 512 + 512],
                                 start=(ic == 0), stop=(ic == 7))
                    t.fin(p_mvh, 1)
                t.wait_ge(v_mve, 2 * mv + 2)
                for c in range(NCH):
                    t.matmul(pd1[:, 384 + c:385 + c], modrow[0:1, c * 128:(c + 1) * 128],
                             ones1[0:1, 0:1], start=True, stop=True)
                t.fin(p_mtr, 1)
            t.wait_ge(v_dga, 2)
            t.matmul(pd1[0:8, 384:512], gbc[:, :], idn[:, :], start=True, stop=True)
            t.fin(p_gb1, 1)
            t.wait_ge(g_gbr, 16 * 8)
            t.matmul(pd0[:, 0:512], ones1[0:1, :], gbrow[0:1, 0:512], start=True, stop=True)
            t.matmul(pd1[:, 0:512], ones1[0:1, :], gbrow[0:1, 512:1024], start=True, stop=True)
            t.fin(p_gbt, 1)
            t.wait_ge(v_gbt, 1)
            def emit_u1t(t, i):
                t.wait_ge(v_u1, i + 1)
                if i == 0:
                    t.wait_ge(v_h1b, 1)
                    t.wait_ge(v_xbf, 1)
                    for c in range(NCH):
                        t.matmul(pa[:, c * 128:(c + 1) * 128], u1[:, c * 128:(c + 1) * 128],
                                 idn[:, :], start=True, stop=True)
                else:
                    t.wait_ge(v_rin, scb(i))
                    t.wait_ge(s_non, scb(i))
                    t.wait_ge(v_vbe, scb(i))
                    for c in range(NCH):
                        t.matmul(pb[:, c * 128:(c + 1) * 128], u1[:, c * 128:(c + 1) * 128],
                                 idn[:, :], start=True, stop=True)
                t.fin(p_u1t, 1)

            def emit_q(t, i):
                t.wait_ge(s_h1, i + 1)
                t.wait_ge(ld_w, LW["wq"])
                for mc in range(NCH):
                    for kc in range(NCH):
                        t.matmul(pb[:, mc * 128:(mc + 1) * 128],
                                 W["wq"][:, kc * D + mc * 128: kc * D + (mc + 1) * 128],
                                 h1f[:, kc * 128:(kc + 1) * 128],
                                 start=(kc == 0), stop=(kc == 7))
                        if i == 0:
                            t.matmul(pd0[:, 0 + mc:1 + mc],
                                     W["wq"][:, kc * D + mc * 128: kc * D + (mc + 1) * 128],
                                     h1bf[:, kc:kc + 1], start=(kc == 0), stop=(kc == 7))
                t.fin(p_q, 1)

            def emit_k(t, i):
                t.wait_ge(ld_w, LW["wk"])
                if i > 0:
                    t.wait_ge(s_kr, i)
                for mc in range(NCH):
                    for kc in range(NCH):
                        t.matmul(pc[:, mc * 128:(mc + 1) * 128],
                                 W["wk"][:, kc * D + mc * 128: kc * D + (mc + 1) * 128],
                                 h1f[:, kc * 128:(kc + 1) * 128],
                                 start=(kc == 0), stop=(kc == 7))
                        if i == 0:
                            t.matmul(pd0[:, 8 + mc:9 + mc],
                                     W["wk"][:, kc * D + mc * 128: kc * D + (mc + 1) * 128],
                                     h1bf[:, kc:kc + 1], start=(kc == 0), stop=(kc == 7))
                t.fin(p_k, 1)

            def emit_vp(t, i):
                t.wait_ge(ld_w, LW["wv"])
                t.wait_ge(v_qe, i + 1)
                for mc in range(NCH):
                    for kc in range(NCH):
                        t.matmul(pb[:, mc * 128:(mc + 1) * 128],
                                 W["wv"][:, kc * D + mc * 128: kc * D + (mc + 1) * 128],
                                 h1f[:, kc * 128:(kc + 1) * 128],
                                 start=(kc == 0), stop=(kc == 7))
                        if i == 0:
                            t.matmul(pd0[:, 16 + mc:17 + mc],
                                     W["wv"][:, kc * D + mc * 128: kc * D + (mc + 1) * 128],
                                     h1bf[:, kc:kc + 1], start=(kc == 0), stop=(kc == 7))
                t.fin(p_v, 1)
                if i == 0:
                    t.fin(p_bqkv, 1)

            emit_u1t(t, 0)
            emit_q(t, 0)
            emit_k(t, 0)
            emit_vp(t, 0)
            for i in range(NT):
                t.wait_ge(v_ve, i + 1)
                t.wait_ge(v_ke, i + 1)
                if i == 0:
                    t.wait_ge(v_bev, 1)
                if i == 6:
                    t.wait_ge(v_mod, 6)
                for b in range(nbatch(i)):
                    gl = batch_groups(i, b)
                    pdx = score_pd(i, b)
                    sb_ = scb(i) + b
                    is_bnd = (len(gl) == 1)
                    pb_ = (b - 1 if i == 0 else b)
                    if b == 0 and i >= 5:
                        t.wait_ge(s_h2, i)
                    if b == 0 and i >= 4:
                        t.wait_ge(s_h1, i + 1)
                    if pdx is pd0:
                        if i == 0 and b == 1:
                            t.wait_ge(v_ext, 1)
                            t.wait_ge(v_rin, 1)
                            t.wait_ge(s_non, 1)
                        if sb_ > 0:
                            t.wait_ge(s_exp, sb_ if i < 6 else sb_ - 1)
                    else:
                        if sb_ >= 2:
                            t.wait_ge(s_exp, sb_ - 1)
                    for kk, gidx in enumerate(gl):
                        t.matmul(pdx[:, kk * 128:(kk + 1) * 128], kg[:, gidx * 128:(gidx + 1) * 128],
                                 qg[:, gidx * 128:(gidx + 1) * 128], start=True, stop=True)
                    t.fin(p_sc, 1)
                    if sb_ >= 2:
                        t.wait_ge(v_vbe, sb_ - 1)
                    if b == 0 and i > 0:
                        t.wait_ge(v_ffn, i)
                    if i == 0 and not is_bnd:
                        t.wait_ge(v_h2b, 1)
                    for kk, gidx in enumerate(gl):
                        if is_bnd:
                            vt_out = pa[:, 256:320]
                        else:
                            vt_out = pa[:, (pb_ % 2) * 512 + 64 * kk:(pb_ % 2) * 512 + 64 * kk + 64]
                        t.matmul(vt_out, vg[0:64, gidx * 128:(gidx + 1) * 128],
                                 idn[0:64, 0:64], start=True, stop=True)
                    t.fin(p_vt, 1)
                    t.wait_ge(s_exp, sb_ + 1)
                    t.wait_ge(v_vbe, sb_ + 1)
                    if sb_ >= 2:
                        t.wait_ge(s_non, sb_ - 1)
                        t.wait_ge(v_rin, sb_ - 1)
                    for kk, gidx in enumerate(gl):
                        if is_bnd:
                            oa_out = pd0[:, 256:321]
                        else:
                            cb_ = (pb_ % 2) * 512 + 65 * kk
                            oa_out = pb[:, cb_:cb_ + 65]
                        t.matmul(oa_out, a2u[:, gidx * 128:(gidx + 1) * 128],
                                 vbs[:, gidx * 72:gidx * 72 + 65], start=True, stop=True)
                    t.fin(p_av, 1)
                    t.wait_ge(s_non, sb_ + 1)
                    if is_bnd:
                        t.matmul(pd0[0:64, 128:256], on_[:, 0:64], idn[:, :], start=True, stop=True)
                    else:
                        if pb_ % 2 == 0:
                            if pb_ == 0:
                                t.wait_ge(v_ke, i + 1)
                                if i > 0:
                                    t.wait_ge(s_kr, i)
                                    t.wait_ge(v_asm, 2 * i)
                            else:
                                t.wait_ge(v_asm, 2 * i + 1)
                        for gidx in gl:
                            s_ = (gidx - 1) % 8
                            t.matmul(pc[0:64, s_ * 128:(s_ + 1) * 128],
                                     on_[:, gidx * 64:(gidx + 1) * 64], idn[:, :],
                                     start=True, stop=True)
                    t.fin(p_ont, 1)
                    if i == 0 and b == 0:
                        t.fin(p_bont, 1)
                        t.wait_ge(v_ext, 1)
                        t.matmul(pa[0:1, 992:993], red_b[:, 0:1], onescol[:, 0:1], start=True, stop=True)
                        t.matmul(pa[0:1, 993:994], red_b[:, 1:2], onescol[:, 0:1], start=True, stop=True)
                        t.fin(p_bst, 1)
                        t.wait_ge(v_ext, 3)
                        t.matmul(pa[:, 994:995], ones1[0:1, :], lb_c[0:1, 1:2], start=True, stop=True)
                        t.matmul(pa[:, 995:996], ones1[0:1, :], lb_c[0:1, 4:5], start=True, stop=True)
                        t.fin(p_bbc, 1)
                t.wait_ge(v_asm, 2 * (i + 1))
                t.wait_ge(v_dga, 1)
                t.wait_ge(s_h1, i + 1)
                if i == 0:
                    t.wait_ge(v_h2b, 1)
                for c in range(NCH):
                    t.matmul(pa[:, c * 128:(c + 1) * 128], fma[:, c * 128:(c + 1) * 128],
                             dga[:, c * 128:(c + 1) * 128], start=True, stop=True)
                t.fin(p_atm, 1)
                if i + 1 < NT:
                    emit_u1t(t, i + 1)
                t.wait_ge(v_u2, i + 1)
                if i >= 4:
                    t.wait_ge(s_h1, i + 1)
                    if i == 4:
                        t.wait_ge(v_mod, 6)
                        t.wait_ge(s_exp, scb(4))
                    else:
                        t.wait_ge(s_h2, i)
                    for c in range(NCH):
                        dstp = pd0 if c < 4 else pd1
                        t.matmul(dstp[:, (c % 4) * 128:(c % 4 + 1) * 128],
                                 u2[:, c * 128:(c + 1) * 128], idn[:, :], start=True, stop=True)
                else:
                    t.wait_ge(v_x2, i + 1)
                    for c in range(NCH):
                        t.matmul(pa[:, c * 128:(c + 1) * 128], u2[:, c * 128:(c + 1) * 128],
                                 idn[:, :], start=True, stop=True)
                t.fin(p_u2t, 1)
                t.wait_ge(v_xr, i + 1)
                t.wait_ge(ld_w, LW["wr"])
                for hh2 in range(2):
                    for kc in range(NCH):
                        t.matmul(pa[:, hh2 * 512:hh2 * 512 + 512],
                                 xr[:, kc * 128:(kc + 1) * 128],
                                 W["wr"][:, kc * D + hh2 * 512: kc * D + hh2 * 512 + 512],
                                 start=(kc == 0), stop=(kc == 7))
                t.fin(p_r, 1)
                t.wait_ge(ld_w, LW["wkf"])
                t.wait_ge(v_asm, 2 * (i + 1))
                for mc in range(NCH):
                    for kc in range(NCH):
                        t.matmul(pc[:, mc * 128:(mc + 1) * 128],
                                 W["wkf"][:, kc * D + mc * 128: kc * D + (mc + 1) * 128],
                                 xk[:, kc * 128:(kc + 1) * 128],
                                 start=(kc == 0), stop=(kc == 7))
                t.fin(p_kf, 1)
                if i + 1 < NT:
                    emit_q(t, i + 1)
                    emit_k(t, i + 1)
                t.wait_ge(ld_w, LW["wvf"])
                t.wait_ge(s_kr, i + 1)
                t.wait_ge(s_rt, i + 1)
                for hh2 in range(2):
                    for kc in range(NCH):
                        t.matmul(pa[:, hh2 * 512:hh2 * 512 + 512],
                                 ksq[:, kc * 128:(kc + 1) * 128],
                                 W["wvf"][:, kc * D + hh2 * 512: kc * D + hh2 * 512 + 512],
                                 start=(kc == 0), stop=(kc == 7))
                t.fin(p_vf, 1)
                if i + 1 < NT:
                    emit_vp(t, i + 1)

        if DEBUG_DUMP:
            dbg_sem = sem("dbg_sem")
            name2sb = dict(vg=vg[0:64, :],
                           h2fv=h2f[:, :].rearrange("p (c w) -> p c w", w=132)[:, :, 0:129],
                           xin0=xin[0], xin1=xin[1], xin2=xin[2], u1=u1, u2=u2,
                           h1f=h1f, dt=dt_, xr=xr, xk=xk, rt=rt,
                           krel=krel, ksq=ksq, ffnf=ffnf, qg=qg, kg=kg,
                           a2u=a2u, vbs=vbs, on_=on_, fma=fma, rinv=rinv,
                           modf=modf, dga=dga, gbT=gbT, sT=sT, h1bf=h1bf,
                           x2b0=x2b_[0], x2b1=x2b_[1], yb0=yb_[0], yb1=yb_[1],
                           rstdb=rstdb, h2bf=h2bf, xbf=xbf, atnb=atnb,
                           x2bf=x2bf, lb_c=lb_c, ln_c=ln_c, ln_s=ln_s)
            dbg_outs = {}
            def _ap(x):
                return x if isinstance(x, bass.AP) else x[:, :]
            for nm in DEBUG_DUMP:
                ap_t = _ap(name2sb[nm])
                dbg_outs[nm] = nc.dram_tensor("dbg_" + nm, list(ap_t.shape), ap_t.dtype,
                                              kind="ExternalOutput").ap()

            @block.sync
            def _(s):
                s.wait_ge(st_y, 16 * NT)
                nd = 0
                for nm in DEBUG_DUMP:
                    s.dma_start(out=dbg_outs[nm], in_=_ap(name2sb[nm])).then_inc(dbg_sem, 16)
                    nd += 1
                s.wait_ge(dbg_sem, 16 * nd)

    return nc


def _host_prep(inputs):
    bf = ml_dtypes.bfloat16
    x = np.ascontiguousarray(np.asarray(inputs["L0_attn_x"], dtype=np.float32))
    B, S, _ = x.shape
    perm = np.concatenate([np.r_[mc * 64:(mc + 1) * 64, (mc + 8) * 64:(mc + 9) * 64]
                           for mc in range(8)])
    wq = np.ascontiguousarray(np.asarray(inputs["L0_attn_Wq"], np.float32)[:, perm])
    wk = np.ascontiguousarray(np.asarray(inputs["L0_attn_Wk"], np.float32)[:, perm])
    wv = np.ascontiguousarray(np.asarray(inputs["L0_attn_Wv"], np.float32)[:, perm])
    bq = np.asarray(inputs["L0_attn_bq"], np.float32)[perm]
    bk = np.asarray(inputs["L0_attn_bk"], np.float32)[perm]
    bv = np.asarray(inputs["L0_attn_bv"], np.float32)[perm]
    wr = np.ascontiguousarray(np.asarray(inputs["L0_ffn_W_r"], np.float32).T)
    wkf = np.ascontiguousarray(np.asarray(inputs["L0_ffn_W_k"], np.float32).T)
    wvf = np.ascontiguousarray(np.asarray(inputs["L0_ffn_W_v"], np.float32).T)
    wmod = np.ascontiguousarray(np.asarray(inputs["L0_wmod"], np.float32))
    cc = np.asarray(inputs["L0_c"], np.float32)
    mur = np.ascontiguousarray(np.asarray(inputs["L0_ffn_mu_r"], np.float32).reshape(8, 128).T)
    muk = np.ascontiguousarray(np.asarray(inputs["L0_ffn_mu_k"], np.float32).reshape(8, 128).T)

    def bias_pat(b):
        p = np.zeros((64, 16), np.float32)
        for mc in range(8):
            p[:, mc] = b[mc * 128: mc * 128 + 64]
            p[:, 8 + mc] = b[mc * 128 + 64: (mc + 1) * 128]
        return p

    consts = np.zeros((128, 16), np.float32)
    for col, val in [(C_EPS, 1e-5), (C_NHALF, -0.5), (C_QSC, 0.125), (C_NSH, -20.0),
                     (C_HALF, 0.5), (C_NEG1, -1.0), (C_RD, 1.0 / 1024.0), (C_ONE, 1.0),
                     (C_QTR, 0.25), (C_NQTR, -0.25), (C_15, 1.5)]:
        consts[:, col] = val
    consts[:, C_QK] = np.full(128, 0x5F3759DF, np.int32).view(np.float32)
    consts[:, C_SH1] = np.full(128, 1, np.int32).view(np.float32)
    idn = np.eye(128, dtype=np.float32).astype(bf)
    qs = np.zeros((64, GSLOT * 128), np.float32)
    ks = np.zeros((64, GSLOT * 128), np.float32)
    col_t = np.arange(GSLOT * 128) % 8
    for tt in range(8):
        qs[tt, col_t == tt] = 1.0
        ks[tt, col_t == tt] = 1024.0
    qs[8, :] = 1.0
    ks[8, :] = -1024.0
    qs = qs.astype(bf)
    ks = ks.astype(bf)
    ones1 = np.ones((1, 128), np.float32)
    ones1b = np.ones((1, 128), np.float32).astype(bf)
    onescol = np.ones((128, 1), np.float32)

    bv_nat = np.asarray(inputs["L0_attn_bv"], np.float32)
    bvt = np.ascontiguousarray(np.repeat(bv_nat.reshape(16, 1, 64), 8, axis=1).reshape(128, 64))
    shared = dict(wq=wq, wk=wk, wv=wv, wr=wr, wkf=wkf, wvf=wvf, wmod=wmod,
                  consts=consts, idn=idn, qstat=qs, kstat=ks, bvt=bvt,
                  bqp=bias_pat(bq), bkp=bias_pat(bk), bvp=bias_pat(bv),
                  mur=mur, muk=muk, ones1=ones1, ones1b=ones1b, onescol=onescol)
    in_maps = []
    for ci in range(8):
        bidx, half = ci // 2, ci % 2
        t0 = half * 2048
        m = dict(shared)
        m["x"] = np.ascontiguousarray(x[bidx, t0:t0 + 2048])
        m["xbnd"] = (np.zeros((1, D), np.float32) if half == 0
                     else np.ascontiguousarray(x[bidx, 2047:2048]))
        m["flag"] = np.full((128, 1), float(half), np.float32)
        m["crow"] = np.ascontiguousarray(cc[bidx:bidx + 1])
        in_maps.append(m)
    return in_maps


def kernel(**inputs):
    in_maps = _host_prep(inputs)
    nc = build_nc()
    res = run_bass_kernel_spmd(nc, in_maps, core_ids=list(range(8)))
    x = np.asarray(inputs["L0_attn_x"])
    B, S, _ = x.shape
    out = np.empty((B, S, D), np.float32)
    for ci in range(8):
        bidx, half = ci // 2, ci % 2
        out[bidx, half * 2048:(half + 1) * 2048] = res.results[ci]["y"]
    return out


# version 14
# speedup vs baseline: 1.0993x; 1.0147x over previous
import numpy as np
import ml_dtypes
import contextlib
import concourse.bass as bass
import concourse.mybir as mybir
from concourse.bass_utils import run_bass_kernel_spmd

F32 = mybir.dt.float32
BF16 = mybir.dt.bfloat16
AF = mybir.ActivationFunctionType
OP = mybir.AluOpType
AX = mybir.AxisListType

D = 1024
NT = 16
TPT = 128
NCH = 8
GSLOT = 17

(C_EPS, C_NHALF, C_QSC, C_NSH, C_HALF, C_NEG1, C_RD, C_ONE, C_QTR,
 C_NQTR, C_QK, C_SH1, C_15, C_X0, C_X1, C_X2) = range(16)

MVNAME = ["sa", "ca", "ga", "sb", "cb", "gb"]

DEBUG_DUMP = []


class _Trk:

    _OPS = {"tensor_copy", "tensor_tensor", "tensor_scalar", "memset",
            "scalar_tensor_tensor", "reciprocal", "reduce_sum", "reduce_max",
            "tensor_reduce", "bn_stats", "bn_aggr", "activation", "copy",
            "add", "mul", "matmul", "transpose"}

    def __init__(self, eng):
        object.__setattr__(self, "_e", eng)
        object.__setattr__(self, "last", None)
        object.__setattr__(self, "_finned", None)
        object.__setattr__(self, "_pend", set())

    @staticmethod
    def _apname(x):
        t = getattr(x, "tensor", None)
        nm = getattr(t, "name", None)
        return nm

    def __getattr__(self, n):
        eng = object.__getattribute__(self, "_e")
        a = getattr(eng, n)
        if not callable(a) or n in ("wait_ge",):
            return a
        trk = self

        def w(*args, **kw):
            if n in _Trk._OPS:
                pend = object.__getattribute__(trk, "_pend")
                names = []
                for x in list(args) + list(kw.values()):
                    nm = _Trk._apname(x)
                    if nm is not None:
                        names.append(nm)
                outs = names[:1]
                if "accum_out" in kw and kw["accum_out"] is not None:
                    nm = _Trk._apname(kw["accum_out"])
                    if nm:
                        outs.append(nm)
                ins = names[1:]
                if any(nm in pend for nm in ins):
                    eng.drain()
                    pend.clear()
                r = a(*args, **kw)
                for nm in outs:
                    pend.add(nm)
            else:
                r = a(*args, **kw)
            if r is not None and hasattr(r, "then_inc"):
                object.__setattr__(trk, "last", r)
            return r
        return w

    def fin(self, sem_, n_=1):
        li = object.__getattribute__(self, "last")
        if li is object.__getattribute__(self, "_finned"):
            object.__getattribute__(self, "_e").drain().then_inc(sem_, n_)
        else:
            li.then_inc(sem_, n_)
            object.__setattr__(self, "_finned", li)



def nbatch(i):
    return 5 if i == 0 else 4


def scb(i):
    return sum(nbatch(j) for j in range(i))


def batch_groups(i, b):
    if i == 0 and b == 0:
        return [0]
    bb = b - 1 if i == 0 else b
    return [1 + 4 * bb + k for k in range(4)]



def _emit_rsqrt(v, consts, out_col, var_ap, eps_bias, scr):
    I32 = mybir.dt.int32
    P = var_ap.shape[0]
    ve = scr[0:P, 0:1]
    t_ = scr[0:P, 1:2]
    v.tensor_scalar(ve, var_ap, eps_bias, None, OP.add)
    v.tensor_tensor(out_col.bitcast(I32), ve.bitcast(I32),
                    consts[0:P, C_SH1:C_SH1 + 1].bitcast(I32), OP.logical_shift_right)
    v.tensor_tensor(out_col.bitcast(I32), consts[0:P, C_QK:C_QK + 1].bitcast(I32),
                    out_col.bitcast(I32), OP.subtract)
    for _ in range(2):
        v.tensor_tensor(t_, out_col, out_col, OP.mult)
        v.tensor_tensor(t_, t_, ve, OP.mult)
        v.tensor_scalar(t_, t_, consts[0:P, C_NHALF:C_NHALF + 1], consts[0:P, C_15:C_15 + 1],
                        OP.mult, OP.add)
        v.tensor_tensor(out_col, out_col, t_, OP.mult)


def build_nc():
    nc = bass.Bass("TRN2", target_bir_lowering=False, detect_race_conditions=False)
    x_h = nc.dram_tensor("x", [NT * TPT, D], F32, kind="ExternalInput").ap()
    xb_h = nc.dram_tensor("xbnd", [1, D], F32, kind="ExternalInput").ap()
    flag_h = nc.dram_tensor("flag", [128, 1], F32, kind="ExternalInput").ap()
    w_h = {}
    for w in ["wq", "wk", "wv", "wr", "wkf", "wvf"]:
        w_h[w] = nc.dram_tensor(w, [D, D], F32, kind="ExternalInput").ap()
    wmod_h = nc.dram_tensor("wmod", [D, 6 * D], F32, kind="ExternalInput").ap()
    crow_h = nc.dram_tensor("crow", [1, D], F32, kind="ExternalInput").ap()
    consts_h = nc.dram_tensor("consts", [128, 16], F32, kind="ExternalInput").ap()
    idn_h = nc.dram_tensor("idn", [128, 128], BF16, kind="ExternalInput").ap()
    qstat_h = nc.dram_tensor("qstat", [64, GSLOT * 128], BF16, kind="ExternalInput").ap()
    kstat_h = nc.dram_tensor("kstat", [64, GSLOT * 128], BF16, kind="ExternalInput").ap()
    bqp_h = nc.dram_tensor("bqp", [64, 16], F32, kind="ExternalInput").ap()
    bkp_h = nc.dram_tensor("bkp", [64, 16], F32, kind="ExternalInput").ap()
    bvp_h = nc.dram_tensor("bvp", [64, 16], F32, kind="ExternalInput").ap()
    bvt_h = nc.dram_tensor("bvt", [128, 64], F32, kind="ExternalInput").ap()
    mur_h = nc.dram_tensor("mur", [128, 8], F32, kind="ExternalInput").ap()
    muk_h = nc.dram_tensor("muk", [128, 8], F32, kind="ExternalInput").ap()
    ones1_h = nc.dram_tensor("ones1", [1, 128], F32, kind="ExternalInput").ap()
    ones1b_h = nc.dram_tensor("ones1b", [1, 128], BF16, kind="ExternalInput").ap()
    onescol_h = nc.dram_tensor("onescol", [128, 1], F32, kind="ExternalInput").ap()
    y_h = nc.dram_tensor("y", [NT * TPT, D], F32, kind="ExternalOutput").ap()

    es = contextlib.ExitStack()
    sbuf = lambda n, s, d: es.enter_context(nc.sbuf_tensor(n, s, d))
    psum = lambda n, s, d: es.enter_context(nc.psum_tensor(n, s, d))
    sem = lambda n: es.enter_context(nc.semaphore(n))
    with es:
        W = {w: sbuf(w + "s", [128, 8 * D], BF16) for w in ["wq", "wk", "wv", "wr", "wkf", "wvf"]}
        wmb = [sbuf(f"wmb{i}", [128, 4 * 512], BF16) for i in range(2)]
        xin = [sbuf(f"xin{i}", [128, D], F32) for i in range(3)]
        x2b_ = [sbuf(f"x2b{i}", [128, D], F32) for i in range(2)]
        yb_ = [sbuf(f"yb{i}", [128, D], F32) for i in range(2)]
        u1 = sbuf("u1", [128, D], BF16)
        u2 = sbuf("u2", [128, D], BF16)
        h1f = sbuf("h1f", [128, D], BF16)
        h2f = sbuf("h2f", [128, 8 * 132], BF16)
        dt_ = sbuf("dt", [128, D], BF16)
        xr = sbuf("xr", [128, D], BF16)
        xk = sbuf("xk", [128, D], BF16)
        rt = sbuf("rt", [128, D], BF16)
        krel = sbuf("krel", [128, D], BF16)
        ksq = sbuf("ksq", [128, D], BF16)
        ffnf = sbuf("ffnf", [128, D], BF16)
        qg = sbuf("qg", [128, GSLOT * 128], BF16)
        kg = sbuf("kg", [128, GSLOT * 128], BF16)
        vg = sbuf("vg", [128, GSLOT * 128], BF16)
        a2u = sbuf("a2u", [128, GSLOT * 128], BF16)
        vbs = sbuf("vbs", [128, GSLOT * 72], BF16)
        on_ = sbuf("on", [128, GSLOT * 64], BF16)
        fma = sbuf("fma", [128, D], BF16)
        rinv = sbuf("rinv", [128, GSLOT], F32)
        idn = sbuf("idn_s", [128, 128], BF16)
        consts = sbuf("consts_s", [128, 16], F32)
        bqp = sbuf("bqp_s", [64, 16], F32)
        bkp = sbuf("bkp_s", [64, 16], F32)
        bvp = sbuf("bvp_s", [64, 16], F32)
        bvt = sbuf("bvt_s", [128, 64], F32)
        murf = sbuf("murf", [128, 8], F32)
        mukf = sbuf("mukf", [128, 8], F32)
        modf = sbuf("modf", [128, 48], F32)
        clsc = sbuf("clsc", [128, 24], F32)
        dga = sbuf("dga", [128, 8 * 128], BF16)
        gbT = sbuf("gbT", [128, D], BF16)
        gbrow = sbuf("gbrow", [1, D], F32)
        gbc = sbuf("gbc", [128, 8], BF16)
        gbT8 = sbuf("gbT8", [8, 128], F32)
        ln_s = sbuf("ln_s", [128, 28], F32)
        ln_c = sbuf("ln_c", [128, 4], F32)
        crow = sbuf("crow_s", [1, D], F32)
        silcb = sbuf("silcb", [1, D], BF16)
        sT = sbuf("sT", [128, 8], BF16)
        modrow = sbuf("modrow", [1, D], F32)
        xbnd = sbuf("xbnd_s", [1, D], F32)
        u1b = sbuf("u1b", [1, D], BF16)
        lb_s = sbuf("lb_s", [1, 14], F32)
        lb_c = sbuf("lb_c", [1, 6], F32)
        h1bf = sbuf("h1bf", [128, 8], BF16)
        xbf = sbuf("xbf", [128, 8], F32)
        atnb = sbuf("atnb", [128, 8], F32)
        x2bf = sbuf("x2bf", [128, 8], F32)
        x2sq = sbuf("x2sq", [128, 8], F32)
        red_b = sbuf("red_b", [128, 2], F32)
        rstdb = sbuf("rstdb", [128, 2], F32)
        h2bf = sbuf("h2bf", [128, 8], BF16)
        flagc = sbuf("flagc", [128, 1], F32)
        ones1 = sbuf("ones1_s", [1, 128], F32)
        ones1b = sbuf("ones1b_s", [1, 128], BF16)
        onescol = sbuf("onescol_s", [128, 1], F32)

        pa = psum("pa", [128, 1024], F32)
        pb = psum("pb", [128, 1024], F32)
        pc = psum("pc", [128, 1024], F32)
        pd0 = psum("pd0", [128, 512], F32)
        pd1 = psum("pd1", [128, 512], F32)

        ld_s, ld_x, ld_w, st_y = sem("ld_s"), sem("ld_x"), sem("ld_w"), sem("st_y")
        p_sT, p_xbf, p_mvh, p_mtr = sem("p_sT"), sem("p_xbf"), sem("p_mvh"), sem("p_mtr")
        p_q, p_k, p_v, p_bqkv = sem("p_q"), sem("p_k"), sem("p_v"), sem("p_bqkv")
        p_sc, p_vt, p_av, p_ont = sem("p_sc"), sem("p_vt"), sem("p_av"), sem("p_ont")
        p_bont, p_bst, p_bbc = sem("p_bont"), sem("p_bst"), sem("p_bbc")
        p_atm, p_u2t, p_u1t = sem("p_atm"), sem("p_u2t"), sem("p_u1t")
        p_r, p_kf, p_vf = sem("p_r"), sem("p_kf"), sem("p_vf")
        g_gbr, p_gbt, v_gbt = sem("g_gbr"), sem("p_gbt"), sem("v_gbt")
        p_gb1, v_gb8 = sem("p_gb1"), sem("v_gb8")
        s_h2h, v_xrh = sem("s_h2h"), sem("v_xrh")
        v_lnb, v_u1b, v_silu, v_sT = sem("v_lnb"), sem("v_u1b"), sem("v_silu"), sem("v_sT")
        v_mve, v_mod, v_dga = sem("v_mve"), sem("v_mod"), sem("v_dga")
        v_xbf, v_h1b = sem("v_xbf"), sem("v_h1b")
        v_ln1, v_u1, v_qe, v_ke, v_ve = sem("v_ln1"), sem("v_u1"), sem("v_qe"), sem("v_ke"), sem("v_ve")
        v_vbe, v_rin, v_asm, v_x2 = sem("v_vbe"), sem("v_rin"), sem("v_asm"), sem("v_x2")
        v_ln2, v_u2, v_xr, v_ffn, v_y = sem("v_ln2"), sem("v_u2"), sem("v_xr"), sem("v_ffn"), sem("v_y")
        v_bev, v_ext, v_h2b = sem("v_bev"), sem("v_ext"), sem("v_h2b")
        s_silu, s_lb1, s_lb2, s_ln = sem("s_silu"), sem("s_lb1"), sem("s_lb2"), sem("s_ln")
        s_h1, s_exp, s_non, s_h2 = sem("s_h1"), sem("s_exp"), sem("s_non"), sem("s_h2")
        s_rt, s_kr, s_asm = sem("s_rt"), sem("s_kr"), sem("s_asm")

        block = es.enter_context(nc.Block())

        def score_pd(i, b):
            if i < 6 or (i == 0 and b == 0):
                return pd0
            return pd0 if b % 2 == 0 else pd1

        def mv_loads(nm):
            return [nm + jh + h for jh in "01" for h in "01"]
        dma_order = (mv_loads("sa") + mv_loads("ca") + ["wq"] + mv_loads("ga")
                     + ["wk", "wv"] + mv_loads("sb") + mv_loads("cb")
                     + ["wr", "wkf", "wvf"] + mv_loads("gb"))
        LW = {nm: 16 * (k + 1) for k, nm in enumerate(dma_order)}

        @block.scalar
        def _(s):
            for dst, src in [(consts[:], consts_h), (flagc[:], flag_h),
                             (crow[:], crow_h), (xbnd[:], xb_h),
                             (bqp[:], bqp_h), (bkp[:], bkp_h), (bvp[:], bvp_h), (bvt[:], bvt_h),
                             (murf[:], mur_h), (mukf[:], muk_h),
                             (idn[:], idn_h), (ones1[:], ones1_h),
                             (ones1b[:], ones1b_h), (onescol[:], onescol_h),
                             (qg[64:128, :], qstat_h), (kg[64:128, :], kstat_h)]:
                s.dma_start(out=dst, in_=src).then_inc(ld_s, 16)
        NLD_S = 16

        @block.sync
        def _(s):
            for i in range(min(3, NT)):
                s.dma_start(out=xin[i % 3][:], in_=x_h[i * TPT:(i + 1) * TPT, :]).then_inc(ld_x, 16)
            for i in range(NT):
                s.wait_ge(v_y, i + 1)
                s.dma_start(out=y_h[i * TPT:(i + 1) * TPT, :], in_=yb_[i % 2][:]).then_inc(st_y, 16)
                j = i + 3
                if j < NT:
                    s.dma_start(out=xin[j % 3][:], in_=x_h[j * TPT:(j + 1) * TPT, :]).then_inc(ld_x, 16)
            s.wait_ge(st_y, 16 * NT)

        @block.gpsimd
        def _(g):
            for nm in dma_order:
                if nm in W:
                    g.dma_start(out=W[nm][:, :].rearrange("p (kc d) -> p kc d", d=D),
                                in_=w_h[nm].rearrange("(kc p) d -> p kc d", p=128)).then_inc(ld_w, 16)
                else:
                    v = MVNAME.index(nm[:2])
                    jh, h = int(nm[2]), int(nm[3])
                    ell = 4 * v + 2 * jh + h
                    if ell >= 2:
                        g.wait_ge(p_mvh, 2 * v + jh)
                    src = wmod_h[h * 512:(h + 1) * 512, v * D + jh * 512: v * D + jh * 512 + 512]
                    g.dma_start(out=wmb[h][:, :].rearrange("p (ic j) -> p ic j", j=512),
                                in_=src.rearrange("(ic p) j -> p ic j", p=128)).then_inc(ld_w, 16)
            g.wait_ge(v_gb8, 1)
            for c in range(NCH):
                g.dma_start(out=gbrow[0:1, c * 128:(c + 1) * 128],
                            in_=gbT8[c:c + 1, :]).then_inc(g_gbr, 16)

        @block.scalar
        def _(s):
            s = _Trk(s)
            s.wait_ge(ld_s, 16 * NLD_S)
            s.activation(modrow[:], crow[:], AF.Silu).then_inc(s_silu, 1)

            def emit_h1(s, i):
                s.wait_ge(p_u1t, i + 1)
                s.wait_ge(v_mod, 2)
                for c in range(NCH):
                    srcp = (pa if i == 0 else pb)[:, c * 128:(c + 1) * 128]
                    s.activation(h1f[:, c * 128:(c + 1) * 128], srcp,
                                 AF.Identity, scale=modf[:, 8 + c:9 + c], bias=modf[:, c:c + 1])
                s.fin(s_h1, 1)

            emit_h1(s, 0)
            for i in range(NT):
                for b in range(nbatch(i)):
                    gl = batch_groups(i, b)
                    s.wait_ge(p_sc, scb(i) + b + 1)
                    pdx = score_pd(i, b)
                    ncols = 128 * len(gl)
                    s.activation(a2u[:, gl[0] * 128: gl[0] * 128 + ncols], pdx[:, 0:ncols],
                                 AF.Exp, scale=consts[:, C_QSC:C_QSC + 1],
                                 bias=consts[:, C_NSH:C_NSH + 1]).then_inc(s_exp, 1)
                    s.wait_ge(p_av, scb(i) + b + 1)
                    s.wait_ge(v_rin, scb(i) + b + 1)
                    for gidx in gl:
                        if gidx == 0:
                            src = pd0[:, 256:320]
                        else:
                            pb_ = (gidx - 1) // 4
                            k = (gidx - 1) % 4
                            cb = (pb_ % 2) * 512 + k * 65
                            src = pb[:, cb:cb + 64]
                        s.activation(on_[:, gidx * 64:(gidx + 1) * 64], src,
                                     AF.Identity, scale=rinv[:, gidx:gidx + 1])
                    s.fin(s_non, 1)
                s.wait_ge(p_u2t, i + 1)
                s.wait_ge(v_mod, 5)
                for c in range(NCH):
                    if i >= 4:
                        srcp = (pd0 if c < 4 else pd1)[:, (c % 4) * 128:(c % 4 + 1) * 128]
                    else:
                        srcp = pa[:, c * 128:(c + 1) * 128]
                    s.activation(h2f[:, c * 132 + 1: c * 132 + 129], srcp,
                                 AF.Identity, scale=modf[:, 32 + c:33 + c], bias=modf[:, 24 + c:25 + c])
                    if c == 3:
                        s.fin(s_h2h, 1)
                s.fin(s_h2, 1)
                if i + 1 < NT:
                    emit_h1(s, i + 1)
                s.wait_ge(p_r, i + 1)
                s.activation(rt[:], pa[:, :], AF.Tanh, scale=consts[:, C_HALF:C_HALF + 1]).then_inc(s_rt, 1)
                s.wait_ge(p_kf, i + 1)
                s.activation(krel[:], pc[:, :], AF.Relu)
                s.activation(ksq[:], krel[:], AF.Square).then_inc(s_kr, 1)

        @block.vector
        def _(v):
            v = _Trk(v)
            v.wait_ge(ld_s, 16 * NLD_S)
            v.memset(qg[0:64, 0:128], 0.0)
            v.memset(kg[0:64, 0:128], 0.0)
            v.memset(vg[0:64, 0:128], 0.0)
            v.memset(vbs[:, :], 0.0)
            v.memset(vbs[:, :].rearrange("p (s c) -> p s c", c=72)[:, :, 64:65], 1.0)
            v.memset(h2f[:, :].rearrange("p (s c) -> p s c", c=132)[:, :, 0:1], 0.0)
            v.bn_stats(lb_s[:, 0:6], xbnd[0:1, 0:512])
            v.bn_stats(lb_s[:, 6:12], xbnd[0:1, 512:1024])
            v.bn_aggr(lb_s[:, 12:14], lb_s[:, 0:12]).then_inc(v_lnb, 1)
            _emit_rsqrt(v, consts, lb_c[0:1, 0:1], lb_s[0:1, 13:14],
                        consts[0:1, C_EPS:C_EPS + 1], clsc)
            v.tensor_scalar(lb_c[0:1, 5:6], lb_s[0:1, 12:13], lb_c[0:1, 0:1],
                            consts[0:1, C_NEG1:C_NEG1 + 1], OP.mult, OP.mult)
            v.tensor_scalar(u1b[:], xbnd[:], lb_c[0:1, 0:1], lb_c[0:1, 5:6],
                            OP.mult, OP.add).then_inc(v_u1b, 1)
            v.wait_ge(s_silu, 1)
            v.tensor_copy(silcb[:], modrow[:]).then_inc(v_silu, 1)
            v.wait_ge(p_sT, 1)
            v.tensor_copy(sT[:, :], pd0[:, 480:488]).then_inc(v_sT, 1)
            v.wait_ge(p_xbf, 1)
            v.tensor_copy(xbf[:, :], pa[:, 1000:1008]).then_inc(v_xbf, 1)
            for mv in range(6):
                for jh in range(2):
                    v.wait_ge(p_mvh, 2 * mv + jh + 1)
                    v.tensor_copy(modrow[0:1, jh * 512:(jh + 1) * 512], pd1[0:1, 0:512]).then_inc(v_mve, 1)
                v.wait_ge(p_mtr, mv + 1)
                v.tensor_copy(modf[:, 8 * mv:8 * mv + 8], pd1[:, 384:392])
                if mv in (1, 4):
                    v.tensor_scalar(modf[:, 8 * mv:8 * mv + 8], modf[:, 8 * mv:8 * mv + 8],
                                    consts[:, C_ONE:C_ONE + 1], None, OP.add)
                if mv in (2, 5):
                    cf = 8 * mv
                    gsl = modf[:, cf:cf + 8]
                    v.tensor_scalar(clsc[:, 0:8], gsl, 0.0, None, OP.is_ge)
                    v.tensor_scalar(clsc[:, 8:16], gsl, consts[:, C_QTR:C_QTR + 1], None, OP.max)
                    v.tensor_scalar(clsc[:, 16:24], gsl, consts[:, C_NQTR:C_NQTR + 1], None, OP.min)
                    v.tensor_tensor(clsc[:, 8:16], clsc[:, 8:16], clsc[:, 16:24], OP.subtract)
                    v.tensor_tensor(clsc[:, 8:16], clsc[:, 8:16], clsc[:, 0:8], OP.mult)
                    v.tensor_tensor(gsl, clsc[:, 8:16], clsc[:, 16:24], OP.add)
                    if mv == 5:
                        v.tensor_scalar(gsl, gsl, consts[:, C_HALF:C_HALF + 1], None, OP.mult)
                    if mv == 2:
                        for c in range(NCH):
                            v.tensor_scalar(dga[:, c * 128:(c + 1) * 128], idn[:, :],
                                            modf[:, cf + c:cf + c + 1], None, OP.mult)
                    else:
                        v.tensor_copy(gbc[:, :], gsl)
                    v.fin(v_dga, 1)
                v.fin(v_mod, 1)
                if mv == 1:
                    v.tensor_tensor(x2sq[:, 0:8], pa[:, 1008:1016], modf[:, 8:16], OP.mult)
                    v.tensor_tensor(x2sq[:, 0:8], x2sq[:, 0:8], modf[:, 0:8], OP.add)
                    v.tensor_copy(h1bf[:, :], x2sq[:, 0:8]).then_inc(v_h1b, 1)
            def emit_ln1(v, i):
                v.wait_ge(ld_x, 16 * (i + 1))
                v.wait_ge(p_u1t, i)
                v.bn_stats(ln_s[:, 0:6], xin[i % 3][:, 0:512])
                v.bn_stats(ln_s[:, 6:12], xin[i % 3][:, 512:1024])
                v.bn_aggr(ln_s[:, 24:26], ln_s[:, 0:12]).then_inc(v_ln1, 1)
                _emit_rsqrt(v, consts, ln_c[:, 0:1], ln_s[:, 25:26],
                            consts[:, C_EPS:C_EPS + 1], clsc)
                v.tensor_scalar(ln_c[:, 2:3], ln_s[:, 24:25], ln_c[:, 0:1],
                                consts[:, C_NEG1:C_NEG1 + 1], OP.mult, OP.mult)
                v.tensor_scalar(u1[:], xin[i % 3][:], ln_c[:, 0:1], ln_c[:, 2:3],
                                OP.mult, OP.add).then_inc(v_u1, 1)

            v.wait_ge(p_gb1, 1)
            v.tensor_copy(gbT8[:, :], pd1[0:8, 384:512])
            v.fin(v_gb8, 1)
            v.wait_ge(p_gbt, 1)
            v.tensor_copy(gbT[:, 0:512], pd0[:, 0:512])
            v.tensor_copy(gbT[:, 512:1024], pd1[:, 0:512])
            v.fin(v_gbt, 1)

            def emit_evac(v, i, which):
                psrc, gdst, bpat, vsem, pwait = {
                    "q": (pb, qg, bqp, v_qe, p_q), "k": (pc, kg, bkp, v_ke, p_k),
                    "v": (pb, vg, None, v_ve, p_v)}[which]
                v.wait_ge(pwait, i + 1)
                for par in range(2):
                    in3 = psrc[64 * par:64 * par + 64, :].rearrange(
                        "p (mc g t) -> p mc g t", g=16, t=8)
                    out3 = gdst[0:64, :].rearrange("p (g jj t) -> p g jj t", jj=16, t=8)[
                        :, 1:GSLOT, 8 * par:8 * par + 8, :].rearrange("p g j t -> p j g t")
                    if bpat is None:
                        v.tensor_copy(out3, in3)
                    else:
                        bias = bpat[:, 8 * par:8 * par + 8].unsqueeze(2).unsqueeze(3).broadcast_to([64, 8, 16, 8])
                        v.tensor_tensor(out3, in3, bias, OP.add)
                v.fin(vsem, 1)

            emit_ln1(v, 0)
            emit_evac(v, 0, "q")
            emit_evac(v, 0, "k")
            for i in range(NT):
                emit_evac(v, i, "v")
                if i == 0:
                    v.wait_ge(p_bqkv, 1)
                    for (tix, gdst, bpat) in [(0, qg, bqp), (1, kg, bkp)]:
                        for par in range(2):
                            in2 = pd0[64 * par:64 * par + 64, 8 * tix:8 * tix + 8]
                            out2 = gdst[0:64, :].rearrange("p (g jj t) -> p g jj t", jj=16, t=8)[
                                :, 0, 8 * par:8 * par + 8, 0]
                            v.tensor_tensor(out2, in2, bpat[:, 8 * par:8 * par + 8], OP.add)
                    for par in range(2):
                        in2 = pd0[64 * par:64 * par + 64, 16:24]
                        out2 = vg[0:64, :].rearrange("p (g jj t) -> p g jj t", jj=16, t=8)[
                            :, 0, 8 * par:8 * par + 8, 0]
                        v.tensor_copy(out2, in2)
                    v.fin(v_bev, 1)
                for b in range(nbatch(i)):
                    gl = batch_groups(i, b)
                    v.wait_ge(p_vt, scb(i) + b + 1)
                    if len(gl) == 1:
                        v.tensor_tensor(vbs[:, 0:64], pa[:, 256:320], bvt[:, :], OP.add)
                    else:
                        pb_ = b - 1 if i == 0 else b
                        base = (pb_ % 2) * 512
                        dstv = vbs[:, :].rearrange("p (s c) -> p s c", c=72)[:, gl[0]:gl[0] + 4, 0:64]
                        srcv = pa[:, base:base + 256].rearrange("p (kk c) -> p kk c", c=64)
                        bias4 = bvt[:, :].unsqueeze(1).broadcast_to([128, 4, 64])
                        v.tensor_tensor(dstv, srcv, bias4, OP.add)
                    v.fin(v_vbe, 1)
                    v.wait_ge(p_av, scb(i) + b + 1)
                    if len(gl) == 1:
                        v.reciprocal(rinv[:, 0:1], pd0[:, 320:321])
                    else:
                        pb_ = b - 1 if i == 0 else b
                        base = (pb_ % 2) * 512 + 64
                        src_s = pb[:, base - 64:base - 64 + 260].rearrange("p (s c) -> p s c", c=65)[:, :, 64]
                        v.reciprocal(rinv[:, gl[0]:gl[0] + 4], src_s)
                    v.fin(v_rin, 1)
                    pb_h = b - 1 if i == 0 else b
                    if pb_h in (1, 3) and len(gl) == 4:
                        hh = pb_h // 2
                        bofs = 1 if i == 0 else 0
                        v.wait_ge(p_ont, scb(i) + bofs + 2 * (hh + 1))
                        for par in range(2):
                            srca = pc[0:64, :].rearrange(
                                "p (g c2 pr t) -> p c2 pr g t", t=8, pr=2, c2=8)[:, :, par, :, :]
                            dsta = fma[64 * par:64 * par + 64, :].rearrange(
                                "p (c h2 g t) -> p c h2 g t", h2=2, g=8, t=8)[:, :, hh, :, :]
                            v.tensor_copy(dsta, srca)
                        v.fin(v_asm, 1)
                        if pb_h == 1 and i + 1 < NT:
                            emit_ln1(v, i + 1)
                    if i == 0 and b == 0:
                        v.wait_ge(p_bont, 1)
                        for par in range(2):
                            src2 = pd0[0:64, 128:256].rearrange("p (c2 pr t) -> p c2 pr t", pr=2, t=8)[:, :, par, 0]
                            v.tensor_copy(atnb[64 * par:64 * par + 64, 0:8], src2)
                        v.wait_ge(v_dga, 1)
                        v.tensor_tensor(atnb[:], atnb[:], modf[:, 16:24], OP.mult)
                        v.wait_ge(v_xbf, 1)
                        v.tensor_tensor(x2bf[:], xbf[:], atnb[:], OP.add)
                        v.tensor_tensor(x2sq[:], x2bf[:], x2bf[:], OP.mult)
                        v.reduce_sum(red_b[:, 0:1], x2bf[:], axis=AX.X)
                        v.reduce_sum(red_b[:, 1:2], x2sq[:], axis=AX.X)
                        v.fin(v_ext, 1)
                        v.wait_ge(p_bst, 1)
                        v.tensor_scalar(lb_c[0:1, 2:3], pa[0:1, 992:993], consts[0:1, C_RD:C_RD + 1],
                                        None, OP.mult)
                        v.tensor_tensor(lb_c[0:1, 3:4], lb_c[0:1, 2:3], lb_c[0:1, 2:3], OP.mult)
                        v.tensor_scalar(lb_c[0:1, 3:4], lb_c[0:1, 3:4], consts[0:1, C_NEG1:C_NEG1 + 1],
                                        consts[0:1, C_EPS:C_EPS + 1], OP.mult, OP.add)
                        v.tensor_scalar(lb_c[0:1, 4:5], pa[0:1, 993:994],
                                        consts[0:1, C_RD:C_RD + 1], None, OP.mult)
                        v.tensor_tensor(lb_c[0:1, 4:5], lb_c[0:1, 4:5], lb_c[0:1, 3:4], OP.add)
                        _emit_rsqrt(v, consts, lb_c[0:1, 1:2], lb_c[0:1, 4:5], 0.0, clsc)
                        v.fin(v_ext, 1)
                        v.tensor_scalar(lb_c[0:1, 4:5], lb_c[0:1, 2:3], lb_c[0:1, 1:2],
                                        consts[0:1, C_NEG1:C_NEG1 + 1], OP.mult, OP.mult)
                        v.fin(v_ext, 1)
                        v.wait_ge(p_bbc, 1)
                        v.tensor_copy(rstdb[:, 0:2], pa[:, 994:996])
                        v.tensor_scalar(x2bf[:], x2bf[:], rstdb[:, 0:1], rstdb[:, 1:2], OP.mult, OP.add)
                        v.tensor_tensor(x2bf[:], x2bf[:], modf[:, 32:40], OP.mult)
                        v.tensor_tensor(x2bf[:], x2bf[:], modf[:, 24:32], OP.add)
                        v.tensor_scalar(h2bf[:], x2bf[:], flagc[:, 0:1], None, OP.mult)
                        v.tensor_copy(h2f[:, :].rearrange("p (c w) -> p c w", w=132)[:, :, 0:1],
                                      h2bf[:, :].unsqueeze(2))
                        v.fin(v_h2b, 1)
                v.wait_ge(p_atm, i + 1)
                v.tensor_tensor(x2b_[i % 2][:], xin[i % 3][:], pa[:, :], OP.add).then_inc(v_x2, 1)
                v.bn_stats(ln_s[:, 12:18], x2b_[i % 2][:, 0:512])
                v.bn_stats(ln_s[:, 18:24], x2b_[i % 2][:, 512:1024])
                v.bn_aggr(ln_s[:, 26:28], ln_s[:, 12:24]).then_inc(v_ln2, 1)
                _emit_rsqrt(v, consts, ln_c[:, 1:2], ln_s[:, 27:28],
                            consts[:, C_EPS:C_EPS + 1], clsc)
                v.tensor_scalar(ln_c[:, 2:3], ln_s[:, 26:27], ln_c[:, 1:2],
                                consts[:, C_NEG1:C_NEG1 + 1], OP.mult, OP.mult)
                v.tensor_scalar(u2[:], x2b_[i % 2][:], ln_c[:, 1:2], ln_c[:, 2:3],
                                OP.mult, OP.add).then_inc(v_u2, 1)
                dt3 = dt_[:, :].rearrange("p (c t) -> p c t", t=128)
                h2s = h2f[:, :].rearrange("p (c w) -> p c w", w=132)
                v.wait_ge(s_h2h, i + 1)
                if i == 0:
                    v.wait_ge(v_h2b, 1)
                v.tensor_tensor(dt3[:, 0:4, :], h2s[:, 0:4, 1:129], h2s[:, 0:4, 0:128], OP.subtract)
                for c in range(4):
                    v.scalar_tensor_tensor(xr[:, c * 128:(c + 1) * 128], dt_[:, c * 128:(c + 1) * 128],
                                           murf[:, c:c + 1], h2f[:, c * 132:c * 132 + 128],
                                           OP.mult, OP.add)
                v.fin(v_xrh, 1)
                v.wait_ge(s_h2, i + 1)
                v.tensor_tensor(dt3[:, 4:8, :], h2s[:, 4:8, 1:129], h2s[:, 4:8, 0:128], OP.subtract)
                for c in range(4, NCH):
                    v.scalar_tensor_tensor(xr[:, c * 128:(c + 1) * 128], dt_[:, c * 128:(c + 1) * 128],
                                           murf[:, c:c + 1], h2f[:, c * 132:c * 132 + 128],
                                           OP.mult, OP.add)
                for c in range(NCH):
                    v.scalar_tensor_tensor(xk[:, c * 128:(c + 1) * 128], dt_[:, c * 128:(c + 1) * 128],
                                           mukf[:, c:c + 1], h2f[:, c * 132:c * 132 + 128],
                                           OP.mult, OP.add)
                if i < NT - 1:
                    v.tensor_copy(h2f[:, :].rearrange("p (c w) -> p c w", w=132)[:, :, 0:1],
                                  h2f[:, :].rearrange("p (c w) -> p c w", w=132)[:, :, 128:129])
                v.fin(v_xr, 1)
                if i + 1 < NT:
                    emit_evac(v, i + 1, "q")
                    emit_evac(v, i + 1, "k")
                v.wait_ge(p_vf, i + 1)
                v.scalar_tensor_tensor(ffnf[:], rt[:], 1.0, pa[:, :], OP.add, OP.mult).then_inc(v_ffn, 1)
                if i >= 2:
                    v.wait_ge(st_y, 16 * (i - 1))
                v.tensor_tensor(yb_[i % 2][:], ffnf[:], gbT[:, :], OP.mult)
                v.tensor_tensor(yb_[i % 2][:], yb_[i % 2][:], x2b_[i % 2][:], OP.add).then_inc(v_y, 1)

        @block.tensor
        def _(t):
            t = _Trk(t)
            t.wait_ge(v_silu, 1)
            for c in range(NCH):
                t.matmul(pd0[:, 480 + c:481 + c], silcb[0:1, c * 128:(c + 1) * 128],
                         ones1b[0:1, 0:1], start=True, stop=True)
            t.fin(p_sT, 1)
            for c in range(NCH):
                t.matmul(pa[:, 1000 + c:1001 + c], xbnd[0:1, c * 128:(c + 1) * 128],
                         ones1[0:1, 0:1], start=True, stop=True)
            t.wait_ge(v_u1b, 1)
            for c in range(NCH):
                t.matmul(pa[:, 1008 + c:1009 + c], u1b[0:1, c * 128:(c + 1) * 128],
                         ones1b[0:1, 0:1], start=True, stop=True)
            t.fin(p_xbf, 1)
            t.wait_ge(v_sT, 1)
            for mv in range(6):
                if mv > 0:
                    t.wait_ge(v_mod, mv)
                for jh in range(2):
                    t.wait_ge(ld_w, LW[MVNAME[mv] + str(jh) + "1"])
                    if jh == 1:
                        t.wait_ge(v_mve, 2 * mv + 1)
                    for ic in range(8):
                        t.matmul(pd1[0:1, 0:512], sT[:, ic:ic + 1],
                                 wmb[ic // 4][:, (ic % 4) * 512:(ic % 4) * 512 + 512],
                                 start=(ic == 0), stop=(ic == 7))
                    t.fin(p_mvh, 1)
                t.wait_ge(v_mve, 2 * mv + 2)
                for c in range(NCH):
                    t.matmul(pd1[:, 384 + c:385 + c], modrow[0:1, c * 128:(c + 1) * 128],
                             ones1[0:1, 0:1], start=True, stop=True)
                t.fin(p_mtr, 1)
            t.wait_ge(v_dga, 2)
            t.matmul(pd1[0:8, 384:512], gbc[:, :], idn[:, :], start=True, stop=True)
            t.fin(p_gb1, 1)
            t.wait_ge(g_gbr, 16 * 8)
            t.matmul(pd0[:, 0:512], ones1[0:1, :], gbrow[0:1, 0:512], start=True, stop=True)
            t.matmul(pd1[:, 0:512], ones1[0:1, :], gbrow[0:1, 512:1024], start=True, stop=True)
            t.fin(p_gbt, 1)
            t.wait_ge(v_gbt, 1)
            def emit_u1t(t, i):
                t.wait_ge(v_u1, i + 1)
                if i == 0:
                    t.wait_ge(v_h1b, 1)
                    t.wait_ge(v_xbf, 1)
                    for c in range(NCH):
                        t.matmul(pa[:, c * 128:(c + 1) * 128], u1[:, c * 128:(c + 1) * 128],
                                 idn[:, :], start=True, stop=True)
                else:
                    t.wait_ge(v_rin, scb(i))
                    t.wait_ge(s_non, scb(i))
                    t.wait_ge(v_vbe, scb(i))
                    for c in range(NCH):
                        t.matmul(pb[:, c * 128:(c + 1) * 128], u1[:, c * 128:(c + 1) * 128],
                                 idn[:, :], start=True, stop=True)
                t.fin(p_u1t, 1)

            def emit_q(t, i):
                t.wait_ge(s_h1, i + 1)
                t.wait_ge(ld_w, LW["wq"])
                for mc in range(NCH):
                    for kc in range(NCH):
                        t.matmul(pb[:, mc * 128:(mc + 1) * 128],
                                 W["wq"][:, kc * D + mc * 128: kc * D + (mc + 1) * 128],
                                 h1f[:, kc * 128:(kc + 1) * 128],
                                 start=(kc == 0), stop=(kc == 7))
                        if i == 0:
                            t.matmul(pd0[:, 0 + mc:1 + mc],
                                     W["wq"][:, kc * D + mc * 128: kc * D + (mc + 1) * 128],
                                     h1bf[:, kc:kc + 1], start=(kc == 0), stop=(kc == 7))
                t.fin(p_q, 1)

            def emit_k(t, i):
                t.wait_ge(ld_w, LW["wk"])
                if i > 0:
                    t.wait_ge(s_kr, i)
                for mc in range(NCH):
                    for kc in range(NCH):
                        t.matmul(pc[:, mc * 128:(mc + 1) * 128],
                                 W["wk"][:, kc * D + mc * 128: kc * D + (mc + 1) * 128],
                                 h1f[:, kc * 128:(kc + 1) * 128],
                                 start=(kc == 0), stop=(kc == 7))
                        if i == 0:
                            t.matmul(pd0[:, 8 + mc:9 + mc],
                                     W["wk"][:, kc * D + mc * 128: kc * D + (mc + 1) * 128],
                                     h1bf[:, kc:kc + 1], start=(kc == 0), stop=(kc == 7))
                t.fin(p_k, 1)

            def emit_vp(t, i):
                t.wait_ge(ld_w, LW["wv"])
                t.wait_ge(v_qe, i + 1)
                for mc in range(NCH):
                    for kc in range(NCH):
                        t.matmul(pb[:, mc * 128:(mc + 1) * 128],
                                 W["wv"][:, kc * D + mc * 128: kc * D + (mc + 1) * 128],
                                 h1f[:, kc * 128:(kc + 1) * 128],
                                 start=(kc == 0), stop=(kc == 7))
                        if i == 0:
                            t.matmul(pd0[:, 16 + mc:17 + mc],
                                     W["wv"][:, kc * D + mc * 128: kc * D + (mc + 1) * 128],
                                     h1bf[:, kc:kc + 1], start=(kc == 0), stop=(kc == 7))
                t.fin(p_v, 1)
                if i == 0:
                    t.fin(p_bqkv, 1)

            emit_u1t(t, 0)
            emit_q(t, 0)
            emit_k(t, 0)
            emit_vp(t, 0)
            for i in range(NT):
                t.wait_ge(v_ve, i + 1)
                t.wait_ge(v_ke, i + 1)
                if i == 0:
                    t.wait_ge(v_bev, 1)
                if i == 6:
                    t.wait_ge(v_mod, 6)
                for b in range(nbatch(i)):
                    gl = batch_groups(i, b)
                    pdx = score_pd(i, b)
                    sb_ = scb(i) + b
                    is_bnd = (len(gl) == 1)
                    pb_ = (b - 1 if i == 0 else b)
                    if b == 0 and i >= 5:
                        t.wait_ge(s_h2, i)
                    if b == 0 and i >= 4:
                        t.wait_ge(s_h1, i + 1)
                    if pdx is pd0:
                        if i == 0 and b == 1:
                            t.wait_ge(v_ext, 1)
                            t.wait_ge(v_rin, 1)
                            t.wait_ge(s_non, 1)
                        if sb_ > 0:
                            t.wait_ge(s_exp, sb_ if i < 6 else sb_ - 1)
                    else:
                        if sb_ >= 2:
                            t.wait_ge(s_exp, sb_ - 1)
                    for kk, gidx in enumerate(gl):
                        t.matmul(pdx[:, kk * 128:(kk + 1) * 128], kg[:, gidx * 128:(gidx + 1) * 128],
                                 qg[:, gidx * 128:(gidx + 1) * 128], start=True, stop=True)
                    t.fin(p_sc, 1)
                    if sb_ >= 2:
                        t.wait_ge(v_vbe, sb_ - 1)
                    if b == 0 and i > 0:
                        t.wait_ge(v_ffn, i)
                    if i == 0 and not is_bnd:
                        t.wait_ge(v_h2b, 1)
                    for kk, gidx in enumerate(gl):
                        if is_bnd:
                            vt_out = pa[:, 256:320]
                        else:
                            vt_out = pa[:, (pb_ % 2) * 512 + 64 * kk:(pb_ % 2) * 512 + 64 * kk + 64]
                        t.matmul(vt_out, vg[0:64, gidx * 128:(gidx + 1) * 128],
                                 idn[0:64, 0:64], start=True, stop=True)
                    t.fin(p_vt, 1)
                    t.wait_ge(s_exp, sb_ + 1)
                    t.wait_ge(v_vbe, sb_ + 1)
                    if sb_ >= 2:
                        t.wait_ge(s_non, sb_ - 1)
                        t.wait_ge(v_rin, sb_ - 1)
                    for kk, gidx in enumerate(gl):
                        if is_bnd:
                            oa_out = pd0[:, 256:321]
                        else:
                            cb_ = (pb_ % 2) * 512 + 65 * kk
                            oa_out = pb[:, cb_:cb_ + 65]
                        t.matmul(oa_out, a2u[:, gidx * 128:(gidx + 1) * 128],
                                 vbs[:, gidx * 72:gidx * 72 + 65], start=True, stop=True)
                    t.fin(p_av, 1)
                    t.wait_ge(s_non, sb_ + 1)
                    if is_bnd:
                        t.matmul(pd0[0:64, 128:256], on_[:, 0:64], idn[:, :], start=True, stop=True)
                    else:
                        if pb_ % 2 == 0:
                            if pb_ == 0:
                                t.wait_ge(v_ke, i + 1)
                                if i > 0:
                                    t.wait_ge(s_kr, i)
                                    t.wait_ge(v_asm, 2 * i)
                            else:
                                t.wait_ge(v_asm, 2 * i + 1)
                        for gidx in gl:
                            s_ = (gidx - 1) % 8
                            t.matmul(pc[0:64, s_ * 128:(s_ + 1) * 128],
                                     on_[:, gidx * 64:(gidx + 1) * 64], idn[:, :],
                                     start=True, stop=True)
                    t.fin(p_ont, 1)
                    if i == 0 and b == 0:
                        t.fin(p_bont, 1)
                        t.wait_ge(v_ext, 1)
                        t.matmul(pa[0:1, 992:993], red_b[:, 0:1], onescol[:, 0:1], start=True, stop=True)
                        t.matmul(pa[0:1, 993:994], red_b[:, 1:2], onescol[:, 0:1], start=True, stop=True)
                        t.fin(p_bst, 1)
                        t.wait_ge(v_ext, 3)
                        t.matmul(pa[:, 994:995], ones1[0:1, :], lb_c[0:1, 1:2], start=True, stop=True)
                        t.matmul(pa[:, 995:996], ones1[0:1, :], lb_c[0:1, 4:5], start=True, stop=True)
                        t.fin(p_bbc, 1)
                t.wait_ge(v_asm, 2 * (i + 1))
                t.wait_ge(v_dga, 1)
                t.wait_ge(s_h1, i + 1)
                if i == 0:
                    t.wait_ge(v_h2b, 1)
                for c in range(NCH):
                    t.matmul(pa[:, c * 128:(c + 1) * 128], fma[:, c * 128:(c + 1) * 128],
                             dga[:, c * 128:(c + 1) * 128], start=True, stop=True)
                t.fin(p_atm, 1)
                if i + 1 < NT:
                    emit_u1t(t, i + 1)
                t.wait_ge(v_u2, i + 1)
                if i >= 4:
                    t.wait_ge(s_h1, i + 1)
                    if i == 4:
                        t.wait_ge(v_mod, 6)
                        t.wait_ge(s_exp, scb(4))
                    else:
                        t.wait_ge(s_h2, i)
                    for c in range(NCH):
                        dstp = pd0 if c < 4 else pd1
                        t.matmul(dstp[:, (c % 4) * 128:(c % 4 + 1) * 128],
                                 u2[:, c * 128:(c + 1) * 128], idn[:, :], start=True, stop=True)
                else:
                    t.wait_ge(v_x2, i + 1)
                    for c in range(NCH):
                        t.matmul(pa[:, c * 128:(c + 1) * 128], u2[:, c * 128:(c + 1) * 128],
                                 idn[:, :], start=True, stop=True)
                t.fin(p_u2t, 1)
                t.wait_ge(v_xrh, i + 1)
                t.wait_ge(ld_w, LW["wr"])
                for hh2 in range(2):
                    for kc in range(NCH):
                        if hh2 == 0 and kc == 4:
                            t.wait_ge(v_xr, i + 1)
                        t.matmul(pa[:, hh2 * 512:hh2 * 512 + 512],
                                 xr[:, kc * 128:(kc + 1) * 128],
                                 W["wr"][:, kc * D + hh2 * 512: kc * D + hh2 * 512 + 512],
                                 start=(kc == 0), stop=(kc == 7))
                t.fin(p_r, 1)
                t.wait_ge(ld_w, LW["wkf"])
                t.wait_ge(v_asm, 2 * (i + 1))
                for mc in range(NCH):
                    for kc in range(NCH):
                        t.matmul(pc[:, mc * 128:(mc + 1) * 128],
                                 W["wkf"][:, kc * D + mc * 128: kc * D + (mc + 1) * 128],
                                 xk[:, kc * 128:(kc + 1) * 128],
                                 start=(kc == 0), stop=(kc == 7))
                t.fin(p_kf, 1)
                if i + 1 < NT:
                    emit_q(t, i + 1)
                    emit_k(t, i + 1)
                t.wait_ge(ld_w, LW["wvf"])
                t.wait_ge(s_kr, i + 1)
                t.wait_ge(s_rt, i + 1)
                for hh2 in range(2):
                    for kc in range(NCH):
                        t.matmul(pa[:, hh2 * 512:hh2 * 512 + 512],
                                 ksq[:, kc * 128:(kc + 1) * 128],
                                 W["wvf"][:, kc * D + hh2 * 512: kc * D + hh2 * 512 + 512],
                                 start=(kc == 0), stop=(kc == 7))
                t.fin(p_vf, 1)
                if i + 1 < NT:
                    emit_vp(t, i + 1)

        if DEBUG_DUMP:
            dbg_sem = sem("dbg_sem")
            name2sb = dict(vg=vg[0:64, :],
                           h2fv=h2f[:, :].rearrange("p (c w) -> p c w", w=132)[:, :, 0:129],
                           xin0=xin[0], xin1=xin[1], xin2=xin[2], u1=u1, u2=u2,
                           h1f=h1f, dt=dt_, xr=xr, xk=xk, rt=rt,
                           krel=krel, ksq=ksq, ffnf=ffnf, qg=qg, kg=kg,
                           a2u=a2u, vbs=vbs, on_=on_, fma=fma, rinv=rinv,
                           modf=modf, dga=dga, gbT=gbT, sT=sT, h1bf=h1bf,
                           x2b0=x2b_[0], x2b1=x2b_[1], yb0=yb_[0], yb1=yb_[1],
                           rstdb=rstdb, h2bf=h2bf, xbf=xbf, atnb=atnb,
                           x2bf=x2bf, lb_c=lb_c, ln_c=ln_c, ln_s=ln_s)
            dbg_outs = {}
            def _ap(x):
                return x if isinstance(x, bass.AP) else x[:, :]
            for nm in DEBUG_DUMP:
                ap_t = _ap(name2sb[nm])
                dbg_outs[nm] = nc.dram_tensor("dbg_" + nm, list(ap_t.shape), ap_t.dtype,
                                              kind="ExternalOutput").ap()

            @block.sync
            def _(s):
                s.wait_ge(st_y, 16 * NT)
                nd = 0
                for nm in DEBUG_DUMP:
                    s.dma_start(out=dbg_outs[nm], in_=_ap(name2sb[nm])).then_inc(dbg_sem, 16)
                    nd += 1
                s.wait_ge(dbg_sem, 16 * nd)

    return nc


def _host_prep(inputs):
    bf = ml_dtypes.bfloat16
    x = np.ascontiguousarray(np.asarray(inputs["L0_attn_x"], dtype=np.float32))
    B, S, _ = x.shape
    perm = np.concatenate([np.r_[mc * 64:(mc + 1) * 64, (mc + 8) * 64:(mc + 9) * 64]
                           for mc in range(8)])
    wq = np.ascontiguousarray(np.asarray(inputs["L0_attn_Wq"], np.float32)[:, perm])
    wk = np.ascontiguousarray(np.asarray(inputs["L0_attn_Wk"], np.float32)[:, perm])
    wv = np.ascontiguousarray(np.asarray(inputs["L0_attn_Wv"], np.float32)[:, perm])
    bq = np.asarray(inputs["L0_attn_bq"], np.float32)[perm]
    bk = np.asarray(inputs["L0_attn_bk"], np.float32)[perm]
    bv = np.asarray(inputs["L0_attn_bv"], np.float32)[perm]
    wr = np.ascontiguousarray(np.asarray(inputs["L0_ffn_W_r"], np.float32).T)
    wkf = np.ascontiguousarray(np.asarray(inputs["L0_ffn_W_k"], np.float32).T)
    wvf = np.ascontiguousarray(np.asarray(inputs["L0_ffn_W_v"], np.float32).T)
    wmod = np.ascontiguousarray(np.asarray(inputs["L0_wmod"], np.float32))
    cc = np.asarray(inputs["L0_c"], np.float32)
    mur = np.ascontiguousarray(np.asarray(inputs["L0_ffn_mu_r"], np.float32).reshape(8, 128).T)
    muk = np.ascontiguousarray(np.asarray(inputs["L0_ffn_mu_k"], np.float32).reshape(8, 128).T)

    def bias_pat(b):
        p = np.zeros((64, 16), np.float32)
        for mc in range(8):
            p[:, mc] = b[mc * 128: mc * 128 + 64]
            p[:, 8 + mc] = b[mc * 128 + 64: (mc + 1) * 128]
        return p

    consts = np.zeros((128, 16), np.float32)
    for col, val in [(C_EPS, 1e-5), (C_NHALF, -0.5), (C_QSC, 0.125), (C_NSH, -20.0),
                     (C_HALF, 0.5), (C_NEG1, -1.0), (C_RD, 1.0 / 1024.0), (C_ONE, 1.0),
                     (C_QTR, 0.25), (C_NQTR, -0.25), (C_15, 1.5)]:
        consts[:, col] = val
    consts[:, C_QK] = np.full(128, 0x5F3759DF, np.int32).view(np.float32)
    consts[:, C_SH1] = np.full(128, 1, np.int32).view(np.float32)
    idn = np.eye(128, dtype=np.float32).astype(bf)
    qs = np.zeros((64, GSLOT * 128), np.float32)
    ks = np.zeros((64, GSLOT * 128), np.float32)
    col_t = np.arange(GSLOT * 128) % 8
    for tt in range(8):
        qs[tt, col_t == tt] = 1.0
        ks[tt, col_t == tt] = 1024.0
    qs[8, :] = 1.0
    ks[8, :] = -1024.0
    qs = qs.astype(bf)
    ks = ks.astype(bf)
    ones1 = np.ones((1, 128), np.float32)
    ones1b = np.ones((1, 128), np.float32).astype(bf)
    onescol = np.ones((128, 1), np.float32)

    bv_nat = np.asarray(inputs["L0_attn_bv"], np.float32)
    bvt = np.ascontiguousarray(np.repeat(bv_nat.reshape(16, 1, 64), 8, axis=1).reshape(128, 64))
    shared = dict(wq=wq, wk=wk, wv=wv, wr=wr, wkf=wkf, wvf=wvf, wmod=wmod,
                  consts=consts, idn=idn, qstat=qs, kstat=ks, bvt=bvt,
                  bqp=bias_pat(bq), bkp=bias_pat(bk), bvp=bias_pat(bv),
                  mur=mur, muk=muk, ones1=ones1, ones1b=ones1b, onescol=onescol)
    in_maps = []
    for ci in range(8):
        bidx, half = ci // 2, ci % 2
        t0 = half * 2048
        m = dict(shared)
        m["x"] = np.ascontiguousarray(x[bidx, t0:t0 + 2048])
        m["xbnd"] = (np.zeros((1, D), np.float32) if half == 0
                     else np.ascontiguousarray(x[bidx, 2047:2048]))
        m["flag"] = np.full((128, 1), float(half), np.float32)
        m["crow"] = np.ascontiguousarray(cc[bidx:bidx + 1])
        in_maps.append(m)
    return in_maps


def kernel(**inputs):
    in_maps = _host_prep(inputs)
    nc = build_nc()
    res = run_bass_kernel_spmd(nc, in_maps, core_ids=list(range(8)))
    x = np.asarray(inputs["L0_attn_x"])
    B, S, _ = x.shape
    out = np.empty((B, S, D), np.float32)
    for ci in range(8):
        bidx, half = ci // 2, ci % 2
        out[bidx, half * 2048:(half + 1) * 2048] = res.results[ci]["y"]
    return out


# version 15
# speedup vs baseline: 1.1062x; 1.0062x over previous
import numpy as np
import ml_dtypes
import contextlib
import concourse.bass as bass
import concourse.mybir as mybir
from concourse.bass_utils import run_bass_kernel_spmd

F32 = mybir.dt.float32
BF16 = mybir.dt.bfloat16
AF = mybir.ActivationFunctionType
OP = mybir.AluOpType
AX = mybir.AxisListType

D = 1024
NT = 16
TPT = 128
NCH = 8
GSLOT = 17

(C_EPS, C_NHALF, C_QSC, C_NSH, C_HALF, C_NEG1, C_RD, C_ONE, C_QTR,
 C_NQTR, C_QK, C_SH1, C_15, C_X0, C_X1, C_X2) = range(16)

MVNAME = ["sa", "ca", "ga", "sb", "cb", "gb"]

DEBUG_DUMP = []


class _Trk:

    _OPS = {"tensor_copy", "tensor_tensor", "tensor_scalar", "memset",
            "scalar_tensor_tensor", "reciprocal", "reduce_sum", "reduce_max",
            "tensor_reduce", "bn_stats", "bn_aggr", "activation", "copy",
            "add", "mul", "matmul", "transpose"}

    def __init__(self, eng):
        object.__setattr__(self, "_e", eng)
        object.__setattr__(self, "last", None)
        object.__setattr__(self, "_finned", None)
        object.__setattr__(self, "_pend", set())

    @staticmethod
    def _apname(x):
        t = getattr(x, "tensor", None)
        nm = getattr(t, "name", None)
        return nm

    def __getattr__(self, n):
        eng = object.__getattribute__(self, "_e")
        a = getattr(eng, n)
        if not callable(a) or n in ("wait_ge",):
            return a
        trk = self

        def w(*args, **kw):
            if n in _Trk._OPS:
                pend = object.__getattribute__(trk, "_pend")
                names = []
                for x in list(args) + list(kw.values()):
                    nm = _Trk._apname(x)
                    if nm is not None:
                        names.append(nm)
                outs = names[:1]
                if "accum_out" in kw and kw["accum_out"] is not None:
                    nm = _Trk._apname(kw["accum_out"])
                    if nm:
                        outs.append(nm)
                ins = names[1:]
                if any(nm in pend for nm in ins):
                    eng.drain()
                    pend.clear()
                r = a(*args, **kw)
                for nm in outs:
                    pend.add(nm)
            else:
                r = a(*args, **kw)
            if r is not None and hasattr(r, "then_inc"):
                object.__setattr__(trk, "last", r)
            return r
        return w

    def fin(self, sem_, n_=1):
        li = object.__getattribute__(self, "last")
        if li is object.__getattribute__(self, "_finned"):
            object.__getattribute__(self, "_e").drain().then_inc(sem_, n_)
        else:
            li.then_inc(sem_, n_)
            object.__setattr__(self, "_finned", li)



def nbatch(i):
    return 5 if i == 0 else 4


def scb(i):
    return sum(nbatch(j) for j in range(i))


def batch_groups(i, b):
    if i == 0 and b == 0:
        return [0]
    bb = b - 1 if i == 0 else b
    return [1 + 4 * bb + k for k in range(4)]



def _emit_rsqrt(v, consts, out_col, var_ap, eps_bias, scr):
    I32 = mybir.dt.int32
    P = var_ap.shape[0]
    ve = scr[0:P, 0:1]
    t_ = scr[0:P, 1:2]
    v.tensor_scalar(ve, var_ap, eps_bias, None, OP.add)
    v.tensor_tensor(out_col.bitcast(I32), ve.bitcast(I32),
                    consts[0:P, C_SH1:C_SH1 + 1].bitcast(I32), OP.logical_shift_right)
    v.tensor_tensor(out_col.bitcast(I32), consts[0:P, C_QK:C_QK + 1].bitcast(I32),
                    out_col.bitcast(I32), OP.subtract)
    for _ in range(2):
        v.tensor_tensor(t_, out_col, out_col, OP.mult)
        v.tensor_tensor(t_, t_, ve, OP.mult)
        v.tensor_scalar(t_, t_, consts[0:P, C_NHALF:C_NHALF + 1], consts[0:P, C_15:C_15 + 1],
                        OP.mult, OP.add)
        v.tensor_tensor(out_col, out_col, t_, OP.mult)


def build_nc():
    nc = bass.Bass("TRN2", target_bir_lowering=False, detect_race_conditions=False)
    x_h = nc.dram_tensor("x", [NT * TPT, D], F32, kind="ExternalInput").ap()
    xb_h = nc.dram_tensor("xbnd", [1, D], F32, kind="ExternalInput").ap()
    flag_h = nc.dram_tensor("flag", [128, 1], F32, kind="ExternalInput").ap()
    w_h = {}
    for w in ["wq", "wk", "wv", "wr", "wkf", "wvf"]:
        w_h[w] = nc.dram_tensor(w, [D, D], F32, kind="ExternalInput").ap()
    wmod_h = nc.dram_tensor("wmod", [D, 6 * D], F32, kind="ExternalInput").ap()
    crow_h = nc.dram_tensor("crow", [1, D], F32, kind="ExternalInput").ap()
    consts_h = nc.dram_tensor("consts", [128, 16], F32, kind="ExternalInput").ap()
    idn_h = nc.dram_tensor("idn", [128, 128], BF16, kind="ExternalInput").ap()
    qstat_h = nc.dram_tensor("qstat", [64, GSLOT * 128], BF16, kind="ExternalInput").ap()
    kstat_h = nc.dram_tensor("kstat", [64, GSLOT * 128], BF16, kind="ExternalInput").ap()
    bqp_h = nc.dram_tensor("bqp", [64, 16], F32, kind="ExternalInput").ap()
    bkp_h = nc.dram_tensor("bkp", [64, 16], F32, kind="ExternalInput").ap()
    bvp_h = nc.dram_tensor("bvp", [64, 16], F32, kind="ExternalInput").ap()
    bvt_h = nc.dram_tensor("bvt", [128, 64], F32, kind="ExternalInput").ap()
    mur_h = nc.dram_tensor("mur", [128, 8], F32, kind="ExternalInput").ap()
    muk_h = nc.dram_tensor("muk", [128, 8], F32, kind="ExternalInput").ap()
    ones1_h = nc.dram_tensor("ones1", [1, 128], F32, kind="ExternalInput").ap()
    ones1b_h = nc.dram_tensor("ones1b", [1, 128], BF16, kind="ExternalInput").ap()
    onescol_h = nc.dram_tensor("onescol", [128, 1], F32, kind="ExternalInput").ap()
    y_h = nc.dram_tensor("y", [NT * TPT, D], F32, kind="ExternalOutput").ap()

    es = contextlib.ExitStack()
    sbuf = lambda n, s, d: es.enter_context(nc.sbuf_tensor(n, s, d))
    psum = lambda n, s, d: es.enter_context(nc.psum_tensor(n, s, d))
    sem = lambda n: es.enter_context(nc.semaphore(n))
    with es:
        W = {w: sbuf(w + "s", [128, 8 * D], BF16) for w in ["wq", "wk", "wv", "wr", "wkf", "wvf"]}
        wmb = [sbuf(f"wmb{i}", [128, 4 * 512], BF16) for i in range(2)]
        xin = [sbuf(f"xin{i}", [128, D], F32) for i in range(3)]
        x2b_ = [sbuf(f"x2b{i}", [128, D], F32) for i in range(2)]
        yb_ = [sbuf(f"yb{i}", [128, D], F32) for i in range(2)]
        u1 = sbuf("u1", [128, D], BF16)
        u2 = sbuf("u2", [128, D], BF16)
        h1f = sbuf("h1f", [128, D], BF16)
        h2f = sbuf("h2f", [128, 8 * 132], BF16)
        dt_ = sbuf("dt", [128, D], BF16)
        xr = sbuf("xr", [128, D], BF16)
        xk = sbuf("xk", [128, D], BF16)
        rt = sbuf("rt", [128, D], BF16)
        krel = sbuf("krel", [128, D], BF16)
        ksq = sbuf("ksq", [128, D], BF16)
        ffnf = sbuf("ffnf", [128, D], BF16)
        qg = sbuf("qg", [128, GSLOT * 128], BF16)
        kg = sbuf("kg", [128, GSLOT * 128], BF16)
        vg = sbuf("vg", [128, GSLOT * 128], BF16)
        a2u = sbuf("a2u", [128, GSLOT * 128], BF16)
        vbs = sbuf("vbs", [128, GSLOT * 72], BF16)
        on_ = sbuf("on", [128, GSLOT * 64], BF16)
        fma = sbuf("fma", [128, D], BF16)
        rinv = sbuf("rinv", [128, GSLOT], F32)
        idn = sbuf("idn_s", [128, 128], BF16)
        consts = sbuf("consts_s", [128, 16], F32)
        bqp = sbuf("bqp_s", [64, 16], F32)
        bkp = sbuf("bkp_s", [64, 16], F32)
        bvp = sbuf("bvp_s", [64, 16], F32)
        bvt = sbuf("bvt_s", [128, 64], F32)
        murf = sbuf("murf", [128, 8], F32)
        mukf = sbuf("mukf", [128, 8], F32)
        modf = sbuf("modf", [128, 48], F32)
        clsc = sbuf("clsc", [128, 24], F32)
        dga = sbuf("dga", [128, 8 * 128], BF16)
        gbT = sbuf("gbT", [128, D], BF16)
        gbrow = sbuf("gbrow", [1, D], F32)
        gbc = sbuf("gbc", [128, 8], BF16)
        gbT8 = sbuf("gbT8", [8, 128], F32)
        ln_s = sbuf("ln_s", [128, 28], F32)
        ln_c = sbuf("ln_c", [128, 4], F32)
        crow = sbuf("crow_s", [1, D], F32)
        silcb = sbuf("silcb", [1, D], BF16)
        sT = sbuf("sT", [128, 8], BF16)
        modrow = sbuf("modrow", [1, D], F32)
        xbnd = sbuf("xbnd_s", [1, D], F32)
        u1b = sbuf("u1b", [1, D], BF16)
        lb_s = sbuf("lb_s", [1, 14], F32)
        lb_c = sbuf("lb_c", [1, 6], F32)
        h1bf = sbuf("h1bf", [128, 8], BF16)
        xbf = sbuf("xbf", [128, 8], F32)
        atnb = sbuf("atnb", [128, 8], F32)
        x2bf = sbuf("x2bf", [128, 8], F32)
        x2sq = sbuf("x2sq", [128, 8], F32)
        red_b = sbuf("red_b", [128, 2], F32)
        rstdb = sbuf("rstdb", [128, 2], F32)
        h2bf = sbuf("h2bf", [128, 8], BF16)
        flagc = sbuf("flagc", [128, 1], F32)
        ones1 = sbuf("ones1_s", [1, 128], F32)
        ones1b = sbuf("ones1b_s", [1, 128], BF16)
        onescol = sbuf("onescol_s", [128, 1], F32)

        pa = psum("pa", [128, 1024], F32)
        pb = psum("pb", [128, 1024], F32)
        pc = psum("pc", [128, 1024], F32)
        pd0 = psum("pd0", [128, 512], F32)
        pd1 = psum("pd1", [128, 512], F32)

        ld_s, ld_x, ld_w, st_y = sem("ld_s"), sem("ld_x"), sem("ld_w"), sem("st_y")
        p_sT, p_xbf, p_mvh, p_mtr = sem("p_sT"), sem("p_xbf"), sem("p_mvh"), sem("p_mtr")
        p_q, p_k, p_v, p_bqkv = sem("p_q"), sem("p_k"), sem("p_v"), sem("p_bqkv")
        p_sc, p_vt, p_av, p_ont = sem("p_sc"), sem("p_vt"), sem("p_av"), sem("p_ont")
        p_bont, p_bst, p_bbc = sem("p_bont"), sem("p_bst"), sem("p_bbc")
        p_atm, p_u2t, p_u1t = sem("p_atm"), sem("p_u2t"), sem("p_u1t")
        p_r, p_kf, p_vf = sem("p_r"), sem("p_kf"), sem("p_vf")
        g_gbr, p_gbt, v_gbt = sem("g_gbr"), sem("p_gbt"), sem("v_gbt")
        p_gb1, v_gb8 = sem("p_gb1"), sem("v_gb8")
        s_h2h, v_xrh = sem("s_h2h"), sem("v_xrh")
        s_rth, s_krh = sem("s_rth"), sem("s_krh")
        v_lnb, v_u1b, v_silu, v_sT = sem("v_lnb"), sem("v_u1b"), sem("v_silu"), sem("v_sT")
        v_mve, v_mod, v_dga = sem("v_mve"), sem("v_mod"), sem("v_dga")
        v_xbf, v_h1b = sem("v_xbf"), sem("v_h1b")
        v_ln1, v_u1, v_qe, v_ke, v_ve = sem("v_ln1"), sem("v_u1"), sem("v_qe"), sem("v_ke"), sem("v_ve")
        v_vbe, v_rin, v_asm, v_x2 = sem("v_vbe"), sem("v_rin"), sem("v_asm"), sem("v_x2")
        v_ln2, v_u2, v_xr, v_ffn, v_y = sem("v_ln2"), sem("v_u2"), sem("v_xr"), sem("v_ffn"), sem("v_y")
        v_bev, v_ext, v_h2b = sem("v_bev"), sem("v_ext"), sem("v_h2b")
        s_silu, s_lb1, s_lb2, s_ln = sem("s_silu"), sem("s_lb1"), sem("s_lb2"), sem("s_ln")
        s_h1, s_exp, s_non, s_h2 = sem("s_h1"), sem("s_exp"), sem("s_non"), sem("s_h2")
        s_rt, s_kr, s_asm = sem("s_rt"), sem("s_kr"), sem("s_asm")

        block = es.enter_context(nc.Block())

        def score_pd(i, b):
            if i < 6 or (i == 0 and b == 0):
                return pd0
            return pd0 if b % 2 == 0 else pd1

        def mv_loads(nm):
            return [nm + jh + h for jh in "01" for h in "01"]
        dma_order = (mv_loads("sa") + mv_loads("ca") + ["wq"] + mv_loads("ga")
                     + ["wk", "wv"] + mv_loads("sb") + mv_loads("cb")
                     + ["wr", "wkf", "wvf"] + mv_loads("gb"))
        LW = {nm: 16 * (k + 1) for k, nm in enumerate(dma_order)}

        @block.scalar
        def _(s):
            for dst, src in [(consts[:], consts_h), (flagc[:], flag_h),
                             (crow[:], crow_h), (xbnd[:], xb_h),
                             (bqp[:], bqp_h), (bkp[:], bkp_h), (bvp[:], bvp_h), (bvt[:], bvt_h),
                             (murf[:], mur_h), (mukf[:], muk_h),
                             (idn[:], idn_h), (ones1[:], ones1_h),
                             (ones1b[:], ones1b_h), (onescol[:], onescol_h),
                             (qg[64:128, :], qstat_h), (kg[64:128, :], kstat_h)]:
                s.dma_start(out=dst, in_=src).then_inc(ld_s, 16)
        NLD_S = 16

        @block.sync
        def _(s):
            for i in range(min(3, NT)):
                s.dma_start(out=xin[i % 3][:], in_=x_h[i * TPT:(i + 1) * TPT, :]).then_inc(ld_x, 16)
            for i in range(NT):
                s.wait_ge(v_y, i + 1)
                s.dma_start(out=y_h[i * TPT:(i + 1) * TPT, :], in_=yb_[i % 2][:]).then_inc(st_y, 16)
                j = i + 3
                if j < NT:
                    s.dma_start(out=xin[j % 3][:], in_=x_h[j * TPT:(j + 1) * TPT, :]).then_inc(ld_x, 16)
            s.wait_ge(st_y, 16 * NT)

        @block.gpsimd
        def _(g):
            for nm in dma_order:
                if nm in W:
                    g.dma_start(out=W[nm][:, :].rearrange("p (kc d) -> p kc d", d=D),
                                in_=w_h[nm].rearrange("(kc p) d -> p kc d", p=128)).then_inc(ld_w, 16)
                else:
                    v = MVNAME.index(nm[:2])
                    jh, h = int(nm[2]), int(nm[3])
                    ell = 4 * v + 2 * jh + h
                    if ell >= 2:
                        g.wait_ge(p_mvh, 2 * v + jh)
                    src = wmod_h[h * 512:(h + 1) * 512, v * D + jh * 512: v * D + jh * 512 + 512]
                    g.dma_start(out=wmb[h][:, :].rearrange("p (ic j) -> p ic j", j=512),
                                in_=src.rearrange("(ic p) j -> p ic j", p=128)).then_inc(ld_w, 16)
            g.wait_ge(v_gb8, 1)
            for c in range(NCH):
                g.dma_start(out=gbrow[0:1, c * 128:(c + 1) * 128],
                            in_=gbT8[c:c + 1, :]).then_inc(g_gbr, 16)

        @block.scalar
        def _(s):
            s = _Trk(s)
            s.wait_ge(ld_s, 16 * NLD_S)
            s.activation(modrow[:], crow[:], AF.Silu).then_inc(s_silu, 1)

            def emit_h1(s, i):
                s.wait_ge(p_u1t, i + 1)
                s.wait_ge(v_mod, 2)
                for c in range(NCH):
                    srcp = (pa if i == 0 else pb)[:, c * 128:(c + 1) * 128]
                    s.activation(h1f[:, c * 128:(c + 1) * 128], srcp,
                                 AF.Identity, scale=modf[:, 8 + c:9 + c], bias=modf[:, c:c + 1])
                s.fin(s_h1, 1)

            emit_h1(s, 0)
            for i in range(NT):
                for b in range(nbatch(i)):
                    gl = batch_groups(i, b)
                    s.wait_ge(p_sc, scb(i) + b + 1)
                    pdx = score_pd(i, b)
                    ncols = 128 * len(gl)
                    s.activation(a2u[:, gl[0] * 128: gl[0] * 128 + ncols], pdx[:, 0:ncols],
                                 AF.Exp, scale=consts[:, C_QSC:C_QSC + 1],
                                 bias=consts[:, C_NSH:C_NSH + 1]).then_inc(s_exp, 1)
                    s.wait_ge(p_av, scb(i) + b + 1)
                    s.wait_ge(v_rin, scb(i) + b + 1)
                    for gidx in gl:
                        if gidx == 0:
                            src = pd0[:, 256:320]
                        else:
                            pb_ = (gidx - 1) // 4
                            k = (gidx - 1) % 4
                            cb = (pb_ % 2) * 512 + k * 65
                            src = pb[:, cb:cb + 64]
                        s.activation(on_[:, gidx * 64:(gidx + 1) * 64], src,
                                     AF.Identity, scale=rinv[:, gidx:gidx + 1])
                    s.fin(s_non, 1)
                s.wait_ge(p_u2t, i + 1)
                s.wait_ge(v_mod, 5)
                for c in range(NCH):
                    if i >= 4:
                        srcp = (pd0 if c < 4 else pd1)[:, (c % 4) * 128:(c % 4 + 1) * 128]
                    else:
                        srcp = pa[:, c * 128:(c + 1) * 128]
                    s.activation(h2f[:, c * 132 + 1: c * 132 + 129], srcp,
                                 AF.Identity, scale=modf[:, 32 + c:33 + c], bias=modf[:, 24 + c:25 + c])
                    if c == 3:
                        s.fin(s_h2h, 1)
                s.fin(s_h2, 1)
                if i + 1 < NT:
                    emit_h1(s, i + 1)
                s.wait_ge(p_r, i + 1)
                s.activation(rt[:, 0:512], pa[:, 0:512], AF.Tanh, scale=consts[:, C_HALF:C_HALF + 1])
                s.fin(s_rth, 1)
                s.activation(rt[:, 512:1024], pa[:, 512:1024], AF.Tanh,
                             scale=consts[:, C_HALF:C_HALF + 1]).then_inc(s_rt, 1)
                s.wait_ge(p_kf, i + 1)
                s.activation(krel[:, 0:512], pc[:, 0:512], AF.Relu)
                s.activation(ksq[:, 0:512], krel[:, 0:512], AF.Square)
                s.fin(s_krh, 1)
                s.activation(krel[:, 512:1024], pc[:, 512:1024], AF.Relu)
                s.activation(ksq[:, 512:1024], krel[:, 512:1024], AF.Square).then_inc(s_kr, 1)

        @block.vector
        def _(v):
            v = _Trk(v)
            v.wait_ge(ld_s, 16 * NLD_S)
            v.memset(qg[0:64, 0:128], 0.0)
            v.memset(kg[0:64, 0:128], 0.0)
            v.memset(vg[0:64, 0:128], 0.0)
            v.memset(vbs[:, :], 0.0)
            v.memset(vbs[:, :].rearrange("p (s c) -> p s c", c=72)[:, :, 64:65], 1.0)
            v.memset(h2f[:, :].rearrange("p (s c) -> p s c", c=132)[:, :, 0:1], 0.0)
            v.bn_stats(lb_s[:, 0:6], xbnd[0:1, 0:512])
            v.bn_stats(lb_s[:, 6:12], xbnd[0:1, 512:1024])
            v.bn_aggr(lb_s[:, 12:14], lb_s[:, 0:12]).then_inc(v_lnb, 1)
            _emit_rsqrt(v, consts, lb_c[0:1, 0:1], lb_s[0:1, 13:14],
                        consts[0:1, C_EPS:C_EPS + 1], clsc)
            v.tensor_scalar(lb_c[0:1, 5:6], lb_s[0:1, 12:13], lb_c[0:1, 0:1],
                            consts[0:1, C_NEG1:C_NEG1 + 1], OP.mult, OP.mult)
            v.tensor_scalar(u1b[:], xbnd[:], lb_c[0:1, 0:1], lb_c[0:1, 5:6],
                            OP.mult, OP.add).then_inc(v_u1b, 1)
            v.wait_ge(s_silu, 1)
            v.tensor_copy(silcb[:], modrow[:]).then_inc(v_silu, 1)
            v.wait_ge(p_sT, 1)
            v.tensor_copy(sT[:, :], pd0[:, 480:488]).then_inc(v_sT, 1)
            v.wait_ge(p_xbf, 1)
            v.tensor_copy(xbf[:, :], pa[:, 1000:1008]).then_inc(v_xbf, 1)
            for mv in range(6):
                for jh in range(2):
                    v.wait_ge(p_mvh, 2 * mv + jh + 1)
                    v.tensor_copy(modrow[0:1, jh * 512:(jh + 1) * 512], pd1[0:1, 0:512]).then_inc(v_mve, 1)
                v.wait_ge(p_mtr, mv + 1)
                v.tensor_copy(modf[:, 8 * mv:8 * mv + 8], pd1[:, 384:392])
                if mv in (1, 4):
                    v.tensor_scalar(modf[:, 8 * mv:8 * mv + 8], modf[:, 8 * mv:8 * mv + 8],
                                    consts[:, C_ONE:C_ONE + 1], None, OP.add)
                if mv in (2, 5):
                    cf = 8 * mv
                    gsl = modf[:, cf:cf + 8]
                    v.tensor_scalar(clsc[:, 0:8], gsl, 0.0, None, OP.is_ge)
                    v.tensor_scalar(clsc[:, 8:16], gsl, consts[:, C_QTR:C_QTR + 1], None, OP.max)
                    v.tensor_scalar(clsc[:, 16:24], gsl, consts[:, C_NQTR:C_NQTR + 1], None, OP.min)
                    v.tensor_tensor(clsc[:, 8:16], clsc[:, 8:16], clsc[:, 16:24], OP.subtract)
                    v.tensor_tensor(clsc[:, 8:16], clsc[:, 8:16], clsc[:, 0:8], OP.mult)
                    v.tensor_tensor(gsl, clsc[:, 8:16], clsc[:, 16:24], OP.add)
                    if mv == 5:
                        v.tensor_scalar(gsl, gsl, consts[:, C_HALF:C_HALF + 1], None, OP.mult)
                    if mv == 2:
                        for c in range(NCH):
                            v.tensor_scalar(dga[:, c * 128:(c + 1) * 128], idn[:, :],
                                            modf[:, cf + c:cf + c + 1], None, OP.mult)
                    else:
                        v.tensor_copy(gbc[:, :], gsl)
                    v.fin(v_dga, 1)
                v.fin(v_mod, 1)
                if mv == 1:
                    v.tensor_tensor(x2sq[:, 0:8], pa[:, 1008:1016], modf[:, 8:16], OP.mult)
                    v.tensor_tensor(x2sq[:, 0:8], x2sq[:, 0:8], modf[:, 0:8], OP.add)
                    v.tensor_copy(h1bf[:, :], x2sq[:, 0:8]).then_inc(v_h1b, 1)
            def emit_ln1(v, i):
                v.wait_ge(ld_x, 16 * (i + 1))
                v.wait_ge(p_u1t, i)
                v.bn_stats(ln_s[:, 0:6], xin[i % 3][:, 0:512])
                v.bn_stats(ln_s[:, 6:12], xin[i % 3][:, 512:1024])
                v.bn_aggr(ln_s[:, 24:26], ln_s[:, 0:12]).then_inc(v_ln1, 1)
                _emit_rsqrt(v, consts, ln_c[:, 0:1], ln_s[:, 25:26],
                            consts[:, C_EPS:C_EPS + 1], clsc)
                v.tensor_scalar(ln_c[:, 2:3], ln_s[:, 24:25], ln_c[:, 0:1],
                                consts[:, C_NEG1:C_NEG1 + 1], OP.mult, OP.mult)
                v.tensor_scalar(u1[:], xin[i % 3][:], ln_c[:, 0:1], ln_c[:, 2:3],
                                OP.mult, OP.add).then_inc(v_u1, 1)

            v.wait_ge(p_gb1, 1)
            v.tensor_copy(gbT8[:, :], pd1[0:8, 384:512])
            v.fin(v_gb8, 1)
            v.wait_ge(p_gbt, 1)
            v.tensor_copy(gbT[:, 0:512], pd0[:, 0:512])
            v.tensor_copy(gbT[:, 512:1024], pd1[:, 0:512])
            v.fin(v_gbt, 1)

            def emit_evac(v, i, which):
                psrc, gdst, bpat, vsem, pwait = {
                    "q": (pb, qg, bqp, v_qe, p_q), "k": (pc, kg, bkp, v_ke, p_k),
                    "v": (pb, vg, None, v_ve, p_v)}[which]
                v.wait_ge(pwait, i + 1)
                for par in range(2):
                    in3 = psrc[64 * par:64 * par + 64, :].rearrange(
                        "p (mc g t) -> p mc g t", g=16, t=8)
                    out3 = gdst[0:64, :].rearrange("p (g jj t) -> p g jj t", jj=16, t=8)[
                        :, 1:GSLOT, 8 * par:8 * par + 8, :].rearrange("p g j t -> p j g t")
                    if bpat is None:
                        v.tensor_copy(out3, in3)
                    else:
                        bias = bpat[:, 8 * par:8 * par + 8].unsqueeze(2).unsqueeze(3).broadcast_to([64, 8, 16, 8])
                        v.tensor_tensor(out3, in3, bias, OP.add)
                v.fin(vsem, 1)

            emit_ln1(v, 0)
            emit_evac(v, 0, "q")
            emit_evac(v, 0, "k")
            for i in range(NT):
                emit_evac(v, i, "v")
                if i == 0:
                    v.wait_ge(p_bqkv, 1)
                    for (tix, gdst, bpat) in [(0, qg, bqp), (1, kg, bkp)]:
                        for par in range(2):
                            in2 = pd0[64 * par:64 * par + 64, 8 * tix:8 * tix + 8]
                            out2 = gdst[0:64, :].rearrange("p (g jj t) -> p g jj t", jj=16, t=8)[
                                :, 0, 8 * par:8 * par + 8, 0]
                            v.tensor_tensor(out2, in2, bpat[:, 8 * par:8 * par + 8], OP.add)
                    for par in range(2):
                        in2 = pd0[64 * par:64 * par + 64, 16:24]
                        out2 = vg[0:64, :].rearrange("p (g jj t) -> p g jj t", jj=16, t=8)[
                            :, 0, 8 * par:8 * par + 8, 0]
                        v.tensor_copy(out2, in2)
                    v.fin(v_bev, 1)
                for b in range(nbatch(i)):
                    gl = batch_groups(i, b)
                    v.wait_ge(p_vt, scb(i) + b + 1)
                    if len(gl) == 1:
                        v.tensor_tensor(vbs[:, 0:64], pa[:, 256:320], bvt[:, :], OP.add)
                    else:
                        pb_ = b - 1 if i == 0 else b
                        base = (pb_ % 2) * 512
                        dstv = vbs[:, :].rearrange("p (s c) -> p s c", c=72)[:, gl[0]:gl[0] + 4, 0:64]
                        srcv = pa[:, base:base + 256].rearrange("p (kk c) -> p kk c", c=64)
                        bias4 = bvt[:, :].unsqueeze(1).broadcast_to([128, 4, 64])
                        v.tensor_tensor(dstv, srcv, bias4, OP.add)
                    v.fin(v_vbe, 1)
                    v.wait_ge(p_av, scb(i) + b + 1)
                    if len(gl) == 1:
                        v.reciprocal(rinv[:, 0:1], pd0[:, 320:321])
                    else:
                        pb_ = b - 1 if i == 0 else b
                        base = (pb_ % 2) * 512 + 64
                        src_s = pb[:, base - 64:base - 64 + 260].rearrange("p (s c) -> p s c", c=65)[:, :, 64]
                        v.reciprocal(rinv[:, gl[0]:gl[0] + 4], src_s)
                    v.fin(v_rin, 1)
                    pb_h = b - 1 if i == 0 else b
                    if pb_h in (1, 3) and len(gl) == 4:
                        hh = pb_h // 2
                        bofs = 1 if i == 0 else 0
                        v.wait_ge(p_ont, scb(i) + bofs + 2 * (hh + 1))
                        for par in range(2):
                            srca = pc[0:64, :].rearrange(
                                "p (g c2 pr t) -> p c2 pr g t", t=8, pr=2, c2=8)[:, :, par, :, :]
                            dsta = fma[64 * par:64 * par + 64, :].rearrange(
                                "p (c h2 g t) -> p c h2 g t", h2=2, g=8, t=8)[:, :, hh, :, :]
                            v.tensor_copy(dsta, srca)
                        v.fin(v_asm, 1)
                        if pb_h == 1 and i + 1 < NT:
                            emit_ln1(v, i + 1)
                    if i == 0 and b == 0:
                        v.wait_ge(p_bont, 1)
                        for par in range(2):
                            src2 = pd0[0:64, 128:256].rearrange("p (c2 pr t) -> p c2 pr t", pr=2, t=8)[:, :, par, 0]
                            v.tensor_copy(atnb[64 * par:64 * par + 64, 0:8], src2)
                        v.wait_ge(v_dga, 1)
                        v.tensor_tensor(atnb[:], atnb[:], modf[:, 16:24], OP.mult)
                        v.wait_ge(v_xbf, 1)
                        v.tensor_tensor(x2bf[:], xbf[:], atnb[:], OP.add)
                        v.tensor_tensor(x2sq[:], x2bf[:], x2bf[:], OP.mult)
                        v.reduce_sum(red_b[:, 0:1], x2bf[:], axis=AX.X)
                        v.reduce_sum(red_b[:, 1:2], x2sq[:], axis=AX.X)
                        v.fin(v_ext, 1)
                        v.wait_ge(p_bst, 1)
                        v.tensor_scalar(lb_c[0:1, 2:3], pa[0:1, 992:993], consts[0:1, C_RD:C_RD + 1],
                                        None, OP.mult)
                        v.tensor_tensor(lb_c[0:1, 3:4], lb_c[0:1, 2:3], lb_c[0:1, 2:3], OP.mult)
                        v.tensor_scalar(lb_c[0:1, 3:4], lb_c[0:1, 3:4], consts[0:1, C_NEG1:C_NEG1 + 1],
                                        consts[0:1, C_EPS:C_EPS + 1], OP.mult, OP.add)
                        v.tensor_scalar(lb_c[0:1, 4:5], pa[0:1, 993:994],
                                        consts[0:1, C_RD:C_RD + 1], None, OP.mult)
                        v.tensor_tensor(lb_c[0:1, 4:5], lb_c[0:1, 4:5], lb_c[0:1, 3:4], OP.add)
                        _emit_rsqrt(v, consts, lb_c[0:1, 1:2], lb_c[0:1, 4:5], 0.0, clsc)
                        v.fin(v_ext, 1)
                        v.tensor_scalar(lb_c[0:1, 4:5], lb_c[0:1, 2:3], lb_c[0:1, 1:2],
                                        consts[0:1, C_NEG1:C_NEG1 + 1], OP.mult, OP.mult)
                        v.fin(v_ext, 1)
                        v.wait_ge(p_bbc, 1)
                        v.tensor_copy(rstdb[:, 0:2], pa[:, 994:996])
                        v.tensor_scalar(x2bf[:], x2bf[:], rstdb[:, 0:1], rstdb[:, 1:2], OP.mult, OP.add)
                        v.tensor_tensor(x2bf[:], x2bf[:], modf[:, 32:40], OP.mult)
                        v.tensor_tensor(x2bf[:], x2bf[:], modf[:, 24:32], OP.add)
                        v.tensor_scalar(h2bf[:], x2bf[:], flagc[:, 0:1], None, OP.mult)
                        v.tensor_copy(h2f[:, :].rearrange("p (c w) -> p c w", w=132)[:, :, 0:1],
                                      h2bf[:, :].unsqueeze(2))
                        v.fin(v_h2b, 1)
                v.wait_ge(p_atm, i + 1)
                v.tensor_tensor(x2b_[i % 2][:], xin[i % 3][:], pa[:, :], OP.add).then_inc(v_x2, 1)
                v.bn_stats(ln_s[:, 12:18], x2b_[i % 2][:, 0:512])
                v.bn_stats(ln_s[:, 18:24], x2b_[i % 2][:, 512:1024])
                v.bn_aggr(ln_s[:, 26:28], ln_s[:, 12:24]).then_inc(v_ln2, 1)
                _emit_rsqrt(v, consts, ln_c[:, 1:2], ln_s[:, 27:28],
                            consts[:, C_EPS:C_EPS + 1], clsc)
                v.tensor_scalar(ln_c[:, 2:3], ln_s[:, 26:27], ln_c[:, 1:2],
                                consts[:, C_NEG1:C_NEG1 + 1], OP.mult, OP.mult)
                v.tensor_scalar(u2[:], x2b_[i % 2][:], ln_c[:, 1:2], ln_c[:, 2:3],
                                OP.mult, OP.add).then_inc(v_u2, 1)
                dt3 = dt_[:, :].rearrange("p (c t) -> p c t", t=128)
                h2s = h2f[:, :].rearrange("p (c w) -> p c w", w=132)
                v.wait_ge(s_h2h, i + 1)
                if i == 0:
                    v.wait_ge(v_h2b, 1)
                v.tensor_tensor(dt3[:, 0:4, :], h2s[:, 0:4, 1:129], h2s[:, 0:4, 0:128], OP.subtract)
                for c in range(4):
                    v.scalar_tensor_tensor(xr[:, c * 128:(c + 1) * 128], dt_[:, c * 128:(c + 1) * 128],
                                           murf[:, c:c + 1], h2f[:, c * 132:c * 132 + 128],
                                           OP.mult, OP.add)
                v.fin(v_xrh, 1)
                v.wait_ge(s_h2, i + 1)
                v.tensor_tensor(dt3[:, 4:8, :], h2s[:, 4:8, 1:129], h2s[:, 4:8, 0:128], OP.subtract)
                for c in range(4, NCH):
                    v.scalar_tensor_tensor(xr[:, c * 128:(c + 1) * 128], dt_[:, c * 128:(c + 1) * 128],
                                           murf[:, c:c + 1], h2f[:, c * 132:c * 132 + 128],
                                           OP.mult, OP.add)
                for c in range(NCH):
                    v.scalar_tensor_tensor(xk[:, c * 128:(c + 1) * 128], dt_[:, c * 128:(c + 1) * 128],
                                           mukf[:, c:c + 1], h2f[:, c * 132:c * 132 + 128],
                                           OP.mult, OP.add)
                if i < NT - 1:
                    v.tensor_copy(h2f[:, :].rearrange("p (c w) -> p c w", w=132)[:, :, 0:1],
                                  h2f[:, :].rearrange("p (c w) -> p c w", w=132)[:, :, 128:129])
                v.fin(v_xr, 1)
                if i + 1 < NT:
                    emit_evac(v, i + 1, "q")
                    emit_evac(v, i + 1, "k")
                v.wait_ge(p_vf, i + 1)
                v.scalar_tensor_tensor(ffnf[:], rt[:], 1.0, pa[:, :], OP.add, OP.mult).then_inc(v_ffn, 1)
                if i >= 2:
                    v.wait_ge(st_y, 16 * (i - 1))
                v.tensor_tensor(yb_[i % 2][:], ffnf[:], gbT[:, :], OP.mult)
                v.tensor_tensor(yb_[i % 2][:], yb_[i % 2][:], x2b_[i % 2][:], OP.add).then_inc(v_y, 1)

        @block.tensor
        def _(t):
            t = _Trk(t)
            t.wait_ge(v_silu, 1)
            for c in range(NCH):
                t.matmul(pd0[:, 480 + c:481 + c], silcb[0:1, c * 128:(c + 1) * 128],
                         ones1b[0:1, 0:1], start=True, stop=True)
            t.fin(p_sT, 1)
            for c in range(NCH):
                t.matmul(pa[:, 1000 + c:1001 + c], xbnd[0:1, c * 128:(c + 1) * 128],
                         ones1[0:1, 0:1], start=True, stop=True)
            t.wait_ge(v_u1b, 1)
            for c in range(NCH):
                t.matmul(pa[:, 1008 + c:1009 + c], u1b[0:1, c * 128:(c + 1) * 128],
                         ones1b[0:1, 0:1], start=True, stop=True)
            t.fin(p_xbf, 1)
            t.wait_ge(v_sT, 1)
            for mv in range(6):
                if mv > 0:
                    t.wait_ge(v_mod, mv)
                for jh in range(2):
                    t.wait_ge(ld_w, LW[MVNAME[mv] + str(jh) + "1"])
                    if jh == 1:
                        t.wait_ge(v_mve, 2 * mv + 1)
                    for ic in range(8):
                        t.matmul(pd1[0:1, 0:512], sT[:, ic:ic + 1],
                                 wmb[ic // 4][:, (ic % 4) * 512:(ic % 4) * 512 + 512],
                                 start=(ic == 0), stop=(ic == 7))
                    t.fin(p_mvh, 1)
                t.wait_ge(v_mve, 2 * mv + 2)
                for c in range(NCH):
                    t.matmul(pd1[:, 384 + c:385 + c], modrow[0:1, c * 128:(c + 1) * 128],
                             ones1[0:1, 0:1], start=True, stop=True)
                t.fin(p_mtr, 1)
            t.wait_ge(v_dga, 2)
            t.matmul(pd1[0:8, 384:512], gbc[:, :], idn[:, :], start=True, stop=True)
            t.fin(p_gb1, 1)
            t.wait_ge(g_gbr, 16 * 8)
            t.matmul(pd0[:, 0:512], ones1[0:1, :], gbrow[0:1, 0:512], start=True, stop=True)
            t.matmul(pd1[:, 0:512], ones1[0:1, :], gbrow[0:1, 512:1024], start=True, stop=True)
            t.fin(p_gbt, 1)
            t.wait_ge(v_gbt, 1)
            def emit_u1t(t, i):
                t.wait_ge(v_u1, i + 1)
                if i == 0:
                    t.wait_ge(v_h1b, 1)
                    t.wait_ge(v_xbf, 1)
                    for c in range(NCH):
                        t.matmul(pa[:, c * 128:(c + 1) * 128], u1[:, c * 128:(c + 1) * 128],
                                 idn[:, :], start=True, stop=True)
                else:
                    t.wait_ge(v_rin, scb(i))
                    t.wait_ge(s_non, scb(i))
                    t.wait_ge(v_vbe, scb(i))
                    for c in range(NCH):
                        t.matmul(pb[:, c * 128:(c + 1) * 128], u1[:, c * 128:(c + 1) * 128],
                                 idn[:, :], start=True, stop=True)
                t.fin(p_u1t, 1)

            def emit_q(t, i):
                t.wait_ge(s_h1, i + 1)
                t.wait_ge(ld_w, LW["wq"])
                for mc in range(NCH):
                    for kc in range(NCH):
                        t.matmul(pb[:, mc * 128:(mc + 1) * 128],
                                 W["wq"][:, kc * D + mc * 128: kc * D + (mc + 1) * 128],
                                 h1f[:, kc * 128:(kc + 1) * 128],
                                 start=(kc == 0), stop=(kc == 7))
                        if i == 0:
                            t.matmul(pd0[:, 0 + mc:1 + mc],
                                     W["wq"][:, kc * D + mc * 128: kc * D + (mc + 1) * 128],
                                     h1bf[:, kc:kc + 1], start=(kc == 0), stop=(kc == 7))
                t.fin(p_q, 1)

            def emit_k(t, i):
                t.wait_ge(ld_w, LW["wk"])
                if i > 0:
                    t.wait_ge(s_kr, i)
                for mc in range(NCH):
                    for kc in range(NCH):
                        t.matmul(pc[:, mc * 128:(mc + 1) * 128],
                                 W["wk"][:, kc * D + mc * 128: kc * D + (mc + 1) * 128],
                                 h1f[:, kc * 128:(kc + 1) * 128],
                                 start=(kc == 0), stop=(kc == 7))
                        if i == 0:
                            t.matmul(pd0[:, 8 + mc:9 + mc],
                                     W["wk"][:, kc * D + mc * 128: kc * D + (mc + 1) * 128],
                                     h1bf[:, kc:kc + 1], start=(kc == 0), stop=(kc == 7))
                t.fin(p_k, 1)

            def emit_vp(t, i):
                t.wait_ge(ld_w, LW["wv"])
                t.wait_ge(v_qe, i + 1)
                for mc in range(NCH):
                    for kc in range(NCH):
                        t.matmul(pb[:, mc * 128:(mc + 1) * 128],
                                 W["wv"][:, kc * D + mc * 128: kc * D + (mc + 1) * 128],
                                 h1f[:, kc * 128:(kc + 1) * 128],
                                 start=(kc == 0), stop=(kc == 7))
                        if i == 0:
                            t.matmul(pd0[:, 16 + mc:17 + mc],
                                     W["wv"][:, kc * D + mc * 128: kc * D + (mc + 1) * 128],
                                     h1bf[:, kc:kc + 1], start=(kc == 0), stop=(kc == 7))
                t.fin(p_v, 1)
                if i == 0:
                    t.fin(p_bqkv, 1)

            emit_u1t(t, 0)
            emit_q(t, 0)
            emit_k(t, 0)
            emit_vp(t, 0)
            for i in range(NT):
                t.wait_ge(v_ve, i + 1)
                t.wait_ge(v_ke, i + 1)
                if i == 0:
                    t.wait_ge(v_bev, 1)
                if i == 6:
                    t.wait_ge(v_mod, 6)
                for b in range(nbatch(i)):
                    gl = batch_groups(i, b)
                    pdx = score_pd(i, b)
                    sb_ = scb(i) + b
                    is_bnd = (len(gl) == 1)
                    pb_ = (b - 1 if i == 0 else b)
                    if b == 0 and i >= 5:
                        t.wait_ge(s_h2, i)
                    if b == 0 and i >= 4:
                        t.wait_ge(s_h1, i + 1)
                    if pdx is pd0:
                        if i == 0 and b == 1:
                            t.wait_ge(v_ext, 1)
                            t.wait_ge(v_rin, 1)
                            t.wait_ge(s_non, 1)
                        if sb_ > 0:
                            t.wait_ge(s_exp, sb_ if i < 6 else sb_ - 1)
                    else:
                        if sb_ >= 2:
                            t.wait_ge(s_exp, sb_ - 1)
                    for kk, gidx in enumerate(gl):
                        t.matmul(pdx[:, kk * 128:(kk + 1) * 128], kg[:, gidx * 128:(gidx + 1) * 128],
                                 qg[:, gidx * 128:(gidx + 1) * 128], start=True, stop=True)
                    t.fin(p_sc, 1)
                    if sb_ >= 2:
                        t.wait_ge(v_vbe, sb_ - 1)
                    if b == 0 and i > 0:
                        t.wait_ge(v_ffn, i)
                    if i == 0 and not is_bnd:
                        t.wait_ge(v_h2b, 1)
                    for kk, gidx in enumerate(gl):
                        if is_bnd:
                            vt_out = pa[:, 256:320]
                        else:
                            vt_out = pa[:, (pb_ % 2) * 512 + 64 * kk:(pb_ % 2) * 512 + 64 * kk + 64]
                        t.matmul(vt_out, vg[0:64, gidx * 128:(gidx + 1) * 128],
                                 idn[0:64, 0:64], start=True, stop=True)
                    t.fin(p_vt, 1)
                    t.wait_ge(s_exp, sb_ + 1)
                    t.wait_ge(v_vbe, sb_ + 1)
                    if sb_ >= 2:
                        t.wait_ge(s_non, sb_ - 1)
                        t.wait_ge(v_rin, sb_ - 1)
                    for kk, gidx in enumerate(gl):
                        if is_bnd:
                            oa_out = pd0[:, 256:321]
                        else:
                            cb_ = (pb_ % 2) * 512 + 65 * kk
                            oa_out = pb[:, cb_:cb_ + 65]
                        t.matmul(oa_out, a2u[:, gidx * 128:(gidx + 1) * 128],
                                 vbs[:, gidx * 72:gidx * 72 + 65], start=True, stop=True)
                    t.fin(p_av, 1)
                    t.wait_ge(s_non, sb_ + 1)
                    if is_bnd:
                        t.matmul(pd0[0:64, 128:256], on_[:, 0:64], idn[:, :], start=True, stop=True)
                    else:
                        if pb_ % 2 == 0:
                            if pb_ == 0:
                                t.wait_ge(v_ke, i + 1)
                                if i > 0:
                                    t.wait_ge(s_kr, i)
                                    t.wait_ge(v_asm, 2 * i)
                            else:
                                t.wait_ge(v_asm, 2 * i + 1)
                        for gidx in gl:
                            s_ = (gidx - 1) % 8
                            t.matmul(pc[0:64, s_ * 128:(s_ + 1) * 128],
                                     on_[:, gidx * 64:(gidx + 1) * 64], idn[:, :],
                                     start=True, stop=True)
                    t.fin(p_ont, 1)
                    if i == 0 and b == 0:
                        t.fin(p_bont, 1)
                        t.wait_ge(v_ext, 1)
                        t.matmul(pa[0:1, 992:993], red_b[:, 0:1], onescol[:, 0:1], start=True, stop=True)
                        t.matmul(pa[0:1, 993:994], red_b[:, 1:2], onescol[:, 0:1], start=True, stop=True)
                        t.fin(p_bst, 1)
                        t.wait_ge(v_ext, 3)
                        t.matmul(pa[:, 994:995], ones1[0:1, :], lb_c[0:1, 1:2], start=True, stop=True)
                        t.matmul(pa[:, 995:996], ones1[0:1, :], lb_c[0:1, 4:5], start=True, stop=True)
                        t.fin(p_bbc, 1)
                t.wait_ge(v_asm, 2 * (i + 1))
                t.wait_ge(v_dga, 1)
                t.wait_ge(s_h1, i + 1)
                if i == 0:
                    t.wait_ge(v_h2b, 1)
                for c in range(NCH):
                    t.matmul(pa[:, c * 128:(c + 1) * 128], fma[:, c * 128:(c + 1) * 128],
                             dga[:, c * 128:(c + 1) * 128], start=True, stop=True)
                t.fin(p_atm, 1)
                if i + 1 < NT:
                    emit_u1t(t, i + 1)
                t.wait_ge(v_u2, i + 1)
                if i >= 4:
                    t.wait_ge(s_h1, i + 1)
                    if i == 4:
                        t.wait_ge(v_mod, 6)
                        t.wait_ge(s_exp, scb(4))
                    else:
                        t.wait_ge(s_h2, i)
                    for c in range(NCH):
                        dstp = pd0 if c < 4 else pd1
                        t.matmul(dstp[:, (c % 4) * 128:(c % 4 + 1) * 128],
                                 u2[:, c * 128:(c + 1) * 128], idn[:, :], start=True, stop=True)
                else:
                    t.wait_ge(v_x2, i + 1)
                    for c in range(NCH):
                        t.matmul(pa[:, c * 128:(c + 1) * 128], u2[:, c * 128:(c + 1) * 128],
                                 idn[:, :], start=True, stop=True)
                t.fin(p_u2t, 1)
                t.wait_ge(v_xrh, i + 1)
                t.wait_ge(ld_w, LW["wr"])
                for hh2 in range(2):
                    for kc in range(NCH):
                        if hh2 == 0 and kc == 4:
                            t.wait_ge(v_xr, i + 1)
                        t.matmul(pa[:, hh2 * 512:hh2 * 512 + 512],
                                 xr[:, kc * 128:(kc + 1) * 128],
                                 W["wr"][:, kc * D + hh2 * 512: kc * D + hh2 * 512 + 512],
                                 start=(kc == 0), stop=(kc == 7))
                t.fin(p_r, 1)
                t.wait_ge(ld_w, LW["wkf"])
                t.wait_ge(v_asm, 2 * (i + 1))
                for mc in range(NCH):
                    for kc in range(NCH):
                        t.matmul(pc[:, mc * 128:(mc + 1) * 128],
                                 W["wkf"][:, kc * D + mc * 128: kc * D + (mc + 1) * 128],
                                 xk[:, kc * 128:(kc + 1) * 128],
                                 start=(kc == 0), stop=(kc == 7))
                t.fin(p_kf, 1)
                if i + 1 < NT:
                    emit_q(t, i + 1)
                    emit_k(t, i + 1)
                t.wait_ge(ld_w, LW["wvf"])
                t.wait_ge(s_krh, i + 1)
                t.wait_ge(s_rth, i + 1)
                for hh2 in range(2):
                    if hh2 == 1:
                        t.wait_ge(s_rt, i + 1)
                    for kc in range(NCH):
                        if hh2 == 0 and kc == 4:
                            t.wait_ge(s_kr, i + 1)
                        t.matmul(pa[:, hh2 * 512:hh2 * 512 + 512],
                                 ksq[:, kc * 128:(kc + 1) * 128],
                                 W["wvf"][:, kc * D + hh2 * 512: kc * D + hh2 * 512 + 512],
                                 start=(kc == 0), stop=(kc == 7))
                t.fin(p_vf, 1)
                if i + 1 < NT:
                    emit_vp(t, i + 1)

        if DEBUG_DUMP:
            dbg_sem = sem("dbg_sem")
            name2sb = dict(vg=vg[0:64, :],
                           h2fv=h2f[:, :].rearrange("p (c w) -> p c w", w=132)[:, :, 0:129],
                           xin0=xin[0], xin1=xin[1], xin2=xin[2], u1=u1, u2=u2,
                           h1f=h1f, dt=dt_, xr=xr, xk=xk, rt=rt,
                           krel=krel, ksq=ksq, ffnf=ffnf, qg=qg, kg=kg,
                           a2u=a2u, vbs=vbs, on_=on_, fma=fma, rinv=rinv,
                           modf=modf, dga=dga, gbT=gbT, sT=sT, h1bf=h1bf,
                           x2b0=x2b_[0], x2b1=x2b_[1], yb0=yb_[0], yb1=yb_[1],
                           rstdb=rstdb, h2bf=h2bf, xbf=xbf, atnb=atnb,
                           x2bf=x2bf, lb_c=lb_c, ln_c=ln_c, ln_s=ln_s)
            dbg_outs = {}
            def _ap(x):
                return x if isinstance(x, bass.AP) else x[:, :]
            for nm in DEBUG_DUMP:
                ap_t = _ap(name2sb[nm])
                dbg_outs[nm] = nc.dram_tensor("dbg_" + nm, list(ap_t.shape), ap_t.dtype,
                                              kind="ExternalOutput").ap()

            @block.sync
            def _(s):
                s.wait_ge(st_y, 16 * NT)
                nd = 0
                for nm in DEBUG_DUMP:
                    s.dma_start(out=dbg_outs[nm], in_=_ap(name2sb[nm])).then_inc(dbg_sem, 16)
                    nd += 1
                s.wait_ge(dbg_sem, 16 * nd)

    return nc


def _host_prep(inputs):
    bf = ml_dtypes.bfloat16
    x = np.ascontiguousarray(np.asarray(inputs["L0_attn_x"], dtype=np.float32))
    B, S, _ = x.shape
    perm = np.concatenate([np.r_[mc * 64:(mc + 1) * 64, (mc + 8) * 64:(mc + 9) * 64]
                           for mc in range(8)])
    wq = np.ascontiguousarray(np.asarray(inputs["L0_attn_Wq"], np.float32)[:, perm])
    wk = np.ascontiguousarray(np.asarray(inputs["L0_attn_Wk"], np.float32)[:, perm])
    wv = np.ascontiguousarray(np.asarray(inputs["L0_attn_Wv"], np.float32)[:, perm])
    bq = np.asarray(inputs["L0_attn_bq"], np.float32)[perm]
    bk = np.asarray(inputs["L0_attn_bk"], np.float32)[perm]
    bv = np.asarray(inputs["L0_attn_bv"], np.float32)[perm]
    wr = np.ascontiguousarray(np.asarray(inputs["L0_ffn_W_r"], np.float32).T)
    wkf = np.ascontiguousarray(np.asarray(inputs["L0_ffn_W_k"], np.float32).T)
    wvf = np.ascontiguousarray(np.asarray(inputs["L0_ffn_W_v"], np.float32).T)
    wmod = np.ascontiguousarray(np.asarray(inputs["L0_wmod"], np.float32))
    cc = np.asarray(inputs["L0_c"], np.float32)
    mur = np.ascontiguousarray(np.asarray(inputs["L0_ffn_mu_r"], np.float32).reshape(8, 128).T)
    muk = np.ascontiguousarray(np.asarray(inputs["L0_ffn_mu_k"], np.float32).reshape(8, 128).T)

    def bias_pat(b):
        p = np.zeros((64, 16), np.float32)
        for mc in range(8):
            p[:, mc] = b[mc * 128: mc * 128 + 64]
            p[:, 8 + mc] = b[mc * 128 + 64: (mc + 1) * 128]
        return p

    consts = np.zeros((128, 16), np.float32)
    for col, val in [(C_EPS, 1e-5), (C_NHALF, -0.5), (C_QSC, 0.125), (C_NSH, -20.0),
                     (C_HALF, 0.5), (C_NEG1, -1.0), (C_RD, 1.0 / 1024.0), (C_ONE, 1.0),
                     (C_QTR, 0.25), (C_NQTR, -0.25), (C_15, 1.5)]:
        consts[:, col] = val
    consts[:, C_QK] = np.full(128, 0x5F3759DF, np.int32).view(np.float32)
    consts[:, C_SH1] = np.full(128, 1, np.int32).view(np.float32)
    idn = np.eye(128, dtype=np.float32).astype(bf)
    qs = np.zeros((64, GSLOT * 128), np.float32)
    ks = np.zeros((64, GSLOT * 128), np.float32)
    col_t = np.arange(GSLOT * 128) % 8
    for tt in range(8):
        qs[tt, col_t == tt] = 1.0
        ks[tt, col_t == tt] = 1024.0
    qs[8, :] = 1.0
    ks[8, :] = -1024.0
    qs = qs.astype(bf)
    ks = ks.astype(bf)
    ones1 = np.ones((1, 128), np.float32)
    ones1b = np.ones((1, 128), np.float32).astype(bf)
    onescol = np.ones((128, 1), np.float32)

    bv_nat = np.asarray(inputs["L0_attn_bv"], np.float32)
    bvt = np.ascontiguousarray(np.repeat(bv_nat.reshape(16, 1, 64), 8, axis=1).reshape(128, 64))
    shared = dict(wq=wq, wk=wk, wv=wv, wr=wr, wkf=wkf, wvf=wvf, wmod=wmod,
                  consts=consts, idn=idn, qstat=qs, kstat=ks, bvt=bvt,
                  bqp=bias_pat(bq), bkp=bias_pat(bk), bvp=bias_pat(bv),
                  mur=mur, muk=muk, ones1=ones1, ones1b=ones1b, onescol=onescol)
    in_maps = []
    for ci in range(8):
        bidx, half = ci // 2, ci % 2
        t0 = half * 2048
        m = dict(shared)
        m["x"] = np.ascontiguousarray(x[bidx, t0:t0 + 2048])
        m["xbnd"] = (np.zeros((1, D), np.float32) if half == 0
                     else np.ascontiguousarray(x[bidx, 2047:2048]))
        m["flag"] = np.full((128, 1), float(half), np.float32)
        m["crow"] = np.ascontiguousarray(cc[bidx:bidx + 1])
        in_maps.append(m)
    return in_maps


def kernel(**inputs):
    in_maps = _host_prep(inputs)
    nc = build_nc()
    res = run_bass_kernel_spmd(nc, in_maps, core_ids=list(range(8)))
    x = np.asarray(inputs["L0_attn_x"])
    B, S, _ = x.shape
    out = np.empty((B, S, D), np.float32)
    for ci in range(8):
        bidx, half = ci // 2, ci % 2
        out[bidx, half * 2048:(half + 1) * 2048] = res.results[ci]["y"]
    return out


# version 16
# speedup vs baseline: 1.1497x; 1.0393x over previous
import numpy as np
import ml_dtypes
import contextlib
import concourse.bass as bass
import concourse.mybir as mybir
from concourse.bass_utils import run_bass_kernel_spmd

F32 = mybir.dt.float32
BF16 = mybir.dt.bfloat16
AF = mybir.ActivationFunctionType
OP = mybir.AluOpType
AX = mybir.AxisListType

D = 1024
NT = 16
TPT = 128
NCH = 8
GSLOT = 17

(C_EPS, C_NHALF, C_QSC, C_NSH, C_HALF, C_NEG1, C_RD, C_ONE, C_QTR,
 C_NQTR, C_QK, C_SH1, C_15, C_X0, C_X1, C_X2) = range(16)

MVNAME = ["sa", "ca", "ga", "sb", "cb", "gb"]

DEBUG_DUMP = []


class _Trk:

    _OPS = {"tensor_copy", "tensor_tensor", "tensor_scalar", "memset",
            "scalar_tensor_tensor", "reciprocal", "reduce_sum", "reduce_max",
            "tensor_reduce", "bn_stats", "bn_aggr", "activation", "copy",
            "add", "mul", "matmul", "transpose"}

    def __init__(self, eng):
        object.__setattr__(self, "_e", eng)
        object.__setattr__(self, "last", None)
        object.__setattr__(self, "_finned", None)
        object.__setattr__(self, "_pend", set())

    @staticmethod
    def _apname(x):
        t = getattr(x, "tensor", None)
        nm = getattr(t, "name", None)
        return nm

    def __getattr__(self, n):
        eng = object.__getattribute__(self, "_e")
        a = getattr(eng, n)
        if not callable(a) or n in ("wait_ge",):
            return a
        trk = self

        def w(*args, **kw):
            if n in _Trk._OPS:
                pend = object.__getattribute__(trk, "_pend")
                names = []
                for x in list(args) + list(kw.values()):
                    nm = _Trk._apname(x)
                    if nm is not None:
                        names.append(nm)
                outs = names[:1]
                if "accum_out" in kw and kw["accum_out"] is not None:
                    nm = _Trk._apname(kw["accum_out"])
                    if nm:
                        outs.append(nm)
                ins = names[1:]
                if any(nm in pend for nm in ins):
                    eng.drain()
                    pend.clear()
                r = a(*args, **kw)
                for nm in outs:
                    pend.add(nm)
            else:
                r = a(*args, **kw)
            if r is not None and hasattr(r, "then_inc"):
                object.__setattr__(trk, "last", r)
            return r
        return w

    def fin(self, sem_, n_=1):
        li = object.__getattribute__(self, "last")
        if li is object.__getattribute__(self, "_finned"):
            object.__getattribute__(self, "_e").drain().then_inc(sem_, n_)
        else:
            li.then_inc(sem_, n_)
            object.__setattr__(self, "_finned", li)



def nbatch(i):
    return 5 if i == 0 else 4


def scb(i):
    return sum(nbatch(j) for j in range(i))


def batch_groups(i, b):
    if i == 0 and b == 0:
        return [0]
    bb = b - 1 if i == 0 else b
    return [1 + 4 * bb + k for k in range(4)]



def _emit_rsqrt(v, consts, out_col, var_ap, eps_bias, scr):
    I32 = mybir.dt.int32
    P = var_ap.shape[0]
    ve = scr[0:P, 0:1]
    t_ = scr[0:P, 1:2]
    v.tensor_scalar(ve, var_ap, eps_bias, None, OP.add)
    v.tensor_tensor(out_col.bitcast(I32), ve.bitcast(I32),
                    consts[0:P, C_SH1:C_SH1 + 1].bitcast(I32), OP.logical_shift_right)
    v.tensor_tensor(out_col.bitcast(I32), consts[0:P, C_QK:C_QK + 1].bitcast(I32),
                    out_col.bitcast(I32), OP.subtract)
    for _ in range(2):
        v.tensor_tensor(t_, out_col, out_col, OP.mult)
        v.tensor_tensor(t_, t_, ve, OP.mult)
        v.tensor_scalar(t_, t_, consts[0:P, C_NHALF:C_NHALF + 1], consts[0:P, C_15:C_15 + 1],
                        OP.mult, OP.add)
        v.tensor_tensor(out_col, out_col, t_, OP.mult)


def build_nc():
    nc = bass.Bass("TRN2", target_bir_lowering=False, detect_race_conditions=False)
    x_h = nc.dram_tensor("x", [NT * TPT, D], F32, kind="ExternalInput").ap()
    xb_h = nc.dram_tensor("xbnd", [1, D], F32, kind="ExternalInput").ap()
    flag_h = nc.dram_tensor("flag", [128, 1], F32, kind="ExternalInput").ap()
    w_h = {}
    for w in ["wq", "wk", "wv", "wr", "wkf", "wvf"]:
        w_h[w] = nc.dram_tensor(w, [D, D], F32, kind="ExternalInput").ap()
    wmod_h = nc.dram_tensor("wmod", [D, 6 * D], F32, kind="ExternalInput").ap()
    crow_h = nc.dram_tensor("crow", [1, D], F32, kind="ExternalInput").ap()
    consts_h = nc.dram_tensor("consts", [128, 16], F32, kind="ExternalInput").ap()
    idn_h = nc.dram_tensor("idn", [128, 128], BF16, kind="ExternalInput").ap()
    qstat_h = nc.dram_tensor("qstat", [64, GSLOT * 128], BF16, kind="ExternalInput").ap()
    kstat_h = nc.dram_tensor("kstat", [64, GSLOT * 128], BF16, kind="ExternalInput").ap()
    bqp_h = nc.dram_tensor("bqp", [64, 16], F32, kind="ExternalInput").ap()
    bkp_h = nc.dram_tensor("bkp", [64, 16], F32, kind="ExternalInput").ap()
    bvp_h = nc.dram_tensor("bvp", [64, 16], F32, kind="ExternalInput").ap()
    bvt_h = nc.dram_tensor("bvt", [128, 64], F32, kind="ExternalInput").ap()
    mur_h = nc.dram_tensor("mur", [128, 8], F32, kind="ExternalInput").ap()
    muk_h = nc.dram_tensor("muk", [128, 8], F32, kind="ExternalInput").ap()
    ones1_h = nc.dram_tensor("ones1", [1, 128], F32, kind="ExternalInput").ap()
    ones1b_h = nc.dram_tensor("ones1b", [1, 128], BF16, kind="ExternalInput").ap()
    onescol_h = nc.dram_tensor("onescol", [128, 1], F32, kind="ExternalInput").ap()
    y_h = nc.dram_tensor("y", [NT * TPT, D], F32, kind="ExternalOutput").ap()

    es = contextlib.ExitStack()
    sbuf = lambda n, s, d: es.enter_context(nc.sbuf_tensor(n, s, d))
    psum = lambda n, s, d: es.enter_context(nc.psum_tensor(n, s, d))
    sem = lambda n: es.enter_context(nc.semaphore(n))
    with es:
        W = {w: sbuf(w + "s", [128, 8 * D], BF16) for w in ["wq", "wk", "wv", "wr", "wkf", "wvf"]}
        wmb = [sbuf(f"wmb{i}", [128, 4 * 512], BF16) for i in range(2)]
        xin = [sbuf(f"xin{i}", [128, D], F32) for i in range(3)]
        x2b_ = [sbuf(f"x2b{i}", [128, D], F32) for i in range(2)]
        yb_ = [sbuf(f"yb{i}", [128, D], F32) for i in range(2)]
        u1 = sbuf("u1", [128, D], BF16)
        u2 = sbuf("u2", [128, D], BF16)
        h1f = sbuf("h1f", [128, D], BF16)
        h2f = sbuf("h2f", [128, 8 * 132], BF16)
        dt_ = sbuf("dt", [128, D], BF16)
        xr = sbuf("xr", [128, D], BF16)
        xk = sbuf("xk", [128, D], BF16)
        rt = sbuf("rt", [128, D], BF16)
        krel = sbuf("krel", [128, D], BF16)
        ksq = sbuf("ksq", [128, D], BF16)
        ffnf = sbuf("ffnf", [128, D], BF16)
        qg = sbuf("qg", [128, GSLOT * 128], BF16)
        kg = sbuf("kg", [128, GSLOT * 128], BF16)
        vg = sbuf("vg", [128, GSLOT * 128], BF16)
        a2u = sbuf("a2u", [128, GSLOT * 128], BF16)
        vbs = sbuf("vbs", [128, GSLOT * 72], BF16)
        on_ = sbuf("on", [128, GSLOT * 64], BF16)
        fma = sbuf("fma", [128, D], BF16)
        rinv = sbuf("rinv", [128, GSLOT], F32)
        idn = sbuf("idn_s", [128, 128], BF16)
        consts = sbuf("consts_s", [128, 16], F32)
        bqp = sbuf("bqp_s", [64, 16], F32)
        bkp = sbuf("bkp_s", [64, 16], F32)
        bvp = sbuf("bvp_s", [64, 16], F32)
        bvt = sbuf("bvt_s", [128, 64], F32)
        murf = sbuf("murf", [128, 8], F32)
        mukf = sbuf("mukf", [128, 8], F32)
        modf = sbuf("modf", [128, 48], F32)
        clsc = sbuf("clsc", [128, 24], F32)
        dga = sbuf("dga", [128, 8 * 128], BF16)
        gbT = sbuf("gbT", [128, D], BF16)
        gbrow = sbuf("gbrow", [1, D], F32)
        gbc = sbuf("gbc", [128, 8], BF16)
        gbT8 = sbuf("gbT8", [8, 128], F32)
        ln_s = sbuf("ln_s", [128, 28], F32)
        ln_c = sbuf("ln_c", [128, 4], F32)
        crow = sbuf("crow_s", [1, D], F32)
        silcb = sbuf("silcb", [1, D], BF16)
        sT = sbuf("sT", [128, 8], BF16)
        modrow = sbuf("modrow", [1, D], F32)
        xbnd = sbuf("xbnd_s", [1, D], F32)
        u1b = sbuf("u1b", [1, D], BF16)
        lb_s = sbuf("lb_s", [1, 14], F32)
        lb_c = sbuf("lb_c", [1, 6], F32)
        h1bf = sbuf("h1bf", [128, 8], BF16)
        xbf = sbuf("xbf", [128, 8], F32)
        atnb = sbuf("atnb", [128, 8], F32)
        x2bf = sbuf("x2bf", [128, 8], F32)
        x2sq = sbuf("x2sq", [128, 8], F32)
        red_b = sbuf("red_b", [128, 2], F32)
        rstdb = sbuf("rstdb", [128, 2], F32)
        h2bf = sbuf("h2bf", [128, 8], BF16)
        flagc = sbuf("flagc", [128, 1], F32)
        ones1 = sbuf("ones1_s", [1, 128], F32)
        ones1b = sbuf("ones1b_s", [1, 128], BF16)
        onescol = sbuf("onescol_s", [128, 1], F32)

        pa = psum("pa", [128, 1024], F32)
        pb = psum("pb", [128, 1024], F32)
        pc = psum("pc", [128, 1024], F32)
        pd0 = psum("pd0", [128, 512], F32)
        pd1 = psum("pd1", [128, 512], F32)

        ld_s, ld_x, ld_w, st_y = sem("ld_s"), sem("ld_x"), sem("ld_w"), sem("st_y")
        p_sT, p_xbf, p_mvh, p_mtr = sem("p_sT"), sem("p_xbf"), sem("p_mvh"), sem("p_mtr")
        p_q, p_k, p_v, p_bqkv = sem("p_q"), sem("p_k"), sem("p_v"), sem("p_bqkv")
        p_sc, p_vt, p_av, p_ont = sem("p_sc"), sem("p_vt"), sem("p_av"), sem("p_ont")
        p_bont, p_bst, p_bbc = sem("p_bont"), sem("p_bst"), sem("p_bbc")
        p_atm, p_u2t, p_u1t = sem("p_atm"), sem("p_u2t"), sem("p_u1t")
        p_r, p_kf, p_vf = sem("p_r"), sem("p_kf"), sem("p_vf")
        g_gbr, p_gbt, v_gbt = sem("g_gbr"), sem("p_gbt"), sem("v_gbt")
        p_gb1, v_gb8 = sem("p_gb1"), sem("v_gb8")
        s_h2h, v_xrh = sem("s_h2h"), sem("v_xrh")
        s_rth, s_krh = sem("s_rth"), sem("s_krh")
        v_lnb, v_u1b, v_silu, v_sT = sem("v_lnb"), sem("v_u1b"), sem("v_silu"), sem("v_sT")
        v_mve, v_mod, v_dga = sem("v_mve"), sem("v_mod"), sem("v_dga")
        v_xbf, v_h1b = sem("v_xbf"), sem("v_h1b")
        v_ln1, v_u1, v_qe, v_ke, v_ve = sem("v_ln1"), sem("v_u1"), sem("v_qe"), sem("v_ke"), sem("v_ve")
        v_vbe, v_rin, v_asm, v_x2 = sem("v_vbe"), sem("v_rin"), sem("v_asm"), sem("v_x2")
        v_ln2, v_u2, v_xr, v_ffn, v_y = sem("v_ln2"), sem("v_u2"), sem("v_xr"), sem("v_ffn"), sem("v_y")
        v_bev, v_ext, v_h2b = sem("v_bev"), sem("v_ext"), sem("v_h2b")
        s_silu, s_lb1, s_lb2, s_ln = sem("s_silu"), sem("s_lb1"), sem("s_lb2"), sem("s_ln")
        s_h1, s_exp, s_non, s_h2 = sem("s_h1"), sem("s_exp"), sem("s_non"), sem("s_h2")
        s_rt, s_kr, s_asm = sem("s_rt"), sem("s_kr"), sem("s_asm")

        block = es.enter_context(nc.Block())

        def score_pd(i, b):
            if i < 6 or (i == 0 and b == 0):
                return pd0
            return pd0 if b % 2 == 0 else pd1

        def mv_loads(nm):
            return [nm + jh + h for jh in "01" for h in "01"]
        dma_order = (mv_loads("sa") + mv_loads("ca") + ["wq"] + mv_loads("ga")
                     + ["wk", "wv"] + mv_loads("sb") + mv_loads("cb")
                     + ["wr", "wkf", "wvf"] + mv_loads("gb"))
        LW = {nm: 16 * (k + 1) for k, nm in enumerate(dma_order)}

        @block.scalar
        def _(s):
            for dst, src in [(consts[:], consts_h), (flagc[:], flag_h),
                             (crow[:], crow_h), (xbnd[:], xb_h),
                             (bqp[:], bqp_h), (bkp[:], bkp_h), (bvp[:], bvp_h), (bvt[:], bvt_h),
                             (murf[:], mur_h), (mukf[:], muk_h),
                             (idn[:], idn_h), (ones1[:], ones1_h),
                             (ones1b[:], ones1b_h), (onescol[:], onescol_h),
                             (qg[64:128, :], qstat_h), (kg[64:128, :], kstat_h)]:
                s.dma_start(out=dst, in_=src).then_inc(ld_s, 16)
        NLD_S = 16

        @block.sync
        def _(s):
            for i in range(min(3, NT)):
                s.dma_start(out=xin[i % 3][:], in_=x_h[i * TPT:(i + 1) * TPT, :]).then_inc(ld_x, 16)
            for i in range(NT):
                s.wait_ge(v_y, i + 1)
                s.dma_start(out=y_h[i * TPT:(i + 1) * TPT, :], in_=yb_[i % 2][:]).then_inc(st_y, 16)
                j = i + 3
                if j < NT:
                    s.dma_start(out=xin[j % 3][:], in_=x_h[j * TPT:(j + 1) * TPT, :]).then_inc(ld_x, 16)
            s.wait_ge(st_y, 16 * NT)

        @block.gpsimd
        def _(g):
            for nm in dma_order:
                if nm in W:
                    g.dma_start(out=W[nm][:, :].rearrange("p (kc d) -> p kc d", d=D),
                                in_=w_h[nm].rearrange("(kc p) d -> p kc d", p=128)).then_inc(ld_w, 16)
                else:
                    v = MVNAME.index(nm[:2])
                    jh, h = int(nm[2]), int(nm[3])
                    ell = 4 * v + 2 * jh + h
                    if ell >= 2:
                        g.wait_ge(p_mvh, 2 * v + jh)
                    src = wmod_h[h * 512:(h + 1) * 512, v * D + jh * 512: v * D + jh * 512 + 512]
                    g.dma_start(out=wmb[h][:, :].rearrange("p (ic j) -> p ic j", j=512),
                                in_=src.rearrange("(ic p) j -> p ic j", p=128)).then_inc(ld_w, 16)
            g.wait_ge(v_gb8, 1)
            for c in range(NCH):
                g.dma_start(out=gbrow[0:1, c * 128:(c + 1) * 128],
                            in_=gbT8[c:c + 1, :]).then_inc(g_gbr, 16)

        @block.scalar
        def _(s):
            s = _Trk(s)
            s.wait_ge(ld_s, 16 * NLD_S)
            s.activation(modrow[:], crow[:], AF.Silu).then_inc(s_silu, 1)

            def emit_h1(s, i):
                s.wait_ge(p_u1t, i + 1)
                s.wait_ge(v_mod, 2)
                for c in range(NCH):
                    srcp = (pa if i == 0 else pb)[:, c * 128:(c + 1) * 128]
                    s.activation(h1f[:, c * 128:(c + 1) * 128], srcp,
                                 AF.Identity, scale=modf[:, 8 + c:9 + c], bias=modf[:, c:c + 1])
                s.fin(s_h1, 1)

            emit_h1(s, 0)
            for i in range(NT):
                for b in range(nbatch(i)):
                    gl = batch_groups(i, b)
                    s.wait_ge(p_sc, scb(i) + b + 1)
                    pdx = score_pd(i, b)
                    ncols = 128 * len(gl)
                    s.activation(a2u[:, gl[0] * 128: gl[0] * 128 + ncols], pdx[:, 0:ncols],
                                 AF.Exp, scale=consts[:, C_QSC:C_QSC + 1],
                                 bias=consts[:, C_NSH:C_NSH + 1]).then_inc(s_exp, 1)
                    s.wait_ge(p_av, scb(i) + b + 1)
                    s.wait_ge(v_rin, scb(i) + b + 1)
                    for gidx in gl:
                        if gidx == 0:
                            src = pd0[:, 256:320]
                        else:
                            pb_ = (gidx - 1) // 4
                            k = (gidx - 1) % 4
                            cb = (pb_ % 2) * 512 + k * 65
                            src = pb[:, cb:cb + 64]
                        s.activation(on_[:, gidx * 64:(gidx + 1) * 64], src,
                                     AF.Identity, scale=rinv[:, gidx:gidx + 1])
                    s.fin(s_non, 1)
                s.wait_ge(p_u2t, i + 1)
                s.wait_ge(v_mod, 5)
                for c in range(NCH):
                    if i >= 4:
                        srcp = (pd0 if c < 4 else pd1)[:, (c % 4) * 128:(c % 4 + 1) * 128]
                    else:
                        srcp = pa[:, c * 128:(c + 1) * 128]
                    s.activation(h2f[:, c * 132 + 1: c * 132 + 129], srcp,
                                 AF.Identity, scale=modf[:, 32 + c:33 + c], bias=modf[:, 24 + c:25 + c])
                    if c == 3:
                        s.fin(s_h2h, 1)
                s.fin(s_h2, 1)
                if i + 1 < NT:
                    emit_h1(s, i + 1)
                s.wait_ge(p_r, i + 1)
                s.activation(rt[:, 0:512], pa[:, 0:512], AF.Tanh, scale=consts[:, C_HALF:C_HALF + 1])
                s.fin(s_rth, 1)
                s.activation(rt[:, 512:1024], pa[:, 512:1024], AF.Tanh,
                             scale=consts[:, C_HALF:C_HALF + 1]).then_inc(s_rt, 1)
                s.wait_ge(p_kf, i + 1)
                s.activation(krel[:, 0:512], pc[:, 0:512], AF.Relu)
                s.activation(ksq[:, 0:512], krel[:, 0:512], AF.Square)
                s.fin(s_krh, 1)
                s.activation(krel[:, 512:1024], pc[:, 512:1024], AF.Relu)
                s.activation(ksq[:, 512:1024], krel[:, 512:1024], AF.Square).then_inc(s_kr, 1)

        @block.vector
        def _(v):
            v = _Trk(v)
            v.wait_ge(ld_s, 16 * NLD_S)
            v.memset(qg[0:64, 0:128], 0.0)
            v.memset(kg[0:64, 0:128], 0.0)
            v.memset(vg[0:64, 0:128], 0.0)
            v.memset(vbs[:, :], 0.0)
            v.memset(vbs[:, :].rearrange("p (s c) -> p s c", c=72)[:, :, 64:65], 1.0)
            v.memset(h2f[:, :].rearrange("p (s c) -> p s c", c=132)[:, :, 0:1], 0.0)
            v.bn_stats(lb_s[:, 0:6], xbnd[0:1, 0:512])
            v.bn_stats(lb_s[:, 6:12], xbnd[0:1, 512:1024])
            v.bn_aggr(lb_s[:, 12:14], lb_s[:, 0:12]).then_inc(v_lnb, 1)
            _emit_rsqrt(v, consts, lb_c[0:1, 0:1], lb_s[0:1, 13:14],
                        consts[0:1, C_EPS:C_EPS + 1], clsc)
            v.tensor_scalar(lb_c[0:1, 5:6], lb_s[0:1, 12:13], lb_c[0:1, 0:1],
                            consts[0:1, C_NEG1:C_NEG1 + 1], OP.mult, OP.mult)
            v.tensor_scalar(u1b[:], xbnd[:], lb_c[0:1, 0:1], lb_c[0:1, 5:6],
                            OP.mult, OP.add).then_inc(v_u1b, 1)
            v.wait_ge(s_silu, 1)
            v.tensor_copy(silcb[:], modrow[:]).then_inc(v_silu, 1)
            v.wait_ge(p_sT, 1)
            v.tensor_copy(sT[:, :], pd0[:, 480:488]).then_inc(v_sT, 1)
            v.wait_ge(p_xbf, 1)
            v.tensor_copy(xbf[:, :], pa[:, 1000:1008]).then_inc(v_xbf, 1)
            for mv in range(6):
                for jh in range(2):
                    v.wait_ge(p_mvh, 2 * mv + jh + 1)
                    v.tensor_copy(modrow[0:1, jh * 512:(jh + 1) * 512], pd1[0:1, 0:512]).then_inc(v_mve, 1)
                v.wait_ge(p_mtr, mv + 1)
                v.tensor_copy(modf[:, 8 * mv:8 * mv + 8], pd1[:, 384:392])
                if mv in (1, 4):
                    v.tensor_scalar(modf[:, 8 * mv:8 * mv + 8], modf[:, 8 * mv:8 * mv + 8],
                                    consts[:, C_ONE:C_ONE + 1], None, OP.add)
                if mv in (2, 5):
                    cf = 8 * mv
                    gsl = modf[:, cf:cf + 8]
                    v.tensor_scalar(clsc[:, 0:8], gsl, 0.0, None, OP.is_ge)
                    v.tensor_scalar(clsc[:, 8:16], gsl, consts[:, C_QTR:C_QTR + 1], None, OP.max)
                    v.tensor_scalar(clsc[:, 16:24], gsl, consts[:, C_NQTR:C_NQTR + 1], None, OP.min)
                    v.tensor_tensor(clsc[:, 8:16], clsc[:, 8:16], clsc[:, 16:24], OP.subtract)
                    v.tensor_tensor(clsc[:, 8:16], clsc[:, 8:16], clsc[:, 0:8], OP.mult)
                    v.tensor_tensor(gsl, clsc[:, 8:16], clsc[:, 16:24], OP.add)
                    if mv == 5:
                        v.tensor_scalar(gsl, gsl, consts[:, C_HALF:C_HALF + 1], None, OP.mult)
                    if mv == 2:
                        for c in range(NCH):
                            v.tensor_scalar(dga[:, c * 128:(c + 1) * 128], idn[:, :],
                                            modf[:, cf + c:cf + c + 1], None, OP.mult)
                    else:
                        v.tensor_copy(gbc[:, :], gsl)
                    v.fin(v_dga, 1)
                v.fin(v_mod, 1)
                if mv == 1:
                    v.tensor_tensor(x2sq[:, 0:8], pa[:, 1008:1016], modf[:, 8:16], OP.mult)
                    v.tensor_tensor(x2sq[:, 0:8], x2sq[:, 0:8], modf[:, 0:8], OP.add)
                    v.tensor_copy(h1bf[:, :], x2sq[:, 0:8]).then_inc(v_h1b, 1)
            def emit_ln1_stats(v, i):
                v.wait_ge(ld_x, 16 * (i + 1))
                v.bn_stats(ln_s[:, 0:6], xin[i % 3][:, 0:512])
                v.bn_stats(ln_s[:, 6:12], xin[i % 3][:, 512:1024])
                v.bn_aggr(ln_s[:, 24:26], ln_s[:, 0:12]).then_inc(v_ln1, 1)

            def emit_ln1_fin(v, i):
                v.wait_ge(p_u1t, i)
                _emit_rsqrt(v, consts, ln_c[:, 0:1], ln_s[:, 25:26],
                            consts[:, C_EPS:C_EPS + 1], clsc)
                v.tensor_scalar(ln_c[:, 2:3], ln_s[:, 24:25], ln_c[:, 0:1],
                                consts[:, C_NEG1:C_NEG1 + 1], OP.mult, OP.mult)
                v.tensor_scalar(u1[:], xin[i % 3][:], ln_c[:, 0:1], ln_c[:, 2:3],
                                OP.mult, OP.add).then_inc(v_u1, 1)

            def emit_ln1(v, i):
                emit_ln1_stats(v, i)
                emit_ln1_fin(v, i)

            v.wait_ge(p_gb1, 1)
            v.tensor_copy(gbT8[:, :], pd1[0:8, 384:512])
            v.fin(v_gb8, 1)
            v.wait_ge(p_gbt, 1)
            v.tensor_copy(gbT[:, 0:512], pd0[:, 0:512])
            v.tensor_copy(gbT[:, 512:1024], pd1[:, 0:512])
            v.fin(v_gbt, 1)

            def emit_evac(v, i, which):
                psrc, gdst, bpat, vsem, pwait = {
                    "q": (pb, qg, bqp, v_qe, p_q), "k": (pc, kg, bkp, v_ke, p_k),
                    "v": (pb, vg, None, v_ve, p_v)}[which]
                v.wait_ge(pwait, i + 1)
                for par in range(2):
                    in3 = psrc[64 * par:64 * par + 64, :].rearrange(
                        "p (mc g t) -> p mc g t", g=16, t=8)
                    out3 = gdst[0:64, :].rearrange("p (g jj t) -> p g jj t", jj=16, t=8)[
                        :, 1:GSLOT, 8 * par:8 * par + 8, :].rearrange("p g j t -> p j g t")
                    if bpat is None:
                        v.tensor_copy(out3, in3)
                    else:
                        bias = bpat[:, 8 * par:8 * par + 8].unsqueeze(2).unsqueeze(3).broadcast_to([64, 8, 16, 8])
                        v.tensor_tensor(out3, in3, bias, OP.add)
                v.fin(vsem, 1)

            emit_ln1(v, 0)
            emit_evac(v, 0, "q")
            emit_evac(v, 0, "k")
            for i in range(NT):
                emit_evac(v, i, "v")
                if i == 0:
                    v.wait_ge(p_bqkv, 1)
                    for (tix, gdst, bpat) in [(0, qg, bqp), (1, kg, bkp)]:
                        for par in range(2):
                            in2 = pd0[64 * par:64 * par + 64, 8 * tix:8 * tix + 8]
                            out2 = gdst[0:64, :].rearrange("p (g jj t) -> p g jj t", jj=16, t=8)[
                                :, 0, 8 * par:8 * par + 8, 0]
                            v.tensor_tensor(out2, in2, bpat[:, 8 * par:8 * par + 8], OP.add)
                    for par in range(2):
                        in2 = pd0[64 * par:64 * par + 64, 16:24]
                        out2 = vg[0:64, :].rearrange("p (g jj t) -> p g jj t", jj=16, t=8)[
                            :, 0, 8 * par:8 * par + 8, 0]
                        v.tensor_copy(out2, in2)
                    v.fin(v_bev, 1)
                for b in range(nbatch(i)):
                    gl = batch_groups(i, b)
                    v.wait_ge(p_vt, scb(i) + b + 1)
                    if len(gl) == 1:
                        v.tensor_tensor(vbs[:, 0:64], pa[:, 256:320], bvt[:, :], OP.add)
                    else:
                        pb_ = b - 1 if i == 0 else b
                        base = (pb_ % 2) * 512
                        dstv = vbs[:, :].rearrange("p (s c) -> p s c", c=72)[:, gl[0]:gl[0] + 4, 0:64]
                        srcv = pa[:, base:base + 256].rearrange("p (kk c) -> p kk c", c=64)
                        bias4 = bvt[:, :].unsqueeze(1).broadcast_to([128, 4, 64])
                        v.tensor_tensor(dstv, srcv, bias4, OP.add)
                    v.fin(v_vbe, 1)
                    v.wait_ge(p_av, scb(i) + b + 1)
                    if len(gl) == 1:
                        v.reciprocal(rinv[:, 0:1], pd0[:, 320:321])
                    else:
                        pb_ = b - 1 if i == 0 else b
                        base = (pb_ % 2) * 512 + 64
                        src_s = pb[:, base - 64:base - 64 + 260].rearrange("p (s c) -> p s c", c=65)[:, :, 64]
                        v.reciprocal(rinv[:, gl[0]:gl[0] + 4], src_s)
                    v.fin(v_rin, 1)
                    pb_h = b - 1 if i == 0 else b
                    if pb_h in (1, 3) and len(gl) == 4:
                        hh = pb_h // 2
                        bofs = 1 if i == 0 else 0
                        v.wait_ge(p_ont, scb(i) + bofs + 2 * (hh + 1))
                        for par in range(2):
                            srca = pc[0:64, :].rearrange(
                                "p (g c2 pr t) -> p c2 pr g t", t=8, pr=2, c2=8)[:, :, par, :, :]
                            dsta = fma[64 * par:64 * par + 64, :].rearrange(
                                "p (c h2 g t) -> p c h2 g t", h2=2, g=8, t=8)[:, :, hh, :, :]
                            v.tensor_copy(dsta, srca)
                        v.fin(v_asm, 1)
                        if pb_h == 1 and i + 1 < NT:
                            emit_ln1_stats(v, i + 1)
                        if pb_h == 3 and i + 1 < NT:
                            emit_ln1_fin(v, i + 1)
                    if i == 0 and b == 0:
                        v.wait_ge(p_bont, 1)
                        for par in range(2):
                            src2 = pd0[0:64, 128:256].rearrange("p (c2 pr t) -> p c2 pr t", pr=2, t=8)[:, :, par, 0]
                            v.tensor_copy(atnb[64 * par:64 * par + 64, 0:8], src2)
                        v.wait_ge(v_dga, 1)
                        v.tensor_tensor(atnb[:], atnb[:], modf[:, 16:24], OP.mult)
                        v.wait_ge(v_xbf, 1)
                        v.tensor_tensor(x2bf[:], xbf[:], atnb[:], OP.add)
                        v.tensor_tensor(x2sq[:], x2bf[:], x2bf[:], OP.mult)
                        v.reduce_sum(red_b[:, 0:1], x2bf[:], axis=AX.X)
                        v.reduce_sum(red_b[:, 1:2], x2sq[:], axis=AX.X)
                        v.fin(v_ext, 1)
                        v.wait_ge(p_bst, 1)
                        v.tensor_scalar(lb_c[0:1, 2:3], pa[0:1, 992:993], consts[0:1, C_RD:C_RD + 1],
                                        None, OP.mult)
                        v.tensor_tensor(lb_c[0:1, 3:4], lb_c[0:1, 2:3], lb_c[0:1, 2:3], OP.mult)
                        v.tensor_scalar(lb_c[0:1, 3:4], lb_c[0:1, 3:4], consts[0:1, C_NEG1:C_NEG1 + 1],
                                        consts[0:1, C_EPS:C_EPS + 1], OP.mult, OP.add)
                        v.tensor_scalar(lb_c[0:1, 4:5], pa[0:1, 993:994],
                                        consts[0:1, C_RD:C_RD + 1], None, OP.mult)
                        v.tensor_tensor(lb_c[0:1, 4:5], lb_c[0:1, 4:5], lb_c[0:1, 3:4], OP.add)
                        _emit_rsqrt(v, consts, lb_c[0:1, 1:2], lb_c[0:1, 4:5], 0.0, clsc)
                        v.fin(v_ext, 1)
                        v.tensor_scalar(lb_c[0:1, 4:5], lb_c[0:1, 2:3], lb_c[0:1, 1:2],
                                        consts[0:1, C_NEG1:C_NEG1 + 1], OP.mult, OP.mult)
                        v.fin(v_ext, 1)
                        v.wait_ge(p_bbc, 1)
                        v.tensor_copy(rstdb[:, 0:2], pa[:, 994:996])
                        v.tensor_scalar(x2bf[:], x2bf[:], rstdb[:, 0:1], rstdb[:, 1:2], OP.mult, OP.add)
                        v.tensor_tensor(x2bf[:], x2bf[:], modf[:, 32:40], OP.mult)
                        v.tensor_tensor(x2bf[:], x2bf[:], modf[:, 24:32], OP.add)
                        v.tensor_scalar(h2bf[:], x2bf[:], flagc[:, 0:1], None, OP.mult)
                        v.tensor_copy(h2f[:, :].rearrange("p (c w) -> p c w", w=132)[:, :, 0:1],
                                      h2bf[:, :].unsqueeze(2))
                        v.fin(v_h2b, 1)
                v.wait_ge(p_atm, i + 1)
                v.tensor_tensor(x2b_[i % 2][:], xin[i % 3][:], pa[:, :], OP.add).then_inc(v_x2, 1)
                v.bn_stats(ln_s[:, 12:18], x2b_[i % 2][:, 0:512])
                v.bn_stats(ln_s[:, 18:24], x2b_[i % 2][:, 512:1024])
                v.bn_aggr(ln_s[:, 26:28], ln_s[:, 12:24]).then_inc(v_ln2, 1)
                _emit_rsqrt(v, consts, ln_c[:, 1:2], ln_s[:, 27:28],
                            consts[:, C_EPS:C_EPS + 1], clsc)
                v.tensor_scalar(ln_c[:, 2:3], ln_s[:, 26:27], ln_c[:, 1:2],
                                consts[:, C_NEG1:C_NEG1 + 1], OP.mult, OP.mult)
                v.tensor_scalar(u2[:], x2b_[i % 2][:], ln_c[:, 1:2], ln_c[:, 2:3],
                                OP.mult, OP.add).then_inc(v_u2, 1)
                dt3 = dt_[:, :].rearrange("p (c t) -> p c t", t=128)
                h2s = h2f[:, :].rearrange("p (c w) -> p c w", w=132)
                v.wait_ge(s_h2h, i + 1)
                if i == 0:
                    v.wait_ge(v_h2b, 1)
                v.tensor_tensor(dt3[:, 0:4, :], h2s[:, 0:4, 1:129], h2s[:, 0:4, 0:128], OP.subtract)
                for c in range(4):
                    v.scalar_tensor_tensor(xr[:, c * 128:(c + 1) * 128], dt_[:, c * 128:(c + 1) * 128],
                                           murf[:, c:c + 1], h2f[:, c * 132:c * 132 + 128],
                                           OP.mult, OP.add)
                v.fin(v_xrh, 1)
                v.wait_ge(s_h2, i + 1)
                v.tensor_tensor(dt3[:, 4:8, :], h2s[:, 4:8, 1:129], h2s[:, 4:8, 0:128], OP.subtract)
                for c in range(4, NCH):
                    v.scalar_tensor_tensor(xr[:, c * 128:(c + 1) * 128], dt_[:, c * 128:(c + 1) * 128],
                                           murf[:, c:c + 1], h2f[:, c * 132:c * 132 + 128],
                                           OP.mult, OP.add)
                for c in range(NCH):
                    v.scalar_tensor_tensor(xk[:, c * 128:(c + 1) * 128], dt_[:, c * 128:(c + 1) * 128],
                                           mukf[:, c:c + 1], h2f[:, c * 132:c * 132 + 128],
                                           OP.mult, OP.add)
                if i < NT - 1:
                    v.tensor_copy(h2f[:, :].rearrange("p (c w) -> p c w", w=132)[:, :, 0:1],
                                  h2f[:, :].rearrange("p (c w) -> p c w", w=132)[:, :, 128:129])
                v.fin(v_xr, 1)
                if i + 1 < NT:
                    emit_evac(v, i + 1, "q")
                    emit_evac(v, i + 1, "k")
                v.wait_ge(p_vf, i + 1)
                v.scalar_tensor_tensor(ffnf[:], rt[:], 1.0, pa[:, :], OP.add, OP.mult).then_inc(v_ffn, 1)
                if i >= 2:
                    v.wait_ge(st_y, 16 * (i - 1))
                v.tensor_tensor(yb_[i % 2][:], ffnf[:], gbT[:, :], OP.mult)
                v.tensor_tensor(yb_[i % 2][:], yb_[i % 2][:], x2b_[i % 2][:], OP.add).then_inc(v_y, 1)

        @block.tensor
        def _(t):
            t = _Trk(t)
            t.wait_ge(v_silu, 1)
            for c in range(NCH):
                t.matmul(pd0[:, 480 + c:481 + c], silcb[0:1, c * 128:(c + 1) * 128],
                         ones1b[0:1, 0:1], start=True, stop=True)
            t.fin(p_sT, 1)
            for c in range(NCH):
                t.matmul(pa[:, 1000 + c:1001 + c], xbnd[0:1, c * 128:(c + 1) * 128],
                         ones1[0:1, 0:1], start=True, stop=True)
            t.wait_ge(v_u1b, 1)
            for c in range(NCH):
                t.matmul(pa[:, 1008 + c:1009 + c], u1b[0:1, c * 128:(c + 1) * 128],
                         ones1b[0:1, 0:1], start=True, stop=True)
            t.fin(p_xbf, 1)
            t.wait_ge(v_sT, 1)
            for mv in range(6):
                if mv > 0:
                    t.wait_ge(v_mod, mv)
                for jh in range(2):
                    t.wait_ge(ld_w, LW[MVNAME[mv] + str(jh) + "1"])
                    if jh == 1:
                        t.wait_ge(v_mve, 2 * mv + 1)
                    for ic in range(8):
                        t.matmul(pd1[0:1, 0:512], sT[:, ic:ic + 1],
                                 wmb[ic // 4][:, (ic % 4) * 512:(ic % 4) * 512 + 512],
                                 start=(ic == 0), stop=(ic == 7))
                    t.fin(p_mvh, 1)
                t.wait_ge(v_mve, 2 * mv + 2)
                for c in range(NCH):
                    t.matmul(pd1[:, 384 + c:385 + c], modrow[0:1, c * 128:(c + 1) * 128],
                             ones1[0:1, 0:1], start=True, stop=True)
                t.fin(p_mtr, 1)
            t.wait_ge(v_dga, 2)
            t.matmul(pd1[0:8, 384:512], gbc[:, :], idn[:, :], start=True, stop=True)
            t.fin(p_gb1, 1)
            t.wait_ge(g_gbr, 16 * 8)
            t.matmul(pd0[:, 0:512], ones1[0:1, :], gbrow[0:1, 0:512], start=True, stop=True)
            t.matmul(pd1[:, 0:512], ones1[0:1, :], gbrow[0:1, 512:1024], start=True, stop=True)
            t.fin(p_gbt, 1)
            t.wait_ge(v_gbt, 1)
            def emit_u1t(t, i):
                t.wait_ge(v_u1, i + 1)
                if i == 0:
                    t.wait_ge(v_h1b, 1)
                    t.wait_ge(v_xbf, 1)
                    for c in range(NCH):
                        t.matmul(pa[:, c * 128:(c + 1) * 128], u1[:, c * 128:(c + 1) * 128],
                                 idn[:, :], start=True, stop=True)
                else:
                    t.wait_ge(v_rin, scb(i))
                    t.wait_ge(s_non, scb(i))
                    t.wait_ge(v_vbe, scb(i))
                    for c in range(NCH):
                        t.matmul(pb[:, c * 128:(c + 1) * 128], u1[:, c * 128:(c + 1) * 128],
                                 idn[:, :], start=True, stop=True)
                t.fin(p_u1t, 1)

            def emit_q(t, i):
                t.wait_ge(s_h1, i + 1)
                t.wait_ge(ld_w, LW["wq"])
                for mc in range(NCH):
                    for kc in range(NCH):
                        t.matmul(pb[:, mc * 128:(mc + 1) * 128],
                                 W["wq"][:, kc * D + mc * 128: kc * D + (mc + 1) * 128],
                                 h1f[:, kc * 128:(kc + 1) * 128],
                                 start=(kc == 0), stop=(kc == 7))
                        if i == 0:
                            t.matmul(pd0[:, 0 + mc:1 + mc],
                                     W["wq"][:, kc * D + mc * 128: kc * D + (mc + 1) * 128],
                                     h1bf[:, kc:kc + 1], start=(kc == 0), stop=(kc == 7))
                t.fin(p_q, 1)

            def emit_k(t, i):
                t.wait_ge(ld_w, LW["wk"])
                if i > 0:
                    t.wait_ge(s_kr, i)
                for mc in range(NCH):
                    for kc in range(NCH):
                        t.matmul(pc[:, mc * 128:(mc + 1) * 128],
                                 W["wk"][:, kc * D + mc * 128: kc * D + (mc + 1) * 128],
                                 h1f[:, kc * 128:(kc + 1) * 128],
                                 start=(kc == 0), stop=(kc == 7))
                        if i == 0:
                            t.matmul(pd0[:, 8 + mc:9 + mc],
                                     W["wk"][:, kc * D + mc * 128: kc * D + (mc + 1) * 128],
                                     h1bf[:, kc:kc + 1], start=(kc == 0), stop=(kc == 7))
                t.fin(p_k, 1)

            def emit_vp(t, i):
                t.wait_ge(ld_w, LW["wv"])
                t.wait_ge(v_qe, i + 1)
                for mc in range(NCH):
                    for kc in range(NCH):
                        t.matmul(pb[:, mc * 128:(mc + 1) * 128],
                                 W["wv"][:, kc * D + mc * 128: kc * D + (mc + 1) * 128],
                                 h1f[:, kc * 128:(kc + 1) * 128],
                                 start=(kc == 0), stop=(kc == 7))
                        if i == 0:
                            t.matmul(pd0[:, 16 + mc:17 + mc],
                                     W["wv"][:, kc * D + mc * 128: kc * D + (mc + 1) * 128],
                                     h1bf[:, kc:kc + 1], start=(kc == 0), stop=(kc == 7))
                t.fin(p_v, 1)
                if i == 0:
                    t.fin(p_bqkv, 1)

            emit_u1t(t, 0)
            emit_q(t, 0)
            emit_k(t, 0)
            emit_vp(t, 0)
            for i in range(NT):
                t.wait_ge(v_ve, i + 1)
                t.wait_ge(v_ke, i + 1)
                if i == 0:
                    t.wait_ge(v_bev, 1)
                if i == 6:
                    t.wait_ge(v_mod, 6)
                for b in range(nbatch(i)):
                    gl = batch_groups(i, b)
                    pdx = score_pd(i, b)
                    sb_ = scb(i) + b
                    is_bnd = (len(gl) == 1)
                    pb_ = (b - 1 if i == 0 else b)
                    if b == 0 and i >= 5:
                        t.wait_ge(s_h2, i)
                    if b == 0 and i >= 4:
                        t.wait_ge(s_h1, i + 1)
                    if pdx is pd0:
                        if i == 0 and b == 1:
                            t.wait_ge(v_ext, 1)
                            t.wait_ge(v_rin, 1)
                            t.wait_ge(s_non, 1)
                        if sb_ > 0:
                            t.wait_ge(s_exp, sb_ if i < 6 else sb_ - 1)
                    else:
                        if sb_ >= 2:
                            t.wait_ge(s_exp, sb_ - 1)
                    for kk, gidx in enumerate(gl):
                        t.matmul(pdx[:, kk * 128:(kk + 1) * 128], kg[:, gidx * 128:(gidx + 1) * 128],
                                 qg[:, gidx * 128:(gidx + 1) * 128], start=True, stop=True)
                    t.fin(p_sc, 1)
                    if sb_ >= 2:
                        t.wait_ge(v_vbe, sb_ - 1)
                    if b == 0 and i > 0:
                        t.wait_ge(v_ffn, i)
                    if i == 0 and not is_bnd:
                        t.wait_ge(v_h2b, 1)
                    for kk, gidx in enumerate(gl):
                        if is_bnd:
                            vt_out = pa[:, 256:320]
                        else:
                            vt_out = pa[:, (pb_ % 2) * 512 + 64 * kk:(pb_ % 2) * 512 + 64 * kk + 64]
                        t.matmul(vt_out, vg[0:64, gidx * 128:(gidx + 1) * 128],
                                 idn[0:64, 0:64], start=True, stop=True)
                    t.fin(p_vt, 1)
                    t.wait_ge(s_exp, sb_ + 1)
                    t.wait_ge(v_vbe, sb_ + 1)
                    if sb_ >= 2:
                        t.wait_ge(s_non, sb_ - 1)
                        t.wait_ge(v_rin, sb_ - 1)
                    for kk, gidx in enumerate(gl):
                        if is_bnd:
                            oa_out = pd0[:, 256:321]
                        else:
                            cb_ = (pb_ % 2) * 512 + 65 * kk
                            oa_out = pb[:, cb_:cb_ + 65]
                        t.matmul(oa_out, a2u[:, gidx * 128:(gidx + 1) * 128],
                                 vbs[:, gidx * 72:gidx * 72 + 65], start=True, stop=True)
                    t.fin(p_av, 1)
                    t.wait_ge(s_non, sb_ + 1)
                    if is_bnd:
                        t.matmul(pd0[0:64, 128:256], on_[:, 0:64], idn[:, :], start=True, stop=True)
                    else:
                        if pb_ % 2 == 0:
                            if pb_ == 0:
                                t.wait_ge(v_ke, i + 1)
                                if i > 0:
                                    t.wait_ge(s_kr, i)
                                    t.wait_ge(v_asm, 2 * i)
                            else:
                                t.wait_ge(v_asm, 2 * i + 1)
                        for gidx in gl:
                            s_ = (gidx - 1) % 8
                            t.matmul(pc[0:64, s_ * 128:(s_ + 1) * 128],
                                     on_[:, gidx * 64:(gidx + 1) * 64], idn[:, :],
                                     start=True, stop=True)
                    t.fin(p_ont, 1)
                    if i == 0 and b == 0:
                        t.fin(p_bont, 1)
                        t.wait_ge(v_ext, 1)
                        t.matmul(pa[0:1, 992:993], red_b[:, 0:1], onescol[:, 0:1], start=True, stop=True)
                        t.matmul(pa[0:1, 993:994], red_b[:, 1:2], onescol[:, 0:1], start=True, stop=True)
                        t.fin(p_bst, 1)
                        t.wait_ge(v_ext, 3)
                        t.matmul(pa[:, 994:995], ones1[0:1, :], lb_c[0:1, 1:2], start=True, stop=True)
                        t.matmul(pa[:, 995:996], ones1[0:1, :], lb_c[0:1, 4:5], start=True, stop=True)
                        t.fin(p_bbc, 1)
                t.wait_ge(v_asm, 2 * (i + 1))
                t.wait_ge(v_dga, 1)
                t.wait_ge(s_h1, i + 1)
                if i == 0:
                    t.wait_ge(v_h2b, 1)
                for c in range(NCH):
                    t.matmul(pa[:, c * 128:(c + 1) * 128], fma[:, c * 128:(c + 1) * 128],
                             dga[:, c * 128:(c + 1) * 128], start=True, stop=True)
                t.fin(p_atm, 1)
                if i + 1 < NT:
                    emit_u1t(t, i + 1)
                t.wait_ge(v_u2, i + 1)
                if i >= 4:
                    t.wait_ge(s_h1, i + 1)
                    if i == 4:
                        t.wait_ge(v_mod, 6)
                        t.wait_ge(s_exp, scb(4))
                    else:
                        t.wait_ge(s_h2, i)
                    for c in range(NCH):
                        dstp = pd0 if c < 4 else pd1
                        t.matmul(dstp[:, (c % 4) * 128:(c % 4 + 1) * 128],
                                 u2[:, c * 128:(c + 1) * 128], idn[:, :], start=True, stop=True)
                else:
                    t.wait_ge(v_x2, i + 1)
                    for c in range(NCH):
                        t.matmul(pa[:, c * 128:(c + 1) * 128], u2[:, c * 128:(c + 1) * 128],
                                 idn[:, :], start=True, stop=True)
                t.fin(p_u2t, 1)
                t.wait_ge(v_xrh, i + 1)
                t.wait_ge(ld_w, LW["wr"])
                for hh2 in range(2):
                    for kc in range(NCH):
                        if hh2 == 0 and kc == 4:
                            t.wait_ge(v_xr, i + 1)
                        t.matmul(pa[:, hh2 * 512:hh2 * 512 + 512],
                                 xr[:, kc * 128:(kc + 1) * 128],
                                 W["wr"][:, kc * D + hh2 * 512: kc * D + hh2 * 512 + 512],
                                 start=(kc == 0), stop=(kc == 7))
                t.fin(p_r, 1)
                t.wait_ge(ld_w, LW["wkf"])
                t.wait_ge(v_asm, 2 * (i + 1))
                for mc in range(NCH):
                    for kc in range(NCH):
                        t.matmul(pc[:, mc * 128:(mc + 1) * 128],
                                 W["wkf"][:, kc * D + mc * 128: kc * D + (mc + 1) * 128],
                                 xk[:, kc * 128:(kc + 1) * 128],
                                 start=(kc == 0), stop=(kc == 7))
                t.fin(p_kf, 1)
                if i + 1 < NT:
                    emit_q(t, i + 1)
                    emit_k(t, i + 1)
                t.wait_ge(ld_w, LW["wvf"])
                t.wait_ge(s_krh, i + 1)
                t.wait_ge(s_rth, i + 1)
                for hh2 in range(2):
                    if hh2 == 1:
                        t.wait_ge(s_rt, i + 1)
                    for kc in range(NCH):
                        if hh2 == 0 and kc == 4:
                            t.wait_ge(s_kr, i + 1)
                        t.matmul(pa[:, hh2 * 512:hh2 * 512 + 512],
                                 ksq[:, kc * 128:(kc + 1) * 128],
                                 W["wvf"][:, kc * D + hh2 * 512: kc * D + hh2 * 512 + 512],
                                 start=(kc == 0), stop=(kc == 7))
                t.fin(p_vf, 1)
                if i + 1 < NT:
                    emit_vp(t, i + 1)

        if DEBUG_DUMP:
            dbg_sem = sem("dbg_sem")
            name2sb = dict(vg=vg[0:64, :],
                           h2fv=h2f[:, :].rearrange("p (c w) -> p c w", w=132)[:, :, 0:129],
                           xin0=xin[0], xin1=xin[1], xin2=xin[2], u1=u1, u2=u2,
                           h1f=h1f, dt=dt_, xr=xr, xk=xk, rt=rt,
                           krel=krel, ksq=ksq, ffnf=ffnf, qg=qg, kg=kg,
                           a2u=a2u, vbs=vbs, on_=on_, fma=fma, rinv=rinv,
                           modf=modf, dga=dga, gbT=gbT, sT=sT, h1bf=h1bf,
                           x2b0=x2b_[0], x2b1=x2b_[1], yb0=yb_[0], yb1=yb_[1],
                           rstdb=rstdb, h2bf=h2bf, xbf=xbf, atnb=atnb,
                           x2bf=x2bf, lb_c=lb_c, ln_c=ln_c, ln_s=ln_s)
            dbg_outs = {}
            def _ap(x):
                return x if isinstance(x, bass.AP) else x[:, :]
            for nm in DEBUG_DUMP:
                ap_t = _ap(name2sb[nm])
                dbg_outs[nm] = nc.dram_tensor("dbg_" + nm, list(ap_t.shape), ap_t.dtype,
                                              kind="ExternalOutput").ap()

            @block.sync
            def _(s):
                s.wait_ge(st_y, 16 * NT)
                nd = 0
                for nm in DEBUG_DUMP:
                    s.dma_start(out=dbg_outs[nm], in_=_ap(name2sb[nm])).then_inc(dbg_sem, 16)
                    nd += 1
                s.wait_ge(dbg_sem, 16 * nd)

    return nc


def _host_prep(inputs):
    bf = ml_dtypes.bfloat16
    x = np.ascontiguousarray(np.asarray(inputs["L0_attn_x"], dtype=np.float32))
    B, S, _ = x.shape
    perm = np.concatenate([np.r_[mc * 64:(mc + 1) * 64, (mc + 8) * 64:(mc + 9) * 64]
                           for mc in range(8)])
    wq = np.ascontiguousarray(np.asarray(inputs["L0_attn_Wq"], np.float32)[:, perm])
    wk = np.ascontiguousarray(np.asarray(inputs["L0_attn_Wk"], np.float32)[:, perm])
    wv = np.ascontiguousarray(np.asarray(inputs["L0_attn_Wv"], np.float32)[:, perm])
    bq = np.asarray(inputs["L0_attn_bq"], np.float32)[perm]
    bk = np.asarray(inputs["L0_attn_bk"], np.float32)[perm]
    bv = np.asarray(inputs["L0_attn_bv"], np.float32)[perm]
    wr = np.ascontiguousarray(np.asarray(inputs["L0_ffn_W_r"], np.float32).T)
    wkf = np.ascontiguousarray(np.asarray(inputs["L0_ffn_W_k"], np.float32).T)
    wvf = np.ascontiguousarray(np.asarray(inputs["L0_ffn_W_v"], np.float32).T)
    wmod = np.ascontiguousarray(np.asarray(inputs["L0_wmod"], np.float32))
    cc = np.asarray(inputs["L0_c"], np.float32)
    mur = np.ascontiguousarray(np.asarray(inputs["L0_ffn_mu_r"], np.float32).reshape(8, 128).T)
    muk = np.ascontiguousarray(np.asarray(inputs["L0_ffn_mu_k"], np.float32).reshape(8, 128).T)

    def bias_pat(b):
        p = np.zeros((64, 16), np.float32)
        for mc in range(8):
            p[:, mc] = b[mc * 128: mc * 128 + 64]
            p[:, 8 + mc] = b[mc * 128 + 64: (mc + 1) * 128]
        return p

    consts = np.zeros((128, 16), np.float32)
    for col, val in [(C_EPS, 1e-5), (C_NHALF, -0.5), (C_QSC, 0.125), (C_NSH, -20.0),
                     (C_HALF, 0.5), (C_NEG1, -1.0), (C_RD, 1.0 / 1024.0), (C_ONE, 1.0),
                     (C_QTR, 0.25), (C_NQTR, -0.25), (C_15, 1.5)]:
        consts[:, col] = val
    consts[:, C_QK] = np.full(128, 0x5F3759DF, np.int32).view(np.float32)
    consts[:, C_SH1] = np.full(128, 1, np.int32).view(np.float32)
    idn = np.eye(128, dtype=np.float32).astype(bf)
    qs = np.zeros((64, GSLOT * 128), np.float32)
    ks = np.zeros((64, GSLOT * 128), np.float32)
    col_t = np.arange(GSLOT * 128) % 8
    for tt in range(8):
        qs[tt, col_t == tt] = 1.0
        ks[tt, col_t == tt] = 1024.0
    qs[8, :] = 1.0
    ks[8, :] = -1024.0
    qs = qs.astype(bf)
    ks = ks.astype(bf)
    ones1 = np.ones((1, 128), np.float32)
    ones1b = np.ones((1, 128), np.float32).astype(bf)
    onescol = np.ones((128, 1), np.float32)

    bv_nat = np.asarray(inputs["L0_attn_bv"], np.float32)
    bvt = np.ascontiguousarray(np.repeat(bv_nat.reshape(16, 1, 64), 8, axis=1).reshape(128, 64))
    shared = dict(wq=wq, wk=wk, wv=wv, wr=wr, wkf=wkf, wvf=wvf, wmod=wmod,
                  consts=consts, idn=idn, qstat=qs, kstat=ks, bvt=bvt,
                  bqp=bias_pat(bq), bkp=bias_pat(bk), bvp=bias_pat(bv),
                  mur=mur, muk=muk, ones1=ones1, ones1b=ones1b, onescol=onescol)
    in_maps = []
    for ci in range(8):
        bidx, half = ci // 2, ci % 2
        t0 = half * 2048
        m = dict(shared)
        m["x"] = np.ascontiguousarray(x[bidx, t0:t0 + 2048])
        m["xbnd"] = (np.zeros((1, D), np.float32) if half == 0
                     else np.ascontiguousarray(x[bidx, 2047:2048]))
        m["flag"] = np.full((128, 1), float(half), np.float32)
        m["crow"] = np.ascontiguousarray(cc[bidx:bidx + 1])
        in_maps.append(m)
    return in_maps


def kernel(**inputs):
    in_maps = _host_prep(inputs)
    nc = build_nc()
    res = run_bass_kernel_spmd(nc, in_maps, core_ids=list(range(8)))
    x = np.asarray(inputs["L0_attn_x"])
    B, S, _ = x.shape
    out = np.empty((B, S, D), np.float32)
    for ci in range(8):
        bidx, half = ci // 2, ci % 2
        out[bidx, half * 2048:(half + 1) * 2048] = res.results[ci]["y"]
    return out


# version 17
# speedup vs baseline: 1.1682x; 1.0161x over previous
import numpy as np
import ml_dtypes
import contextlib
import concourse.bass as bass
import concourse.mybir as mybir
from concourse.bass_utils import run_bass_kernel_spmd

F32 = mybir.dt.float32
BF16 = mybir.dt.bfloat16
AF = mybir.ActivationFunctionType
OP = mybir.AluOpType
AX = mybir.AxisListType

D = 1024
NT = 16
TPT = 128
NCH = 8
GSLOT = 17

(C_EPS, C_NHALF, C_QSC, C_NSH, C_HALF, C_NEG1, C_RD, C_ONE, C_QTR,
 C_NQTR, C_QK, C_SH1, C_15, C_X0, C_X1, C_X2) = range(16)

MVNAME = ["sa", "ca", "ga", "sb", "cb", "gb"]

DEBUG_DUMP = []


class _Trk:

    _OPS = {"tensor_copy", "tensor_tensor", "tensor_scalar", "memset",
            "scalar_tensor_tensor", "reciprocal", "reduce_sum", "reduce_max",
            "tensor_reduce", "bn_stats", "bn_aggr", "activation", "copy",
            "add", "mul", "matmul", "transpose"}

    def __init__(self, eng):
        object.__setattr__(self, "_e", eng)
        object.__setattr__(self, "last", None)
        object.__setattr__(self, "_finned", None)
        object.__setattr__(self, "_pend", set())

    @staticmethod
    def _apname(x):
        t = getattr(x, "tensor", None)
        nm = getattr(t, "name", None)
        return nm

    def __getattr__(self, n):
        eng = object.__getattribute__(self, "_e")
        a = getattr(eng, n)
        if not callable(a) or n in ("wait_ge",):
            return a
        trk = self

        def w(*args, **kw):
            if n in _Trk._OPS:
                pend = object.__getattribute__(trk, "_pend")
                names = []
                for x in list(args) + list(kw.values()):
                    nm = _Trk._apname(x)
                    if nm is not None:
                        names.append(nm)
                outs = names[:1]
                if "accum_out" in kw and kw["accum_out"] is not None:
                    nm = _Trk._apname(kw["accum_out"])
                    if nm:
                        outs.append(nm)
                ins = names[1:]
                if any(nm in pend for nm in ins):
                    eng.drain()
                    pend.clear()
                r = a(*args, **kw)
                for nm in outs:
                    pend.add(nm)
            else:
                r = a(*args, **kw)
            if r is not None and hasattr(r, "then_inc"):
                object.__setattr__(trk, "last", r)
            return r
        return w

    def fin(self, sem_, n_=1):
        li = object.__getattribute__(self, "last")
        if li is object.__getattribute__(self, "_finned"):
            object.__getattribute__(self, "_e").drain().then_inc(sem_, n_)
        else:
            li.then_inc(sem_, n_)
            object.__setattr__(self, "_finned", li)



def nbatch(i):
    return 5 if i == 0 else 4


def scb(i):
    return sum(nbatch(j) for j in range(i))


def batch_groups(i, b):
    if i == 0 and b == 0:
        return [0]
    bb = b - 1 if i == 0 else b
    return [1 + 4 * bb + k for k in range(4)]



def _emit_rsqrt(v, consts, out_col, var_ap, eps_bias, scr):
    I32 = mybir.dt.int32
    P = var_ap.shape[0]
    ve = scr[0:P, 0:1]
    t_ = scr[0:P, 1:2]
    v.tensor_scalar(ve, var_ap, eps_bias, None, OP.add)
    v.tensor_tensor(out_col.bitcast(I32), ve.bitcast(I32),
                    consts[0:P, C_SH1:C_SH1 + 1].bitcast(I32), OP.logical_shift_right)
    v.tensor_tensor(out_col.bitcast(I32), consts[0:P, C_QK:C_QK + 1].bitcast(I32),
                    out_col.bitcast(I32), OP.subtract)
    for _ in range(2):
        v.tensor_tensor(t_, out_col, out_col, OP.mult)
        v.tensor_tensor(t_, t_, ve, OP.mult)
        v.tensor_scalar(t_, t_, consts[0:P, C_NHALF:C_NHALF + 1], consts[0:P, C_15:C_15 + 1],
                        OP.mult, OP.add)
        v.tensor_tensor(out_col, out_col, t_, OP.mult)


def build_nc():
    nc = bass.Bass("TRN2", target_bir_lowering=False, detect_race_conditions=False)
    x_h = nc.dram_tensor("x", [NT * TPT, D], F32, kind="ExternalInput").ap()
    xb_h = nc.dram_tensor("xbnd", [1, D], F32, kind="ExternalInput").ap()
    flag_h = nc.dram_tensor("flag", [128, 1], F32, kind="ExternalInput").ap()
    w_h = {}
    for w in ["wq", "wk", "wv", "wr", "wkf", "wvf"]:
        w_h[w] = nc.dram_tensor(w, [D, D], F32, kind="ExternalInput").ap()
    wmod_h = nc.dram_tensor("wmod", [D, 6 * D], F32, kind="ExternalInput").ap()
    crow_h = nc.dram_tensor("crow", [1, D], F32, kind="ExternalInput").ap()
    consts_h = nc.dram_tensor("consts", [128, 16], F32, kind="ExternalInput").ap()
    idn_h = nc.dram_tensor("idn", [128, 128], BF16, kind="ExternalInput").ap()
    qstat_h = nc.dram_tensor("qstat", [64, GSLOT * 128], BF16, kind="ExternalInput").ap()
    kstat_h = nc.dram_tensor("kstat", [64, GSLOT * 128], BF16, kind="ExternalInput").ap()
    bqp_h = nc.dram_tensor("bqp", [64, 16], F32, kind="ExternalInput").ap()
    bkp_h = nc.dram_tensor("bkp", [64, 16], F32, kind="ExternalInput").ap()
    bvp_h = nc.dram_tensor("bvp", [64, 16], F32, kind="ExternalInput").ap()
    bvt_h = nc.dram_tensor("bvt", [128, 64], F32, kind="ExternalInput").ap()
    mur_h = nc.dram_tensor("mur", [128, 8], F32, kind="ExternalInput").ap()
    muk_h = nc.dram_tensor("muk", [128, 8], F32, kind="ExternalInput").ap()
    ones1_h = nc.dram_tensor("ones1", [1, 128], F32, kind="ExternalInput").ap()
    ones1b_h = nc.dram_tensor("ones1b", [1, 128], BF16, kind="ExternalInput").ap()
    onescol_h = nc.dram_tensor("onescol", [128, 1], F32, kind="ExternalInput").ap()
    y_h = nc.dram_tensor("y", [NT * TPT, D], F32, kind="ExternalOutput").ap()

    es = contextlib.ExitStack()
    sbuf = lambda n, s, d: es.enter_context(nc.sbuf_tensor(n, s, d))
    psum = lambda n, s, d: es.enter_context(nc.psum_tensor(n, s, d))
    sem = lambda n: es.enter_context(nc.semaphore(n))
    with es:
        W = {w: sbuf(w + "s", [128, 8 * D], BF16) for w in ["wq", "wk", "wv", "wr", "wkf", "wvf"]}
        wmb = [sbuf(f"wmb{i}", [128, 4 * 512], BF16) for i in range(2)]
        xin = [sbuf(f"xin{i}", [128, D], F32) for i in range(3)]
        x2b_ = [sbuf(f"x2b{i}", [128, D], F32) for i in range(2)]
        yb_ = [sbuf(f"yb{i}", [128, D], F32) for i in range(2)]
        u1 = sbuf("u1", [128, D], BF16)
        u2 = sbuf("u2", [128, D], BF16)
        h1f = sbuf("h1f", [128, D], BF16)
        h2f = sbuf("h2f", [128, 8 * 132], BF16)
        dt_ = sbuf("dt", [128, D], BF16)
        xr = sbuf("xr", [128, D], BF16)
        xk = sbuf("xk", [128, D], BF16)
        rt = sbuf("rt", [128, D], BF16)
        krel = sbuf("krel", [128, D], BF16)
        ksq = sbuf("ksq", [128, D], BF16)
        ffnf = sbuf("ffnf", [128, D], BF16)
        qg = sbuf("qg", [128, GSLOT * 128], BF16)
        kg = sbuf("kg", [128, GSLOT * 128], BF16)
        vg = sbuf("vg", [128, GSLOT * 128], BF16)
        a2u = sbuf("a2u", [128, GSLOT * 128], BF16)
        vbs = sbuf("vbs", [128, GSLOT * 72], BF16)
        on_ = sbuf("on", [128, GSLOT * 64], BF16)
        fma = sbuf("fma", [128, D], BF16)
        rinv = sbuf("rinv", [128, GSLOT], F32)
        idn = sbuf("idn_s", [128, 128], BF16)
        consts = sbuf("consts_s", [128, 16], F32)
        bqp = sbuf("bqp_s", [64, 16], F32)
        bkp = sbuf("bkp_s", [64, 16], F32)
        bvp = sbuf("bvp_s", [64, 16], F32)
        bvt = sbuf("bvt_s", [128, 64], F32)
        murf = sbuf("murf", [128, 8], F32)
        mukf = sbuf("mukf", [128, 8], F32)
        modf = sbuf("modf", [128, 48], F32)
        clsc = sbuf("clsc", [128, 24], F32)
        dga = sbuf("dga", [128, 8 * 128], BF16)
        gbT = sbuf("gbT", [128, D], BF16)
        gbrow = sbuf("gbrow", [1, D], F32)
        gbc = sbuf("gbc", [128, 8], BF16)
        gbT8 = sbuf("gbT8", [8, 128], F32)
        ln_s = sbuf("ln_s", [128, 28], F32)
        ln_c = sbuf("ln_c", [128, 4], F32)
        crow = sbuf("crow_s", [1, D], F32)
        silcb = sbuf("silcb", [1, D], BF16)
        sT = sbuf("sT", [128, 8], BF16)
        modrow = sbuf("modrow", [1, D], F32)
        xbnd = sbuf("xbnd_s", [1, D], F32)
        u1b = sbuf("u1b", [1, D], BF16)
        lb_s = sbuf("lb_s", [1, 14], F32)
        lb_c = sbuf("lb_c", [1, 6], F32)
        h1bf = sbuf("h1bf", [128, 8], BF16)
        xbf = sbuf("xbf", [128, 8], F32)
        atnb = sbuf("atnb", [128, 8], F32)
        x2bf = sbuf("x2bf", [128, 8], F32)
        x2sq = sbuf("x2sq", [128, 8], F32)
        red_b = sbuf("red_b", [128, 2], F32)
        rstdb = sbuf("rstdb", [128, 2], F32)
        h2bf = sbuf("h2bf", [128, 8], BF16)
        flagc = sbuf("flagc", [128, 1], F32)
        ones1 = sbuf("ones1_s", [1, 128], F32)
        ones1b = sbuf("ones1b_s", [1, 128], BF16)
        onescol = sbuf("onescol_s", [128, 1], F32)

        pa = psum("pa", [128, 1024], F32)
        pb = psum("pb", [128, 1024], F32)
        pc = psum("pc", [128, 1024], F32)
        pd0 = psum("pd0", [128, 512], F32)
        pd1 = psum("pd1", [128, 512], F32)

        ld_s, ld_x, ld_w, st_y = sem("ld_s"), sem("ld_x"), sem("ld_w"), sem("st_y")
        p_sT, p_xbf, p_mvh, p_mtr = sem("p_sT"), sem("p_xbf"), sem("p_mvh"), sem("p_mtr")
        p_q, p_k, p_v, p_bqkv = sem("p_q"), sem("p_k"), sem("p_v"), sem("p_bqkv")
        p_sc, p_vt, p_av, p_ont = sem("p_sc"), sem("p_vt"), sem("p_av"), sem("p_ont")
        p_bont, p_bst, p_bbc = sem("p_bont"), sem("p_bst"), sem("p_bbc")
        p_atm, p_u2t, p_u1t = sem("p_atm"), sem("p_u2t"), sem("p_u1t")
        p_r, p_kf, p_vf = sem("p_r"), sem("p_kf"), sem("p_vf")
        g_gbr, p_gbt, v_gbt = sem("g_gbr"), sem("p_gbt"), sem("v_gbt")
        p_gb1, v_gb8 = sem("p_gb1"), sem("v_gb8")
        s_h2h, v_xrh, v_xrf = sem("s_h2h"), sem("v_xrh"), sem("v_xrf")
        s_rth, s_krh = sem("s_rth"), sem("s_krh")
        v_lnb, v_u1b, v_silu, v_sT = sem("v_lnb"), sem("v_u1b"), sem("v_silu"), sem("v_sT")
        v_mve, v_mod, v_dga = sem("v_mve"), sem("v_mod"), sem("v_dga")
        v_xbf, v_h1b = sem("v_xbf"), sem("v_h1b")
        v_ln1, v_u1, v_qe, v_ke, v_ve = sem("v_ln1"), sem("v_u1"), sem("v_qe"), sem("v_ke"), sem("v_ve")
        v_vbe, v_rin, v_asm, v_x2 = sem("v_vbe"), sem("v_rin"), sem("v_asm"), sem("v_x2")
        v_ln2, v_u2, v_xr, v_ffn, v_y = sem("v_ln2"), sem("v_u2"), sem("v_xr"), sem("v_ffn"), sem("v_y")
        v_bev, v_ext, v_h2b = sem("v_bev"), sem("v_ext"), sem("v_h2b")
        s_silu, s_lb1, s_lb2, s_ln = sem("s_silu"), sem("s_lb1"), sem("s_lb2"), sem("s_ln")
        s_h1, s_exp, s_non, s_h2 = sem("s_h1"), sem("s_exp"), sem("s_non"), sem("s_h2")
        s_rt, s_kr, s_asm = sem("s_rt"), sem("s_kr"), sem("s_asm")

        block = es.enter_context(nc.Block())

        def score_pd(i, b):
            if i < 6 or (i == 0 and b == 0):
                return pd0
            return pd0 if b % 2 == 0 else pd1

        def mv_loads(nm):
            return [nm + jh + h for jh in "01" for h in "01"]
        dma_order = (mv_loads("sa") + mv_loads("ca") + ["wq"] + mv_loads("ga")
                     + ["wk", "wv"] + mv_loads("sb") + mv_loads("cb")
                     + ["wr", "wkf", "wvf"] + mv_loads("gb"))
        LW = {nm: 16 * (k + 1) for k, nm in enumerate(dma_order)}

        @block.scalar
        def _(s):
            for dst, src in [(consts[:], consts_h), (flagc[:], flag_h),
                             (crow[:], crow_h), (xbnd[:], xb_h),
                             (bqp[:], bqp_h), (bkp[:], bkp_h), (bvp[:], bvp_h), (bvt[:], bvt_h),
                             (murf[:], mur_h), (mukf[:], muk_h),
                             (idn[:], idn_h), (ones1[:], ones1_h),
                             (ones1b[:], ones1b_h), (onescol[:], onescol_h),
                             (qg[64:128, :], qstat_h), (kg[64:128, :], kstat_h)]:
                s.dma_start(out=dst, in_=src).then_inc(ld_s, 16)
        NLD_S = 16

        @block.sync
        def _(s):
            for i in range(min(3, NT)):
                s.dma_start(out=xin[i % 3][:], in_=x_h[i * TPT:(i + 1) * TPT, :]).then_inc(ld_x, 16)
            for i in range(NT):
                s.wait_ge(v_y, i + 1)
                s.dma_start(out=y_h[i * TPT:(i + 1) * TPT, :], in_=yb_[i % 2][:]).then_inc(st_y, 16)
                j = i + 3
                if j < NT:
                    s.dma_start(out=xin[j % 3][:], in_=x_h[j * TPT:(j + 1) * TPT, :]).then_inc(ld_x, 16)
            s.wait_ge(st_y, 16 * NT)

        @block.gpsimd
        def _(g):
            for nm in dma_order:
                if nm in W:
                    g.dma_start(out=W[nm][:, :].rearrange("p (kc d) -> p kc d", d=D),
                                in_=w_h[nm].rearrange("(kc p) d -> p kc d", p=128)).then_inc(ld_w, 16)
                else:
                    v = MVNAME.index(nm[:2])
                    jh, h = int(nm[2]), int(nm[3])
                    ell = 4 * v + 2 * jh + h
                    if ell >= 2:
                        g.wait_ge(p_mvh, 2 * v + jh)
                    src = wmod_h[h * 512:(h + 1) * 512, v * D + jh * 512: v * D + jh * 512 + 512]
                    g.dma_start(out=wmb[h][:, :].rearrange("p (ic j) -> p ic j", j=512),
                                in_=src.rearrange("(ic p) j -> p ic j", p=128)).then_inc(ld_w, 16)
            g.wait_ge(v_gb8, 1)
            for c in range(NCH):
                g.dma_start(out=gbrow[0:1, c * 128:(c + 1) * 128],
                            in_=gbT8[c:c + 1, :]).then_inc(g_gbr, 16)

        @block.scalar
        def _(s):
            s = _Trk(s)
            s.wait_ge(ld_s, 16 * NLD_S)
            s.activation(modrow[:], crow[:], AF.Silu).then_inc(s_silu, 1)

            def emit_h1(s, i):
                s.wait_ge(p_u1t, i + 1)
                s.wait_ge(v_mod, 2)
                for c in range(NCH):
                    srcp = (pa if i == 0 else pb)[:, c * 128:(c + 1) * 128]
                    s.activation(h1f[:, c * 128:(c + 1) * 128], srcp,
                                 AF.Identity, scale=modf[:, 8 + c:9 + c], bias=modf[:, c:c + 1])
                s.fin(s_h1, 1)

            emit_h1(s, 0)
            for i in range(NT):
                for b in range(nbatch(i)):
                    gl = batch_groups(i, b)
                    s.wait_ge(p_sc, scb(i) + b + 1)
                    pdx = score_pd(i, b)
                    ncols = 128 * len(gl)
                    s.activation(a2u[:, gl[0] * 128: gl[0] * 128 + ncols], pdx[:, 0:ncols],
                                 AF.Exp, scale=consts[:, C_QSC:C_QSC + 1],
                                 bias=consts[:, C_NSH:C_NSH + 1]).then_inc(s_exp, 1)
                    s.wait_ge(p_av, scb(i) + b + 1)
                    s.wait_ge(v_rin, scb(i) + b + 1)
                    for gidx in gl:
                        if gidx == 0:
                            src = pd0[:, 256:320]
                        else:
                            pb_ = (gidx - 1) // 4
                            k = (gidx - 1) % 4
                            cb = (pb_ % 2) * 512 + k * 65
                            src = pb[:, cb:cb + 64]
                        s.activation(on_[:, gidx * 64:(gidx + 1) * 64], src,
                                     AF.Identity, scale=rinv[:, gidx:gidx + 1])
                    s.fin(s_non, 1)
                s.wait_ge(p_u2t, i + 1)
                s.wait_ge(v_mod, 5)
                for c in range(NCH):
                    if i >= 4:
                        srcp = (pd0 if c < 4 else pd1)[:, (c % 4) * 128:(c % 4 + 1) * 128]
                    else:
                        srcp = pa[:, c * 128:(c + 1) * 128]
                    s.activation(h2f[:, c * 132 + 1: c * 132 + 129], srcp,
                                 AF.Identity, scale=modf[:, 32 + c:33 + c], bias=modf[:, 24 + c:25 + c])
                    if c == 3:
                        s.fin(s_h2h, 1)
                s.fin(s_h2, 1)
                if i + 1 < NT:
                    emit_h1(s, i + 1)
                s.wait_ge(p_r, i + 1)
                s.activation(rt[:, 0:512], pa[:, 0:512], AF.Tanh, scale=consts[:, C_HALF:C_HALF + 1])
                s.fin(s_rth, 1)
                s.activation(rt[:, 512:1024], pa[:, 512:1024], AF.Tanh,
                             scale=consts[:, C_HALF:C_HALF + 1]).then_inc(s_rt, 1)
                s.wait_ge(p_kf, i + 1)
                s.activation(krel[:, 0:512], pc[:, 0:512], AF.Relu)
                s.activation(ksq[:, 0:512], krel[:, 0:512], AF.Square)
                s.fin(s_krh, 1)
                s.activation(krel[:, 512:1024], pc[:, 512:1024], AF.Relu)
                s.activation(ksq[:, 512:1024], krel[:, 512:1024], AF.Square).then_inc(s_kr, 1)

        @block.vector
        def _(v):
            v = _Trk(v)
            v.wait_ge(ld_s, 16 * NLD_S)
            v.memset(qg[0:64, 0:128], 0.0)
            v.memset(kg[0:64, 0:128], 0.0)
            v.memset(vg[0:64, 0:128], 0.0)
            v.memset(vbs[:, :], 0.0)
            v.memset(vbs[:, :].rearrange("p (s c) -> p s c", c=72)[:, :, 64:65], 1.0)
            v.memset(h2f[:, :].rearrange("p (s c) -> p s c", c=132)[:, :, 0:1], 0.0)
            v.bn_stats(lb_s[:, 0:6], xbnd[0:1, 0:512])
            v.bn_stats(lb_s[:, 6:12], xbnd[0:1, 512:1024])
            v.bn_aggr(lb_s[:, 12:14], lb_s[:, 0:12]).then_inc(v_lnb, 1)
            _emit_rsqrt(v, consts, lb_c[0:1, 0:1], lb_s[0:1, 13:14],
                        consts[0:1, C_EPS:C_EPS + 1], clsc)
            v.tensor_scalar(lb_c[0:1, 5:6], lb_s[0:1, 12:13], lb_c[0:1, 0:1],
                            consts[0:1, C_NEG1:C_NEG1 + 1], OP.mult, OP.mult)
            v.tensor_scalar(u1b[:], xbnd[:], lb_c[0:1, 0:1], lb_c[0:1, 5:6],
                            OP.mult, OP.add).then_inc(v_u1b, 1)
            v.wait_ge(s_silu, 1)
            v.tensor_copy(silcb[:], modrow[:]).then_inc(v_silu, 1)
            v.wait_ge(p_sT, 1)
            v.tensor_copy(sT[:, :], pd0[:, 480:488]).then_inc(v_sT, 1)
            v.wait_ge(p_xbf, 1)
            v.tensor_copy(xbf[:, :], pa[:, 1000:1008]).then_inc(v_xbf, 1)
            for mv in range(6):
                for jh in range(2):
                    v.wait_ge(p_mvh, 2 * mv + jh + 1)
                    v.tensor_copy(modrow[0:1, jh * 512:(jh + 1) * 512], pd1[0:1, 0:512]).then_inc(v_mve, 1)
                v.wait_ge(p_mtr, mv + 1)
                v.tensor_copy(modf[:, 8 * mv:8 * mv + 8], pd1[:, 384:392])
                if mv in (1, 4):
                    v.tensor_scalar(modf[:, 8 * mv:8 * mv + 8], modf[:, 8 * mv:8 * mv + 8],
                                    consts[:, C_ONE:C_ONE + 1], None, OP.add)
                if mv in (2, 5):
                    cf = 8 * mv
                    gsl = modf[:, cf:cf + 8]
                    v.tensor_scalar(clsc[:, 0:8], gsl, 0.0, None, OP.is_ge)
                    v.tensor_scalar(clsc[:, 8:16], gsl, consts[:, C_QTR:C_QTR + 1], None, OP.max)
                    v.tensor_scalar(clsc[:, 16:24], gsl, consts[:, C_NQTR:C_NQTR + 1], None, OP.min)
                    v.tensor_tensor(clsc[:, 8:16], clsc[:, 8:16], clsc[:, 16:24], OP.subtract)
                    v.tensor_tensor(clsc[:, 8:16], clsc[:, 8:16], clsc[:, 0:8], OP.mult)
                    v.tensor_tensor(gsl, clsc[:, 8:16], clsc[:, 16:24], OP.add)
                    if mv == 5:
                        v.tensor_scalar(gsl, gsl, consts[:, C_HALF:C_HALF + 1], None, OP.mult)
                    if mv == 2:
                        for c in range(NCH):
                            v.tensor_scalar(dga[:, c * 128:(c + 1) * 128], idn[:, :],
                                            modf[:, cf + c:cf + c + 1], None, OP.mult)
                    else:
                        v.tensor_copy(gbc[:, :], gsl)
                    v.fin(v_dga, 1)
                v.fin(v_mod, 1)
                if mv == 1:
                    v.tensor_tensor(x2sq[:, 0:8], pa[:, 1008:1016], modf[:, 8:16], OP.mult)
                    v.tensor_tensor(x2sq[:, 0:8], x2sq[:, 0:8], modf[:, 0:8], OP.add)
                    v.tensor_copy(h1bf[:, :], x2sq[:, 0:8]).then_inc(v_h1b, 1)
            def emit_ln1_stats(v, i):
                v.wait_ge(ld_x, 16 * (i + 1))
                v.bn_stats(ln_s[:, 0:6], xin[i % 3][:, 0:512])
                v.bn_stats(ln_s[:, 6:12], xin[i % 3][:, 512:1024])
                v.bn_aggr(ln_s[:, 24:26], ln_s[:, 0:12]).then_inc(v_ln1, 1)

            def emit_ln1_fin(v, i):
                v.wait_ge(p_u1t, i)
                _emit_rsqrt(v, consts, ln_c[:, 0:1], ln_s[:, 25:26],
                            consts[:, C_EPS:C_EPS + 1], clsc)
                v.tensor_scalar(ln_c[:, 2:3], ln_s[:, 24:25], ln_c[:, 0:1],
                                consts[:, C_NEG1:C_NEG1 + 1], OP.mult, OP.mult)
                v.tensor_scalar(u1[:], xin[i % 3][:], ln_c[:, 0:1], ln_c[:, 2:3],
                                OP.mult, OP.add).then_inc(v_u1, 1)

            def emit_ln1(v, i):
                emit_ln1_stats(v, i)
                emit_ln1_fin(v, i)

            v.wait_ge(p_gb1, 1)
            v.tensor_copy(gbT8[:, :], pd1[0:8, 384:512])
            v.fin(v_gb8, 1)
            v.wait_ge(p_gbt, 1)
            v.tensor_copy(gbT[:, 0:512], pd0[:, 0:512])
            v.tensor_copy(gbT[:, 512:1024], pd1[:, 0:512])
            v.fin(v_gbt, 1)

            def emit_evac(v, i, which):
                psrc, gdst, bpat, vsem, pwait = {
                    "q": (pb, qg, bqp, v_qe, p_q), "k": (pc, kg, bkp, v_ke, p_k),
                    "v": (pb, vg, None, v_ve, p_v)}[which]
                v.wait_ge(pwait, i + 1)
                for par in range(2):
                    in3 = psrc[64 * par:64 * par + 64, :].rearrange(
                        "p (mc g t) -> p mc g t", g=16, t=8)
                    out3 = gdst[0:64, :].rearrange("p (g jj t) -> p g jj t", jj=16, t=8)[
                        :, 1:GSLOT, 8 * par:8 * par + 8, :].rearrange("p g j t -> p j g t")
                    if bpat is None:
                        v.tensor_copy(out3, in3)
                    else:
                        bias = bpat[:, 8 * par:8 * par + 8].unsqueeze(2).unsqueeze(3).broadcast_to([64, 8, 16, 8])
                        v.tensor_tensor(out3, in3, bias, OP.add)
                v.fin(vsem, 1)

            emit_ln1(v, 0)
            emit_evac(v, 0, "q")
            emit_evac(v, 0, "k")
            for i in range(NT):
                emit_evac(v, i, "v")
                if i == 0:
                    v.wait_ge(p_bqkv, 1)
                    for (tix, gdst, bpat) in [(0, qg, bqp), (1, kg, bkp)]:
                        for par in range(2):
                            in2 = pd0[64 * par:64 * par + 64, 8 * tix:8 * tix + 8]
                            out2 = gdst[0:64, :].rearrange("p (g jj t) -> p g jj t", jj=16, t=8)[
                                :, 0, 8 * par:8 * par + 8, 0]
                            v.tensor_tensor(out2, in2, bpat[:, 8 * par:8 * par + 8], OP.add)
                    for par in range(2):
                        in2 = pd0[64 * par:64 * par + 64, 16:24]
                        out2 = vg[0:64, :].rearrange("p (g jj t) -> p g jj t", jj=16, t=8)[
                            :, 0, 8 * par:8 * par + 8, 0]
                        v.tensor_copy(out2, in2)
                    v.fin(v_bev, 1)
                for b in range(nbatch(i)):
                    gl = batch_groups(i, b)
                    v.wait_ge(p_vt, scb(i) + b + 1)
                    if len(gl) == 1:
                        v.tensor_tensor(vbs[:, 0:64], pa[:, 256:320], bvt[:, :], OP.add)
                    else:
                        pb_ = b - 1 if i == 0 else b
                        base = (pb_ % 2) * 512
                        dstv = vbs[:, :].rearrange("p (s c) -> p s c", c=72)[:, gl[0]:gl[0] + 4, 0:64]
                        srcv = pa[:, base:base + 256].rearrange("p (kk c) -> p kk c", c=64)
                        bias4 = bvt[:, :].unsqueeze(1).broadcast_to([128, 4, 64])
                        v.tensor_tensor(dstv, srcv, bias4, OP.add)
                    v.fin(v_vbe, 1)
                    v.wait_ge(p_av, scb(i) + b + 1)
                    if len(gl) == 1:
                        v.reciprocal(rinv[:, 0:1], pd0[:, 320:321])
                    else:
                        pb_ = b - 1 if i == 0 else b
                        base = (pb_ % 2) * 512 + 64
                        src_s = pb[:, base - 64:base - 64 + 260].rearrange("p (s c) -> p s c", c=65)[:, :, 64]
                        v.reciprocal(rinv[:, gl[0]:gl[0] + 4], src_s)
                    v.fin(v_rin, 1)
                    pb_h = b - 1 if i == 0 else b
                    if pb_h in (1, 3) and len(gl) == 4:
                        hh = pb_h // 2
                        bofs = 1 if i == 0 else 0
                        v.wait_ge(p_ont, scb(i) + bofs + 2 * (hh + 1))
                        for par in range(2):
                            srca = pc[0:64, :].rearrange(
                                "p (g c2 pr t) -> p c2 pr g t", t=8, pr=2, c2=8)[:, :, par, :, :]
                            dsta = fma[64 * par:64 * par + 64, :].rearrange(
                                "p (c h2 g t) -> p c h2 g t", h2=2, g=8, t=8)[:, :, hh, :, :]
                            v.tensor_copy(dsta, srca)
                        v.fin(v_asm, 1)
                        if pb_h == 1 and i + 1 < NT:
                            emit_ln1_stats(v, i + 1)
                        if pb_h == 3 and i + 1 < NT:
                            emit_ln1_fin(v, i + 1)
                    if i == 0 and b == 0:
                        v.wait_ge(p_bont, 1)
                        for par in range(2):
                            src2 = pd0[0:64, 128:256].rearrange("p (c2 pr t) -> p c2 pr t", pr=2, t=8)[:, :, par, 0]
                            v.tensor_copy(atnb[64 * par:64 * par + 64, 0:8], src2)
                        v.wait_ge(v_dga, 1)
                        v.tensor_tensor(atnb[:], atnb[:], modf[:, 16:24], OP.mult)
                        v.wait_ge(v_xbf, 1)
                        v.tensor_tensor(x2bf[:], xbf[:], atnb[:], OP.add)
                        v.tensor_tensor(x2sq[:], x2bf[:], x2bf[:], OP.mult)
                        v.reduce_sum(red_b[:, 0:1], x2bf[:], axis=AX.X)
                        v.reduce_sum(red_b[:, 1:2], x2sq[:], axis=AX.X)
                        v.fin(v_ext, 1)
                        v.wait_ge(p_bst, 1)
                        v.tensor_scalar(lb_c[0:1, 2:3], pa[0:1, 992:993], consts[0:1, C_RD:C_RD + 1],
                                        None, OP.mult)
                        v.tensor_tensor(lb_c[0:1, 3:4], lb_c[0:1, 2:3], lb_c[0:1, 2:3], OP.mult)
                        v.tensor_scalar(lb_c[0:1, 3:4], lb_c[0:1, 3:4], consts[0:1, C_NEG1:C_NEG1 + 1],
                                        consts[0:1, C_EPS:C_EPS + 1], OP.mult, OP.add)
                        v.tensor_scalar(lb_c[0:1, 4:5], pa[0:1, 993:994],
                                        consts[0:1, C_RD:C_RD + 1], None, OP.mult)
                        v.tensor_tensor(lb_c[0:1, 4:5], lb_c[0:1, 4:5], lb_c[0:1, 3:4], OP.add)
                        _emit_rsqrt(v, consts, lb_c[0:1, 1:2], lb_c[0:1, 4:5], 0.0, clsc)
                        v.fin(v_ext, 1)
                        v.tensor_scalar(lb_c[0:1, 4:5], lb_c[0:1, 2:3], lb_c[0:1, 1:2],
                                        consts[0:1, C_NEG1:C_NEG1 + 1], OP.mult, OP.mult)
                        v.fin(v_ext, 1)
                        v.wait_ge(p_bbc, 1)
                        v.tensor_copy(rstdb[:, 0:2], pa[:, 994:996])
                        v.tensor_scalar(x2bf[:], x2bf[:], rstdb[:, 0:1], rstdb[:, 1:2], OP.mult, OP.add)
                        v.tensor_tensor(x2bf[:], x2bf[:], modf[:, 32:40], OP.mult)
                        v.tensor_tensor(x2bf[:], x2bf[:], modf[:, 24:32], OP.add)
                        v.tensor_scalar(h2bf[:], x2bf[:], flagc[:, 0:1], None, OP.mult)
                        v.tensor_copy(h2f[:, :].rearrange("p (c w) -> p c w", w=132)[:, :, 0:1],
                                      h2bf[:, :].unsqueeze(2))
                        v.fin(v_h2b, 1)
                v.wait_ge(p_atm, i + 1)
                v.tensor_tensor(x2b_[i % 2][:], xin[i % 3][:], pa[:, :], OP.add).then_inc(v_x2, 1)
                v.bn_stats(ln_s[:, 12:18], x2b_[i % 2][:, 0:512])
                v.bn_stats(ln_s[:, 18:24], x2b_[i % 2][:, 512:1024])
                v.bn_aggr(ln_s[:, 26:28], ln_s[:, 12:24]).then_inc(v_ln2, 1)
                _emit_rsqrt(v, consts, ln_c[:, 1:2], ln_s[:, 27:28],
                            consts[:, C_EPS:C_EPS + 1], clsc)
                v.tensor_scalar(ln_c[:, 2:3], ln_s[:, 26:27], ln_c[:, 1:2],
                                consts[:, C_NEG1:C_NEG1 + 1], OP.mult, OP.mult)
                v.tensor_scalar(u2[:], x2b_[i % 2][:], ln_c[:, 1:2], ln_c[:, 2:3],
                                OP.mult, OP.add).then_inc(v_u2, 1)
                dt3 = dt_[:, :].rearrange("p (c t) -> p c t", t=128)
                h2s = h2f[:, :].rearrange("p (c w) -> p c w", w=132)
                v.wait_ge(s_h2h, i + 1)
                if i == 0:
                    v.wait_ge(v_h2b, 1)
                v.tensor_tensor(dt3[:, 0:4, :], h2s[:, 0:4, 1:129], h2s[:, 0:4, 0:128], OP.subtract)
                for c in range(4):
                    v.scalar_tensor_tensor(xr[:, c * 128:(c + 1) * 128], dt_[:, c * 128:(c + 1) * 128],
                                           murf[:, c:c + 1], h2f[:, c * 132:c * 132 + 128],
                                           OP.mult, OP.add)
                v.fin(v_xrh, 1)
                v.wait_ge(s_h2, i + 1)
                v.tensor_tensor(dt3[:, 4:8, :], h2s[:, 4:8, 1:129], h2s[:, 4:8, 0:128], OP.subtract)
                for c in range(4, NCH):
                    v.scalar_tensor_tensor(xr[:, c * 128:(c + 1) * 128], dt_[:, c * 128:(c + 1) * 128],
                                           murf[:, c:c + 1], h2f[:, c * 132:c * 132 + 128],
                                           OP.mult, OP.add)
                v.fin(v_xrf, 1)
                for c in range(NCH):
                    v.scalar_tensor_tensor(xk[:, c * 128:(c + 1) * 128], dt_[:, c * 128:(c + 1) * 128],
                                           mukf[:, c:c + 1], h2f[:, c * 132:c * 132 + 128],
                                           OP.mult, OP.add)
                if i < NT - 1:
                    v.tensor_copy(h2f[:, :].rearrange("p (c w) -> p c w", w=132)[:, :, 0:1],
                                  h2f[:, :].rearrange("p (c w) -> p c w", w=132)[:, :, 128:129])
                v.fin(v_xr, 1)
                if i + 1 < NT:
                    emit_evac(v, i + 1, "q")
                    emit_evac(v, i + 1, "k")
                v.wait_ge(p_vf, i + 1)
                v.scalar_tensor_tensor(ffnf[:], rt[:], 1.0, pa[:, :], OP.add, OP.mult).then_inc(v_ffn, 1)
                if i >= 2:
                    v.wait_ge(st_y, 16 * (i - 1))
                v.tensor_tensor(yb_[i % 2][:], ffnf[:], gbT[:, :], OP.mult)
                v.tensor_tensor(yb_[i % 2][:], yb_[i % 2][:], x2b_[i % 2][:], OP.add).then_inc(v_y, 1)

        @block.tensor
        def _(t):
            t = _Trk(t)
            t.wait_ge(v_silu, 1)
            for c in range(NCH):
                t.matmul(pd0[:, 480 + c:481 + c], silcb[0:1, c * 128:(c + 1) * 128],
                         ones1b[0:1, 0:1], start=True, stop=True)
            t.fin(p_sT, 1)
            for c in range(NCH):
                t.matmul(pa[:, 1000 + c:1001 + c], xbnd[0:1, c * 128:(c + 1) * 128],
                         ones1[0:1, 0:1], start=True, stop=True)
            t.wait_ge(v_u1b, 1)
            for c in range(NCH):
                t.matmul(pa[:, 1008 + c:1009 + c], u1b[0:1, c * 128:(c + 1) * 128],
                         ones1b[0:1, 0:1], start=True, stop=True)
            t.fin(p_xbf, 1)
            t.wait_ge(v_sT, 1)
            for mv in range(6):
                if mv > 0:
                    t.wait_ge(v_mod, mv)
                for jh in range(2):
                    t.wait_ge(ld_w, LW[MVNAME[mv] + str(jh) + "1"])
                    if jh == 1:
                        t.wait_ge(v_mve, 2 * mv + 1)
                    for ic in range(8):
                        t.matmul(pd1[0:1, 0:512], sT[:, ic:ic + 1],
                                 wmb[ic // 4][:, (ic % 4) * 512:(ic % 4) * 512 + 512],
                                 start=(ic == 0), stop=(ic == 7))
                    t.fin(p_mvh, 1)
                t.wait_ge(v_mve, 2 * mv + 2)
                for c in range(NCH):
                    t.matmul(pd1[:, 384 + c:385 + c], modrow[0:1, c * 128:(c + 1) * 128],
                             ones1[0:1, 0:1], start=True, stop=True)
                t.fin(p_mtr, 1)
            t.wait_ge(v_dga, 2)
            t.matmul(pd1[0:8, 384:512], gbc[:, :], idn[:, :], start=True, stop=True)
            t.fin(p_gb1, 1)
            t.wait_ge(g_gbr, 16 * 8)
            t.matmul(pd0[:, 0:512], ones1[0:1, :], gbrow[0:1, 0:512], start=True, stop=True)
            t.matmul(pd1[:, 0:512], ones1[0:1, :], gbrow[0:1, 512:1024], start=True, stop=True)
            t.fin(p_gbt, 1)
            t.wait_ge(v_gbt, 1)
            def emit_u1t(t, i):
                t.wait_ge(v_u1, i + 1)
                if i == 0:
                    t.wait_ge(v_h1b, 1)
                    t.wait_ge(v_xbf, 1)
                    for c in range(NCH):
                        t.matmul(pa[:, c * 128:(c + 1) * 128], u1[:, c * 128:(c + 1) * 128],
                                 idn[:, :], start=True, stop=True)
                else:
                    t.wait_ge(v_rin, scb(i))
                    t.wait_ge(s_non, scb(i))
                    t.wait_ge(v_vbe, scb(i))
                    for c in range(NCH):
                        t.matmul(pb[:, c * 128:(c + 1) * 128], u1[:, c * 128:(c + 1) * 128],
                                 idn[:, :], start=True, stop=True)
                t.fin(p_u1t, 1)

            def emit_q(t, i):
                t.wait_ge(s_h1, i + 1)
                t.wait_ge(ld_w, LW["wq"])
                for mc in range(NCH):
                    for kc in range(NCH):
                        t.matmul(pb[:, mc * 128:(mc + 1) * 128],
                                 W["wq"][:, kc * D + mc * 128: kc * D + (mc + 1) * 128],
                                 h1f[:, kc * 128:(kc + 1) * 128],
                                 start=(kc == 0), stop=(kc == 7))
                        if i == 0:
                            t.matmul(pd0[:, 0 + mc:1 + mc],
                                     W["wq"][:, kc * D + mc * 128: kc * D + (mc + 1) * 128],
                                     h1bf[:, kc:kc + 1], start=(kc == 0), stop=(kc == 7))
                t.fin(p_q, 1)

            def emit_k(t, i):
                t.wait_ge(ld_w, LW["wk"])
                if i > 0:
                    t.wait_ge(s_kr, i)
                for mc in range(NCH):
                    for kc in range(NCH):
                        t.matmul(pc[:, mc * 128:(mc + 1) * 128],
                                 W["wk"][:, kc * D + mc * 128: kc * D + (mc + 1) * 128],
                                 h1f[:, kc * 128:(kc + 1) * 128],
                                 start=(kc == 0), stop=(kc == 7))
                        if i == 0:
                            t.matmul(pd0[:, 8 + mc:9 + mc],
                                     W["wk"][:, kc * D + mc * 128: kc * D + (mc + 1) * 128],
                                     h1bf[:, kc:kc + 1], start=(kc == 0), stop=(kc == 7))
                t.fin(p_k, 1)

            def emit_vp(t, i):
                t.wait_ge(ld_w, LW["wv"])
                t.wait_ge(v_qe, i + 1)
                for mc in range(NCH):
                    for kc in range(NCH):
                        t.matmul(pb[:, mc * 128:(mc + 1) * 128],
                                 W["wv"][:, kc * D + mc * 128: kc * D + (mc + 1) * 128],
                                 h1f[:, kc * 128:(kc + 1) * 128],
                                 start=(kc == 0), stop=(kc == 7))
                        if i == 0:
                            t.matmul(pd0[:, 16 + mc:17 + mc],
                                     W["wv"][:, kc * D + mc * 128: kc * D + (mc + 1) * 128],
                                     h1bf[:, kc:kc + 1], start=(kc == 0), stop=(kc == 7))
                t.fin(p_v, 1)
                if i == 0:
                    t.fin(p_bqkv, 1)

            emit_u1t(t, 0)
            emit_q(t, 0)
            emit_k(t, 0)
            emit_vp(t, 0)
            for i in range(NT):
                t.wait_ge(v_ve, i + 1)
                t.wait_ge(v_ke, i + 1)
                if i == 0:
                    t.wait_ge(v_bev, 1)
                if i == 6:
                    t.wait_ge(v_mod, 6)
                for b in range(nbatch(i)):
                    gl = batch_groups(i, b)
                    pdx = score_pd(i, b)
                    sb_ = scb(i) + b
                    is_bnd = (len(gl) == 1)
                    pb_ = (b - 1 if i == 0 else b)
                    if b == 0 and i >= 5:
                        t.wait_ge(s_h2, i)
                    if b == 0 and i >= 4:
                        t.wait_ge(s_h1, i + 1)
                    if pdx is pd0:
                        if i == 0 and b == 1:
                            t.wait_ge(v_ext, 1)
                            t.wait_ge(v_rin, 1)
                            t.wait_ge(s_non, 1)
                        if sb_ > 0:
                            t.wait_ge(s_exp, sb_ if i < 6 else sb_ - 1)
                    else:
                        if sb_ >= 2:
                            t.wait_ge(s_exp, sb_ - 1)
                    for kk, gidx in enumerate(gl):
                        t.matmul(pdx[:, kk * 128:(kk + 1) * 128], kg[:, gidx * 128:(gidx + 1) * 128],
                                 qg[:, gidx * 128:(gidx + 1) * 128], start=True, stop=True)
                    t.fin(p_sc, 1)
                    if sb_ >= 2:
                        t.wait_ge(v_vbe, sb_ - 1)
                    if b == 0 and i > 0:
                        t.wait_ge(v_ffn, i)
                    if i == 0 and not is_bnd:
                        t.wait_ge(v_h2b, 1)
                    for kk, gidx in enumerate(gl):
                        if is_bnd:
                            vt_out = pa[:, 256:320]
                        else:
                            vt_out = pa[:, (pb_ % 2) * 512 + 64 * kk:(pb_ % 2) * 512 + 64 * kk + 64]
                        t.matmul(vt_out, vg[0:64, gidx * 128:(gidx + 1) * 128],
                                 idn[0:64, 0:64], start=True, stop=True)
                    t.fin(p_vt, 1)
                    t.wait_ge(s_exp, sb_ + 1)
                    t.wait_ge(v_vbe, sb_ + 1)
                    if sb_ >= 2:
                        t.wait_ge(s_non, sb_ - 1)
                        t.wait_ge(v_rin, sb_ - 1)
                    for kk, gidx in enumerate(gl):
                        if is_bnd:
                            oa_out = pd0[:, 256:321]
                        else:
                            cb_ = (pb_ % 2) * 512 + 65 * kk
                            oa_out = pb[:, cb_:cb_ + 65]
                        t.matmul(oa_out, a2u[:, gidx * 128:(gidx + 1) * 128],
                                 vbs[:, gidx * 72:gidx * 72 + 65], start=True, stop=True)
                    t.fin(p_av, 1)
                    t.wait_ge(s_non, sb_ + 1)
                    if is_bnd:
                        t.matmul(pd0[0:64, 128:256], on_[:, 0:64], idn[:, :], start=True, stop=True)
                    else:
                        if pb_ % 2 == 0:
                            if pb_ == 0:
                                t.wait_ge(v_ke, i + 1)
                                if i > 0:
                                    t.wait_ge(s_kr, i)
                                    t.wait_ge(v_asm, 2 * i)
                            else:
                                t.wait_ge(v_asm, 2 * i + 1)
                        for gidx in gl:
                            s_ = (gidx - 1) % 8
                            t.matmul(pc[0:64, s_ * 128:(s_ + 1) * 128],
                                     on_[:, gidx * 64:(gidx + 1) * 64], idn[:, :],
                                     start=True, stop=True)
                    t.fin(p_ont, 1)
                    if i == 0 and b == 0:
                        t.fin(p_bont, 1)
                        t.wait_ge(v_ext, 1)
                        t.matmul(pa[0:1, 992:993], red_b[:, 0:1], onescol[:, 0:1], start=True, stop=True)
                        t.matmul(pa[0:1, 993:994], red_b[:, 1:2], onescol[:, 0:1], start=True, stop=True)
                        t.fin(p_bst, 1)
                        t.wait_ge(v_ext, 3)
                        t.matmul(pa[:, 994:995], ones1[0:1, :], lb_c[0:1, 1:2], start=True, stop=True)
                        t.matmul(pa[:, 995:996], ones1[0:1, :], lb_c[0:1, 4:5], start=True, stop=True)
                        t.fin(p_bbc, 1)
                t.wait_ge(v_asm, 2 * (i + 1))
                t.wait_ge(v_dga, 1)
                t.wait_ge(s_h1, i + 1)
                if i == 0:
                    t.wait_ge(v_h2b, 1)
                for c in range(NCH):
                    t.matmul(pa[:, c * 128:(c + 1) * 128], fma[:, c * 128:(c + 1) * 128],
                             dga[:, c * 128:(c + 1) * 128], start=True, stop=True)
                t.fin(p_atm, 1)
                if i + 1 < NT:
                    emit_u1t(t, i + 1)
                t.wait_ge(v_u2, i + 1)
                if i >= 4:
                    t.wait_ge(s_h1, i + 1)
                    if i == 4:
                        t.wait_ge(v_mod, 6)
                        t.wait_ge(s_exp, scb(4))
                    else:
                        t.wait_ge(s_h2, i)
                    for c in range(NCH):
                        dstp = pd0 if c < 4 else pd1
                        t.matmul(dstp[:, (c % 4) * 128:(c % 4 + 1) * 128],
                                 u2[:, c * 128:(c + 1) * 128], idn[:, :], start=True, stop=True)
                else:
                    t.wait_ge(v_x2, i + 1)
                    for c in range(NCH):
                        t.matmul(pa[:, c * 128:(c + 1) * 128], u2[:, c * 128:(c + 1) * 128],
                                 idn[:, :], start=True, stop=True)
                t.fin(p_u2t, 1)
                t.wait_ge(v_xrh, i + 1)
                t.wait_ge(ld_w, LW["wr"])
                for hh2 in range(2):
                    for kc in range(NCH):
                        if hh2 == 0 and kc == 4:
                            t.wait_ge(v_xrf, i + 1)
                        t.matmul(pa[:, hh2 * 512:hh2 * 512 + 512],
                                 xr[:, kc * 128:(kc + 1) * 128],
                                 W["wr"][:, kc * D + hh2 * 512: kc * D + hh2 * 512 + 512],
                                 start=(kc == 0), stop=(kc == 7))
                t.fin(p_r, 1)
                t.wait_ge(ld_w, LW["wkf"])
                t.wait_ge(v_xr, i + 1)
                t.wait_ge(v_asm, 2 * (i + 1))
                for mc in range(NCH):
                    for kc in range(NCH):
                        t.matmul(pc[:, mc * 128:(mc + 1) * 128],
                                 W["wkf"][:, kc * D + mc * 128: kc * D + (mc + 1) * 128],
                                 xk[:, kc * 128:(kc + 1) * 128],
                                 start=(kc == 0), stop=(kc == 7))
                t.fin(p_kf, 1)
                if i + 1 < NT:
                    emit_q(t, i + 1)
                    emit_k(t, i + 1)
                t.wait_ge(ld_w, LW["wvf"])
                t.wait_ge(s_krh, i + 1)
                t.wait_ge(s_rth, i + 1)
                for hh2 in range(2):
                    if hh2 == 1:
                        t.wait_ge(s_rt, i + 1)
                    for kc in range(NCH):
                        if hh2 == 0 and kc == 4:
                            t.wait_ge(s_kr, i + 1)
                        t.matmul(pa[:, hh2 * 512:hh2 * 512 + 512],
                                 ksq[:, kc * 128:(kc + 1) * 128],
                                 W["wvf"][:, kc * D + hh2 * 512: kc * D + hh2 * 512 + 512],
                                 start=(kc == 0), stop=(kc == 7))
                t.fin(p_vf, 1)
                if i + 1 < NT:
                    emit_vp(t, i + 1)

        if DEBUG_DUMP:
            dbg_sem = sem("dbg_sem")
            name2sb = dict(vg=vg[0:64, :],
                           h2fv=h2f[:, :].rearrange("p (c w) -> p c w", w=132)[:, :, 0:129],
                           xin0=xin[0], xin1=xin[1], xin2=xin[2], u1=u1, u2=u2,
                           h1f=h1f, dt=dt_, xr=xr, xk=xk, rt=rt,
                           krel=krel, ksq=ksq, ffnf=ffnf, qg=qg, kg=kg,
                           a2u=a2u, vbs=vbs, on_=on_, fma=fma, rinv=rinv,
                           modf=modf, dga=dga, gbT=gbT, sT=sT, h1bf=h1bf,
                           x2b0=x2b_[0], x2b1=x2b_[1], yb0=yb_[0], yb1=yb_[1],
                           rstdb=rstdb, h2bf=h2bf, xbf=xbf, atnb=atnb,
                           x2bf=x2bf, lb_c=lb_c, ln_c=ln_c, ln_s=ln_s)
            dbg_outs = {}
            def _ap(x):
                return x if isinstance(x, bass.AP) else x[:, :]
            for nm in DEBUG_DUMP:
                ap_t = _ap(name2sb[nm])
                dbg_outs[nm] = nc.dram_tensor("dbg_" + nm, list(ap_t.shape), ap_t.dtype,
                                              kind="ExternalOutput").ap()

            @block.sync
            def _(s):
                s.wait_ge(st_y, 16 * NT)
                nd = 0
                for nm in DEBUG_DUMP:
                    s.dma_start(out=dbg_outs[nm], in_=_ap(name2sb[nm])).then_inc(dbg_sem, 16)
                    nd += 1
                s.wait_ge(dbg_sem, 16 * nd)

    return nc


def _host_prep(inputs):
    bf = ml_dtypes.bfloat16
    x = np.ascontiguousarray(np.asarray(inputs["L0_attn_x"], dtype=np.float32))
    B, S, _ = x.shape
    perm = np.concatenate([np.r_[mc * 64:(mc + 1) * 64, (mc + 8) * 64:(mc + 9) * 64]
                           for mc in range(8)])
    wq = np.ascontiguousarray(np.asarray(inputs["L0_attn_Wq"], np.float32)[:, perm])
    wk = np.ascontiguousarray(np.asarray(inputs["L0_attn_Wk"], np.float32)[:, perm])
    wv = np.ascontiguousarray(np.asarray(inputs["L0_attn_Wv"], np.float32)[:, perm])
    bq = np.asarray(inputs["L0_attn_bq"], np.float32)[perm]
    bk = np.asarray(inputs["L0_attn_bk"], np.float32)[perm]
    bv = np.asarray(inputs["L0_attn_bv"], np.float32)[perm]
    wr = np.ascontiguousarray(np.asarray(inputs["L0_ffn_W_r"], np.float32).T)
    wkf = np.ascontiguousarray(np.asarray(inputs["L0_ffn_W_k"], np.float32).T)
    wvf = np.ascontiguousarray(np.asarray(inputs["L0_ffn_W_v"], np.float32).T)
    wmod = np.ascontiguousarray(np.asarray(inputs["L0_wmod"], np.float32))
    cc = np.asarray(inputs["L0_c"], np.float32)
    mur = np.ascontiguousarray(np.asarray(inputs["L0_ffn_mu_r"], np.float32).reshape(8, 128).T)
    muk = np.ascontiguousarray(np.asarray(inputs["L0_ffn_mu_k"], np.float32).reshape(8, 128).T)

    def bias_pat(b):
        p = np.zeros((64, 16), np.float32)
        for mc in range(8):
            p[:, mc] = b[mc * 128: mc * 128 + 64]
            p[:, 8 + mc] = b[mc * 128 + 64: (mc + 1) * 128]
        return p

    consts = np.zeros((128, 16), np.float32)
    for col, val in [(C_EPS, 1e-5), (C_NHALF, -0.5), (C_QSC, 0.125), (C_NSH, -20.0),
                     (C_HALF, 0.5), (C_NEG1, -1.0), (C_RD, 1.0 / 1024.0), (C_ONE, 1.0),
                     (C_QTR, 0.25), (C_NQTR, -0.25), (C_15, 1.5)]:
        consts[:, col] = val
    consts[:, C_QK] = np.full(128, 0x5F3759DF, np.int32).view(np.float32)
    consts[:, C_SH1] = np.full(128, 1, np.int32).view(np.float32)
    idn = np.eye(128, dtype=np.float32).astype(bf)
    qs = np.zeros((64, GSLOT * 128), np.float32)
    ks = np.zeros((64, GSLOT * 128), np.float32)
    col_t = np.arange(GSLOT * 128) % 8
    for tt in range(8):
        qs[tt, col_t == tt] = 1.0
        ks[tt, col_t == tt] = 1024.0
    qs[8, :] = 1.0
    ks[8, :] = -1024.0
    qs = qs.astype(bf)
    ks = ks.astype(bf)
    ones1 = np.ones((1, 128), np.float32)
    ones1b = np.ones((1, 128), np.float32).astype(bf)
    onescol = np.ones((128, 1), np.float32)

    bv_nat = np.asarray(inputs["L0_attn_bv"], np.float32)
    bvt = np.ascontiguousarray(np.repeat(bv_nat.reshape(16, 1, 64), 8, axis=1).reshape(128, 64))
    shared = dict(wq=wq, wk=wk, wv=wv, wr=wr, wkf=wkf, wvf=wvf, wmod=wmod,
                  consts=consts, idn=idn, qstat=qs, kstat=ks, bvt=bvt,
                  bqp=bias_pat(bq), bkp=bias_pat(bk), bvp=bias_pat(bv),
                  mur=mur, muk=muk, ones1=ones1, ones1b=ones1b, onescol=onescol)
    in_maps = []
    for ci in range(8):
        bidx, half = ci // 2, ci % 2
        t0 = half * 2048
        m = dict(shared)
        m["x"] = np.ascontiguousarray(x[bidx, t0:t0 + 2048])
        m["xbnd"] = (np.zeros((1, D), np.float32) if half == 0
                     else np.ascontiguousarray(x[bidx, 2047:2048]))
        m["flag"] = np.full((128, 1), float(half), np.float32)
        m["crow"] = np.ascontiguousarray(cc[bidx:bidx + 1])
        in_maps.append(m)
    return in_maps


def kernel(**inputs):
    in_maps = _host_prep(inputs)
    nc = build_nc()
    res = run_bass_kernel_spmd(nc, in_maps, core_ids=list(range(8)))
    x = np.asarray(inputs["L0_attn_x"])
    B, S, _ = x.shape
    out = np.empty((B, S, D), np.float32)
    for ci in range(8):
        bidx, half = ci // 2, ci % 2
        out[bidx, half * 2048:(half + 1) * 2048] = res.results[ci]["y"]
    return out


# version 18 (final)
# speedup vs baseline: 1.1881x; 1.0170x over previous
import numpy as np
import ml_dtypes
import contextlib
import concourse.bass as bass
import concourse.mybir as mybir
from concourse.bass_utils import run_bass_kernel_spmd

F32 = mybir.dt.float32
BF16 = mybir.dt.bfloat16
AF = mybir.ActivationFunctionType
OP = mybir.AluOpType
AX = mybir.AxisListType

D = 1024
NT = 16
TPT = 128
NCH = 8
GSLOT = 17

(C_EPS, C_NHALF, C_QSC, C_NSH, C_HALF, C_NEG1, C_RD, C_ONE, C_QTR,
 C_NQTR, C_QK, C_SH1, C_15, C_X0, C_X1, C_X2) = range(16)

MVNAME = ["sa", "ca", "ga", "sb", "cb", "gb"]

DEBUG_DUMP = []


class _Trk:

    _OPS = {"tensor_copy", "tensor_tensor", "tensor_scalar", "memset",
            "scalar_tensor_tensor", "reciprocal", "reduce_sum", "reduce_max",
            "tensor_reduce", "bn_stats", "bn_aggr", "activation", "copy",
            "add", "mul", "matmul", "transpose"}

    def __init__(self, eng):
        object.__setattr__(self, "_e", eng)
        object.__setattr__(self, "last", None)
        object.__setattr__(self, "_finned", None)
        object.__setattr__(self, "_pend", set())

    @staticmethod
    def _apname(x):
        t = getattr(x, "tensor", None)
        nm = getattr(t, "name", None)
        return nm

    def __getattr__(self, n):
        eng = object.__getattribute__(self, "_e")
        a = getattr(eng, n)
        if not callable(a) or n in ("wait_ge",):
            return a
        trk = self

        def w(*args, **kw):
            if n in _Trk._OPS:
                pend = object.__getattribute__(trk, "_pend")
                names = []
                for x in list(args) + list(kw.values()):
                    nm = _Trk._apname(x)
                    if nm is not None:
                        names.append(nm)
                outs = names[:1]
                if "accum_out" in kw and kw["accum_out"] is not None:
                    nm = _Trk._apname(kw["accum_out"])
                    if nm:
                        outs.append(nm)
                ins = names[1:]
                if any(nm in pend for nm in ins):
                    eng.drain()
                    pend.clear()
                r = a(*args, **kw)
                for nm in outs:
                    pend.add(nm)
            else:
                r = a(*args, **kw)
            if r is not None and hasattr(r, "then_inc"):
                object.__setattr__(trk, "last", r)
            return r
        return w

    def fin(self, sem_, n_=1):
        li = object.__getattribute__(self, "last")
        if li is object.__getattribute__(self, "_finned"):
            object.__getattribute__(self, "_e").drain().then_inc(sem_, n_)
        else:
            li.then_inc(sem_, n_)
            object.__setattr__(self, "_finned", li)



def nbatch(i):
    return 5 if i == 0 else 4


def scb(i):
    return sum(nbatch(j) for j in range(i))


def batch_groups(i, b):
    if i == 0 and b == 0:
        return [0]
    bb = b - 1 if i == 0 else b
    return [1 + 4 * bb + k for k in range(4)]



def _emit_rsqrt(v, consts, out_col, var_ap, eps_bias, scr):
    I32 = mybir.dt.int32
    P = var_ap.shape[0]
    ve = scr[0:P, 0:1]
    t_ = scr[0:P, 1:2]
    v.tensor_scalar(ve, var_ap, eps_bias, None, OP.add)
    v.tensor_tensor(out_col.bitcast(I32), ve.bitcast(I32),
                    consts[0:P, C_SH1:C_SH1 + 1].bitcast(I32), OP.logical_shift_right)
    v.tensor_tensor(out_col.bitcast(I32), consts[0:P, C_QK:C_QK + 1].bitcast(I32),
                    out_col.bitcast(I32), OP.subtract)
    for _ in range(2):
        v.tensor_tensor(t_, out_col, out_col, OP.mult)
        v.tensor_tensor(t_, t_, ve, OP.mult)
        v.tensor_scalar(t_, t_, consts[0:P, C_NHALF:C_NHALF + 1], consts[0:P, C_15:C_15 + 1],
                        OP.mult, OP.add)
        v.tensor_tensor(out_col, out_col, t_, OP.mult)


def build_nc():
    nc = bass.Bass("TRN2", target_bir_lowering=False, detect_race_conditions=False)
    x_h = nc.dram_tensor("x", [NT * TPT, D], F32, kind="ExternalInput").ap()
    xb_h = nc.dram_tensor("xbnd", [1, D], F32, kind="ExternalInput").ap()
    flag_h = nc.dram_tensor("flag", [128, 1], F32, kind="ExternalInput").ap()
    w_h = {}
    for w in ["wq", "wk", "wv", "wr", "wkf", "wvf"]:
        w_h[w] = nc.dram_tensor(w, [D, D], F32, kind="ExternalInput").ap()
    wmod_h = nc.dram_tensor("wmod", [D, 6 * D], F32, kind="ExternalInput").ap()
    crow_h = nc.dram_tensor("crow", [1, D], F32, kind="ExternalInput").ap()
    consts_h = nc.dram_tensor("consts", [128, 16], F32, kind="ExternalInput").ap()
    idn_h = nc.dram_tensor("idn", [128, 128], BF16, kind="ExternalInput").ap()
    qstat_h = nc.dram_tensor("qstat", [64, GSLOT * 128], BF16, kind="ExternalInput").ap()
    kstat_h = nc.dram_tensor("kstat", [64, GSLOT * 128], BF16, kind="ExternalInput").ap()
    bqp_h = nc.dram_tensor("bqp", [64, 16], F32, kind="ExternalInput").ap()
    bkp_h = nc.dram_tensor("bkp", [64, 16], F32, kind="ExternalInput").ap()
    bvp_h = nc.dram_tensor("bvp", [64, 16], F32, kind="ExternalInput").ap()
    bvt_h = nc.dram_tensor("bvt", [128, 64], F32, kind="ExternalInput").ap()
    mur_h = nc.dram_tensor("mur", [128, 8], F32, kind="ExternalInput").ap()
    muk_h = nc.dram_tensor("muk", [128, 8], F32, kind="ExternalInput").ap()
    ones1_h = nc.dram_tensor("ones1", [1, 128], F32, kind="ExternalInput").ap()
    ones1b_h = nc.dram_tensor("ones1b", [1, 128], BF16, kind="ExternalInput").ap()
    onescol_h = nc.dram_tensor("onescol", [128, 1], F32, kind="ExternalInput").ap()
    y_h = nc.dram_tensor("y", [NT * TPT, D], F32, kind="ExternalOutput").ap()

    es = contextlib.ExitStack()
    sbuf = lambda n, s, d: es.enter_context(nc.sbuf_tensor(n, s, d))
    psum = lambda n, s, d: es.enter_context(nc.psum_tensor(n, s, d))
    sem = lambda n: es.enter_context(nc.semaphore(n))
    with es:
        W = {w: sbuf(w + "s", [128, 8 * D], BF16) for w in ["wq", "wk", "wv", "wr", "wkf", "wvf"]}
        wmb = [sbuf(f"wmb{i}", [128, 4 * 512], BF16) for i in range(2)]
        xin = [sbuf(f"xin{i}", [128, D], F32) for i in range(3)]
        x2b_ = [sbuf(f"x2b{i}", [128, D], F32) for i in range(2)]
        yb_ = [sbuf(f"yb{i}", [128, D], F32) for i in range(2)]
        u1 = sbuf("u1", [128, D], BF16)
        u2 = sbuf("u2", [128, D], BF16)
        h1f = sbuf("h1f", [128, D], BF16)
        h2f = sbuf("h2f", [128, 8 * 132], BF16)
        dt_ = sbuf("dt", [128, D], BF16)
        xr = sbuf("xr", [128, D], BF16)
        xk = sbuf("xk", [128, D], BF16)
        rt = sbuf("rt", [128, D], BF16)
        krel = sbuf("krel", [128, D], BF16)
        ksq = sbuf("ksq", [128, D], BF16)
        ffnf = sbuf("ffnf", [128, D], BF16)
        qg = sbuf("qg", [128, GSLOT * 128], BF16)
        kg = sbuf("kg", [128, GSLOT * 128], BF16)
        vg = sbuf("vg", [128, GSLOT * 128], BF16)
        a2u = sbuf("a2u", [128, GSLOT * 128], BF16)
        vbs = sbuf("vbs", [128, GSLOT * 72], BF16)
        on_ = sbuf("on", [128, GSLOT * 64], BF16)
        fma = sbuf("fma", [128, D], BF16)
        rinv = sbuf("rinv", [128, GSLOT], F32)
        idn = sbuf("idn_s", [128, 128], BF16)
        consts = sbuf("consts_s", [128, 16], F32)
        bqp = sbuf("bqp_s", [64, 16], F32)
        bkp = sbuf("bkp_s", [64, 16], F32)
        bvp = sbuf("bvp_s", [64, 16], F32)
        bvt = sbuf("bvt_s", [128, 64], F32)
        murf = sbuf("murf", [128, 8], F32)
        mukf = sbuf("mukf", [128, 8], F32)
        modf = sbuf("modf", [128, 48], F32)
        clsc = sbuf("clsc", [128, 24], F32)
        dga = sbuf("dga", [128, 8 * 128], BF16)
        gbT = sbuf("gbT", [128, D], BF16)
        gbrow = sbuf("gbrow", [1, D], F32)
        gbc = sbuf("gbc", [128, 8], BF16)
        gbT8 = sbuf("gbT8", [8, 128], F32)
        ln_s = sbuf("ln_s", [128, 28], F32)
        ln_c = sbuf("ln_c", [128, 4], F32)
        crow = sbuf("crow_s", [1, D], F32)
        silcb = sbuf("silcb", [1, D], BF16)
        sT = sbuf("sT", [128, 8], BF16)
        modrow = sbuf("modrow", [1, D], F32)
        xbnd = sbuf("xbnd_s", [1, D], F32)
        u1b = sbuf("u1b", [1, D], BF16)
        lb_s = sbuf("lb_s", [1, 14], F32)
        lb_c = sbuf("lb_c", [1, 6], F32)
        h1bf = sbuf("h1bf", [128, 8], BF16)
        xbf = sbuf("xbf", [128, 8], F32)
        atnb = sbuf("atnb", [128, 8], F32)
        x2bf = sbuf("x2bf", [128, 8], F32)
        x2sq = sbuf("x2sq", [128, 8], F32)
        red_b = sbuf("red_b", [128, 2], F32)
        rstdb = sbuf("rstdb", [128, 2], F32)
        h2bf = sbuf("h2bf", [128, 8], BF16)
        flagc = sbuf("flagc", [128, 1], F32)
        ones1 = sbuf("ones1_s", [1, 128], F32)
        ones1b = sbuf("ones1b_s", [1, 128], BF16)
        onescol = sbuf("onescol_s", [128, 1], F32)

        pa = psum("pa", [128, 1024], F32)
        pb = psum("pb", [128, 1024], F32)
        pc = psum("pc", [128, 1024], F32)
        pd0 = psum("pd0", [128, 512], F32)
        pd1 = psum("pd1", [128, 512], F32)

        ld_s, ld_x, ld_w, st_y = sem("ld_s"), sem("ld_x"), sem("ld_w"), sem("st_y")
        p_sT, p_xbf, p_mvh, p_mtr = sem("p_sT"), sem("p_xbf"), sem("p_mvh"), sem("p_mtr")
        p_q, p_k, p_v, p_bqkv = sem("p_q"), sem("p_k"), sem("p_v"), sem("p_bqkv")
        p_sc, p_vt, p_av, p_ont = sem("p_sc"), sem("p_vt"), sem("p_av"), sem("p_ont")
        p_bont, p_bst, p_bbc = sem("p_bont"), sem("p_bst"), sem("p_bbc")
        p_atm, p_u2t, p_u1t = sem("p_atm"), sem("p_u2t"), sem("p_u1t")
        p_r, p_kf, p_vf = sem("p_r"), sem("p_kf"), sem("p_vf")
        g_gbr, p_gbt, v_gbt = sem("g_gbr"), sem("p_gbt"), sem("v_gbt")
        p_gb1, v_gb8 = sem("p_gb1"), sem("v_gb8")
        s_h2h, v_xrh, v_xrf = sem("s_h2h"), sem("v_xrh"), sem("v_xrf")
        s_rth, s_krh, s_krl = sem("s_rth"), sem("s_krh"), sem("s_krl")
        v_lnb, v_u1b, v_silu, v_sT = sem("v_lnb"), sem("v_u1b"), sem("v_silu"), sem("v_sT")
        v_mve, v_mod, v_dga = sem("v_mve"), sem("v_mod"), sem("v_dga")
        v_xbf, v_h1b = sem("v_xbf"), sem("v_h1b")
        v_ln1, v_u1, v_qe, v_ke, v_ve = sem("v_ln1"), sem("v_u1"), sem("v_qe"), sem("v_ke"), sem("v_ve")
        v_vbe, v_rin, v_asm, v_x2 = sem("v_vbe"), sem("v_rin"), sem("v_asm"), sem("v_x2")
        v_ln2, v_u2, v_xr, v_ffn, v_y = sem("v_ln2"), sem("v_u2"), sem("v_xr"), sem("v_ffn"), sem("v_y")
        v_bev, v_ext, v_h2b = sem("v_bev"), sem("v_ext"), sem("v_h2b")
        s_silu, s_lb1, s_lb2, s_ln = sem("s_silu"), sem("s_lb1"), sem("s_lb2"), sem("s_ln")
        s_h1, s_exp, s_non, s_h2 = sem("s_h1"), sem("s_exp"), sem("s_non"), sem("s_h2")
        s_rt, s_kr, s_asm = sem("s_rt"), sem("s_kr"), sem("s_asm")

        block = es.enter_context(nc.Block())

        def score_pd(i, b):
            if i < 6 or (i == 0 and b == 0):
                return pd0
            return pd0 if b % 2 == 0 else pd1

        def mv_loads(nm):
            return [nm + jh + h for jh in "01" for h in "01"]
        dma_order = (mv_loads("sa") + mv_loads("ca") + ["wq"] + mv_loads("ga")
                     + ["wk", "wv"] + mv_loads("sb") + mv_loads("cb")
                     + ["wr", "wkf", "wvf"] + mv_loads("gb"))
        LW = {nm: 16 * (k + 1) for k, nm in enumerate(dma_order)}

        @block.scalar
        def _(s):
            for dst, src in [(consts[:], consts_h), (flagc[:], flag_h),
                             (crow[:], crow_h), (xbnd[:], xb_h),
                             (bqp[:], bqp_h), (bkp[:], bkp_h), (bvp[:], bvp_h), (bvt[:], bvt_h),
                             (murf[:], mur_h), (mukf[:], muk_h),
                             (idn[:], idn_h), (ones1[:], ones1_h),
                             (ones1b[:], ones1b_h), (onescol[:], onescol_h),
                             (qg[64:128, :], qstat_h), (kg[64:128, :], kstat_h)]:
                s.dma_start(out=dst, in_=src).then_inc(ld_s, 16)
        NLD_S = 16

        @block.sync
        def _(s):
            for i in range(min(3, NT)):
                s.dma_start(out=xin[i % 3][:], in_=x_h[i * TPT:(i + 1) * TPT, :]).then_inc(ld_x, 16)
            for i in range(NT):
                s.wait_ge(v_y, i + 1)
                s.dma_start(out=y_h[i * TPT:(i + 1) * TPT, :], in_=yb_[i % 2][:]).then_inc(st_y, 16)
                j = i + 3
                if j < NT:
                    s.dma_start(out=xin[j % 3][:], in_=x_h[j * TPT:(j + 1) * TPT, :]).then_inc(ld_x, 16)
            s.wait_ge(st_y, 16 * NT)

        @block.gpsimd
        def _(g):
            for nm in dma_order:
                if nm in W:
                    g.dma_start(out=W[nm][:, :].rearrange("p (kc d) -> p kc d", d=D),
                                in_=w_h[nm].rearrange("(kc p) d -> p kc d", p=128)).then_inc(ld_w, 16)
                else:
                    v = MVNAME.index(nm[:2])
                    jh, h = int(nm[2]), int(nm[3])
                    ell = 4 * v + 2 * jh + h
                    if ell >= 2:
                        g.wait_ge(p_mvh, 2 * v + jh)
                    src = wmod_h[h * 512:(h + 1) * 512, v * D + jh * 512: v * D + jh * 512 + 512]
                    g.dma_start(out=wmb[h][:, :].rearrange("p (ic j) -> p ic j", j=512),
                                in_=src.rearrange("(ic p) j -> p ic j", p=128)).then_inc(ld_w, 16)
            g.wait_ge(v_gb8, 1)
            for c in range(NCH):
                g.dma_start(out=gbrow[0:1, c * 128:(c + 1) * 128],
                            in_=gbT8[c:c + 1, :]).then_inc(g_gbr, 16)

        @block.scalar
        def _(s):
            s = _Trk(s)
            s.wait_ge(ld_s, 16 * NLD_S)
            s.activation(modrow[:], crow[:], AF.Silu).then_inc(s_silu, 1)

            def emit_h1(s, i):
                s.wait_ge(p_u1t, i + 1)
                s.wait_ge(v_mod, 2)
                for c in range(NCH):
                    srcp = (pa if i == 0 else pb)[:, c * 128:(c + 1) * 128]
                    s.activation(h1f[:, c * 128:(c + 1) * 128], srcp,
                                 AF.Identity, scale=modf[:, 8 + c:9 + c], bias=modf[:, c:c + 1])
                s.fin(s_h1, 1)

            emit_h1(s, 0)
            for i in range(NT):
                for b in range(nbatch(i)):
                    gl = batch_groups(i, b)
                    s.wait_ge(p_sc, scb(i) + b + 1)
                    pdx = score_pd(i, b)
                    ncols = 128 * len(gl)
                    s.activation(a2u[:, gl[0] * 128: gl[0] * 128 + ncols], pdx[:, 0:ncols],
                                 AF.Exp, scale=consts[:, C_QSC:C_QSC + 1],
                                 bias=consts[:, C_NSH:C_NSH + 1]).then_inc(s_exp, 1)
                    s.wait_ge(p_av, scb(i) + b + 1)
                    s.wait_ge(v_rin, scb(i) + b + 1)
                    for gidx in gl:
                        if gidx == 0:
                            src = pd0[:, 256:320]
                        else:
                            pb_ = (gidx - 1) // 4
                            k = (gidx - 1) % 4
                            cb = (pb_ % 2) * 512 + k * 65
                            src = pb[:, cb:cb + 64]
                        s.activation(on_[:, gidx * 64:(gidx + 1) * 64], src,
                                     AF.Identity, scale=rinv[:, gidx:gidx + 1])
                    s.fin(s_non, 1)
                s.wait_ge(p_u2t, i + 1)
                s.wait_ge(v_mod, 5)
                for c in range(NCH):
                    if i >= 4:
                        srcp = (pd0 if c < 4 else pd1)[:, (c % 4) * 128:(c % 4 + 1) * 128]
                    else:
                        srcp = pa[:, c * 128:(c + 1) * 128]
                    s.activation(h2f[:, c * 132 + 1: c * 132 + 129], srcp,
                                 AF.Identity, scale=modf[:, 32 + c:33 + c], bias=modf[:, 24 + c:25 + c])
                    if c == 3:
                        s.fin(s_h2h, 1)
                s.fin(s_h2, 1)
                if i + 1 < NT:
                    emit_h1(s, i + 1)
                s.wait_ge(p_r, i + 1)
                s.activation(rt[:, 0:512], pa[:, 0:512], AF.Tanh, scale=consts[:, C_HALF:C_HALF + 1])
                s.fin(s_rth, 1)
                s.activation(rt[:, 512:1024], pa[:, 512:1024], AF.Tanh,
                             scale=consts[:, C_HALF:C_HALF + 1]).then_inc(s_rt, 1)
                s.wait_ge(p_kf, i + 1)
                s.activation(krel[:, 0:512], pc[:, 0:512], AF.Relu)
                s.activation(ksq[:, 0:512], krel[:, 0:512], AF.Square)
                s.fin(s_krh, 1)
                s.activation(krel[:, 512:1024], pc[:, 512:1024], AF.Relu)
                s.fin(s_krl, 1)
                s.activation(ksq[:, 512:1024], krel[:, 512:1024], AF.Square).then_inc(s_kr, 1)

        @block.vector
        def _(v):
            v = _Trk(v)
            v.wait_ge(ld_s, 16 * NLD_S)
            v.memset(qg[0:64, 0:128], 0.0)
            v.memset(kg[0:64, 0:128], 0.0)
            v.memset(vg[0:64, 0:128], 0.0)
            v.memset(vbs[:, :], 0.0)
            v.memset(vbs[:, :].rearrange("p (s c) -> p s c", c=72)[:, :, 64:65], 1.0)
            v.memset(h2f[:, :].rearrange("p (s c) -> p s c", c=132)[:, :, 0:1], 0.0)
            v.bn_stats(lb_s[:, 0:6], xbnd[0:1, 0:512])
            v.bn_stats(lb_s[:, 6:12], xbnd[0:1, 512:1024])
            v.bn_aggr(lb_s[:, 12:14], lb_s[:, 0:12]).then_inc(v_lnb, 1)
            _emit_rsqrt(v, consts, lb_c[0:1, 0:1], lb_s[0:1, 13:14],
                        consts[0:1, C_EPS:C_EPS + 1], clsc)
            v.tensor_scalar(lb_c[0:1, 5:6], lb_s[0:1, 12:13], lb_c[0:1, 0:1],
                            consts[0:1, C_NEG1:C_NEG1 + 1], OP.mult, OP.mult)
            v.tensor_scalar(u1b[:], xbnd[:], lb_c[0:1, 0:1], lb_c[0:1, 5:6],
                            OP.mult, OP.add).then_inc(v_u1b, 1)
            v.wait_ge(s_silu, 1)
            v.tensor_copy(silcb[:], modrow[:]).then_inc(v_silu, 1)
            v.wait_ge(p_sT, 1)
            v.tensor_copy(sT[:, :], pd0[:, 480:488]).then_inc(v_sT, 1)
            v.wait_ge(p_xbf, 1)
            v.tensor_copy(xbf[:, :], pa[:, 1000:1008]).then_inc(v_xbf, 1)
            for mv in range(6):
                for jh in range(2):
                    v.wait_ge(p_mvh, 2 * mv + jh + 1)
                    v.tensor_copy(modrow[0:1, jh * 512:(jh + 1) * 512], pd1[0:1, 0:512]).then_inc(v_mve, 1)
                v.wait_ge(p_mtr, mv + 1)
                v.tensor_copy(modf[:, 8 * mv:8 * mv + 8], pd1[:, 384:392])
                if mv in (1, 4):
                    v.tensor_scalar(modf[:, 8 * mv:8 * mv + 8], modf[:, 8 * mv:8 * mv + 8],
                                    consts[:, C_ONE:C_ONE + 1], None, OP.add)
                if mv in (2, 5):
                    cf = 8 * mv
                    gsl = modf[:, cf:cf + 8]
                    v.tensor_scalar(clsc[:, 0:8], gsl, 0.0, None, OP.is_ge)
                    v.tensor_scalar(clsc[:, 8:16], gsl, consts[:, C_QTR:C_QTR + 1], None, OP.max)
                    v.tensor_scalar(clsc[:, 16:24], gsl, consts[:, C_NQTR:C_NQTR + 1], None, OP.min)
                    v.tensor_tensor(clsc[:, 8:16], clsc[:, 8:16], clsc[:, 16:24], OP.subtract)
                    v.tensor_tensor(clsc[:, 8:16], clsc[:, 8:16], clsc[:, 0:8], OP.mult)
                    v.tensor_tensor(gsl, clsc[:, 8:16], clsc[:, 16:24], OP.add)
                    if mv == 5:
                        v.tensor_scalar(gsl, gsl, consts[:, C_HALF:C_HALF + 1], None, OP.mult)
                    if mv == 2:
                        for c in range(NCH):
                            v.tensor_scalar(dga[:, c * 128:(c + 1) * 128], idn[:, :],
                                            modf[:, cf + c:cf + c + 1], None, OP.mult)
                    else:
                        v.tensor_copy(gbc[:, :], gsl)
                    v.fin(v_dga, 1)
                v.fin(v_mod, 1)
                if mv == 1:
                    v.tensor_tensor(x2sq[:, 0:8], pa[:, 1008:1016], modf[:, 8:16], OP.mult)
                    v.tensor_tensor(x2sq[:, 0:8], x2sq[:, 0:8], modf[:, 0:8], OP.add)
                    v.tensor_copy(h1bf[:, :], x2sq[:, 0:8]).then_inc(v_h1b, 1)
            def emit_ln1_stats(v, i):
                v.wait_ge(ld_x, 16 * (i + 1))
                v.bn_stats(ln_s[:, 0:6], xin[i % 3][:, 0:512])
                v.bn_stats(ln_s[:, 6:12], xin[i % 3][:, 512:1024])
                v.bn_aggr(ln_s[:, 24:26], ln_s[:, 0:12]).then_inc(v_ln1, 1)

            def emit_ln1_fin(v, i):
                v.wait_ge(p_u1t, i)
                _emit_rsqrt(v, consts, ln_c[:, 0:1], ln_s[:, 25:26],
                            consts[:, C_EPS:C_EPS + 1], clsc)
                v.tensor_scalar(ln_c[:, 2:3], ln_s[:, 24:25], ln_c[:, 0:1],
                                consts[:, C_NEG1:C_NEG1 + 1], OP.mult, OP.mult)
                v.tensor_scalar(u1[:], xin[i % 3][:], ln_c[:, 0:1], ln_c[:, 2:3],
                                OP.mult, OP.add).then_inc(v_u1, 1)

            def emit_ln1(v, i):
                emit_ln1_stats(v, i)
                emit_ln1_fin(v, i)

            v.wait_ge(p_gb1, 1)
            v.tensor_copy(gbT8[:, :], pd1[0:8, 384:512])
            v.fin(v_gb8, 1)
            v.wait_ge(p_gbt, 1)
            v.tensor_copy(gbT[:, 0:512], pd0[:, 0:512])
            v.tensor_copy(gbT[:, 512:1024], pd1[:, 0:512])
            v.fin(v_gbt, 1)

            def emit_evac(v, i, which):
                psrc, gdst, bpat, vsem, pwait = {
                    "q": (pb, qg, bqp, v_qe, p_q), "k": (pc, kg, bkp, v_ke, p_k),
                    "v": (pb, vg, None, v_ve, p_v)}[which]
                v.wait_ge(pwait, i + 1)
                for par in range(2):
                    in3 = psrc[64 * par:64 * par + 64, :].rearrange(
                        "p (mc g t) -> p mc g t", g=16, t=8)
                    out3 = gdst[0:64, :].rearrange("p (g jj t) -> p g jj t", jj=16, t=8)[
                        :, 1:GSLOT, 8 * par:8 * par + 8, :].rearrange("p g j t -> p j g t")
                    if bpat is None:
                        v.tensor_copy(out3, in3)
                    else:
                        bias = bpat[:, 8 * par:8 * par + 8].unsqueeze(2).unsqueeze(3).broadcast_to([64, 8, 16, 8])
                        v.tensor_tensor(out3, in3, bias, OP.add)
                v.fin(vsem, 1)

            emit_ln1(v, 0)
            emit_evac(v, 0, "q")
            emit_evac(v, 0, "k")
            for i in range(NT):
                emit_evac(v, i, "v")
                if i == 0:
                    v.wait_ge(p_bqkv, 1)
                    for (tix, gdst, bpat) in [(0, qg, bqp), (1, kg, bkp)]:
                        for par in range(2):
                            in2 = pd0[64 * par:64 * par + 64, 8 * tix:8 * tix + 8]
                            out2 = gdst[0:64, :].rearrange("p (g jj t) -> p g jj t", jj=16, t=8)[
                                :, 0, 8 * par:8 * par + 8, 0]
                            v.tensor_tensor(out2, in2, bpat[:, 8 * par:8 * par + 8], OP.add)
                    for par in range(2):
                        in2 = pd0[64 * par:64 * par + 64, 16:24]
                        out2 = vg[0:64, :].rearrange("p (g jj t) -> p g jj t", jj=16, t=8)[
                            :, 0, 8 * par:8 * par + 8, 0]
                        v.tensor_copy(out2, in2)
                    v.fin(v_bev, 1)
                for b in range(nbatch(i)):
                    gl = batch_groups(i, b)
                    v.wait_ge(p_vt, scb(i) + b + 1)
                    if len(gl) == 1:
                        v.tensor_tensor(vbs[:, 0:64], pa[:, 256:320], bvt[:, :], OP.add)
                    else:
                        pb_ = b - 1 if i == 0 else b
                        base = (pb_ % 2) * 512
                        dstv = vbs[:, :].rearrange("p (s c) -> p s c", c=72)[:, gl[0]:gl[0] + 4, 0:64]
                        srcv = pa[:, base:base + 256].rearrange("p (kk c) -> p kk c", c=64)
                        bias4 = bvt[:, :].unsqueeze(1).broadcast_to([128, 4, 64])
                        v.tensor_tensor(dstv, srcv, bias4, OP.add)
                    v.fin(v_vbe, 1)
                    v.wait_ge(p_av, scb(i) + b + 1)
                    if len(gl) == 1:
                        v.reciprocal(rinv[:, 0:1], pd0[:, 320:321])
                    else:
                        pb_ = b - 1 if i == 0 else b
                        base = (pb_ % 2) * 512 + 64
                        src_s = pb[:, base - 64:base - 64 + 260].rearrange("p (s c) -> p s c", c=65)[:, :, 64]
                        v.reciprocal(rinv[:, gl[0]:gl[0] + 4], src_s)
                    v.fin(v_rin, 1)
                    pb_h = b - 1 if i == 0 else b
                    if pb_h in (1, 3) and len(gl) == 4:
                        hh = pb_h // 2
                        bofs = 1 if i == 0 else 0
                        v.wait_ge(p_ont, scb(i) + bofs + 2 * (hh + 1))
                        for par in range(2):
                            srca = pc[0:64, :].rearrange(
                                "p (g c2 pr t) -> p c2 pr g t", t=8, pr=2, c2=8)[:, :, par, :, :]
                            dsta = fma[64 * par:64 * par + 64, :].rearrange(
                                "p (c h2 g t) -> p c h2 g t", h2=2, g=8, t=8)[:, :, hh, :, :]
                            v.tensor_copy(dsta, srca)
                        v.fin(v_asm, 1)
                        if pb_h == 1 and i + 1 < NT:
                            emit_ln1_stats(v, i + 1)
                        if pb_h == 3 and i + 1 < NT:
                            emit_ln1_fin(v, i + 1)
                    if i == 0 and b == 0:
                        v.wait_ge(p_bont, 1)
                        for par in range(2):
                            src2 = pd0[0:64, 128:256].rearrange("p (c2 pr t) -> p c2 pr t", pr=2, t=8)[:, :, par, 0]
                            v.tensor_copy(atnb[64 * par:64 * par + 64, 0:8], src2)
                        v.wait_ge(v_dga, 1)
                        v.tensor_tensor(atnb[:], atnb[:], modf[:, 16:24], OP.mult)
                        v.wait_ge(v_xbf, 1)
                        v.tensor_tensor(x2bf[:], xbf[:], atnb[:], OP.add)
                        v.tensor_tensor(x2sq[:], x2bf[:], x2bf[:], OP.mult)
                        v.reduce_sum(red_b[:, 0:1], x2bf[:], axis=AX.X)
                        v.reduce_sum(red_b[:, 1:2], x2sq[:], axis=AX.X)
                        v.fin(v_ext, 1)
                        v.wait_ge(p_bst, 1)
                        v.tensor_scalar(lb_c[0:1, 2:3], pa[0:1, 992:993], consts[0:1, C_RD:C_RD + 1],
                                        None, OP.mult)
                        v.tensor_tensor(lb_c[0:1, 3:4], lb_c[0:1, 2:3], lb_c[0:1, 2:3], OP.mult)
                        v.tensor_scalar(lb_c[0:1, 3:4], lb_c[0:1, 3:4], consts[0:1, C_NEG1:C_NEG1 + 1],
                                        consts[0:1, C_EPS:C_EPS + 1], OP.mult, OP.add)
                        v.tensor_scalar(lb_c[0:1, 4:5], pa[0:1, 993:994],
                                        consts[0:1, C_RD:C_RD + 1], None, OP.mult)
                        v.tensor_tensor(lb_c[0:1, 4:5], lb_c[0:1, 4:5], lb_c[0:1, 3:4], OP.add)
                        _emit_rsqrt(v, consts, lb_c[0:1, 1:2], lb_c[0:1, 4:5], 0.0, clsc)
                        v.fin(v_ext, 1)
                        v.tensor_scalar(lb_c[0:1, 4:5], lb_c[0:1, 2:3], lb_c[0:1, 1:2],
                                        consts[0:1, C_NEG1:C_NEG1 + 1], OP.mult, OP.mult)
                        v.fin(v_ext, 1)
                        v.wait_ge(p_bbc, 1)
                        v.tensor_copy(rstdb[:, 0:2], pa[:, 994:996])
                        v.tensor_scalar(x2bf[:], x2bf[:], rstdb[:, 0:1], rstdb[:, 1:2], OP.mult, OP.add)
                        v.tensor_tensor(x2bf[:], x2bf[:], modf[:, 32:40], OP.mult)
                        v.tensor_tensor(x2bf[:], x2bf[:], modf[:, 24:32], OP.add)
                        v.tensor_scalar(h2bf[:], x2bf[:], flagc[:, 0:1], None, OP.mult)
                        v.tensor_copy(h2f[:, :].rearrange("p (c w) -> p c w", w=132)[:, :, 0:1],
                                      h2bf[:, :].unsqueeze(2))
                        v.fin(v_h2b, 1)
                v.wait_ge(p_atm, i + 1)
                v.tensor_tensor(x2b_[i % 2][:], xin[i % 3][:], pa[:, :], OP.add).then_inc(v_x2, 1)
                v.bn_stats(ln_s[:, 12:18], x2b_[i % 2][:, 0:512])
                v.bn_stats(ln_s[:, 18:24], x2b_[i % 2][:, 512:1024])
                v.bn_aggr(ln_s[:, 26:28], ln_s[:, 12:24]).then_inc(v_ln2, 1)
                _emit_rsqrt(v, consts, ln_c[:, 1:2], ln_s[:, 27:28],
                            consts[:, C_EPS:C_EPS + 1], clsc)
                v.tensor_scalar(ln_c[:, 2:3], ln_s[:, 26:27], ln_c[:, 1:2],
                                consts[:, C_NEG1:C_NEG1 + 1], OP.mult, OP.mult)
                v.tensor_scalar(u2[:], x2b_[i % 2][:], ln_c[:, 1:2], ln_c[:, 2:3],
                                OP.mult, OP.add).then_inc(v_u2, 1)
                dt3 = dt_[:, :].rearrange("p (c t) -> p c t", t=128)
                h2s = h2f[:, :].rearrange("p (c w) -> p c w", w=132)
                v.wait_ge(s_h2h, i + 1)
                if i == 0:
                    v.wait_ge(v_h2b, 1)
                v.tensor_tensor(dt3[:, 0:4, :], h2s[:, 0:4, 1:129], h2s[:, 0:4, 0:128], OP.subtract)
                for c in range(4):
                    v.scalar_tensor_tensor(xr[:, c * 128:(c + 1) * 128], dt_[:, c * 128:(c + 1) * 128],
                                           murf[:, c:c + 1], h2f[:, c * 132:c * 132 + 128],
                                           OP.mult, OP.add)
                v.fin(v_xrh, 1)
                v.wait_ge(s_h2, i + 1)
                v.tensor_tensor(dt3[:, 4:8, :], h2s[:, 4:8, 1:129], h2s[:, 4:8, 0:128], OP.subtract)
                for c in range(4, NCH):
                    v.scalar_tensor_tensor(xr[:, c * 128:(c + 1) * 128], dt_[:, c * 128:(c + 1) * 128],
                                           murf[:, c:c + 1], h2f[:, c * 132:c * 132 + 128],
                                           OP.mult, OP.add)
                v.fin(v_xrf, 1)
                for c in range(NCH):
                    v.scalar_tensor_tensor(xk[:, c * 128:(c + 1) * 128], dt_[:, c * 128:(c + 1) * 128],
                                           mukf[:, c:c + 1], h2f[:, c * 132:c * 132 + 128],
                                           OP.mult, OP.add)
                if i < NT - 1:
                    v.tensor_copy(h2f[:, :].rearrange("p (c w) -> p c w", w=132)[:, :, 0:1],
                                  h2f[:, :].rearrange("p (c w) -> p c w", w=132)[:, :, 128:129])
                v.fin(v_xr, 1)
                if i + 1 < NT:
                    emit_evac(v, i + 1, "q")
                    emit_evac(v, i + 1, "k")
                v.wait_ge(p_vf, i + 1)
                v.scalar_tensor_tensor(ffnf[:], rt[:], 1.0, pa[:, :], OP.add, OP.mult).then_inc(v_ffn, 1)
                if i >= 2:
                    v.wait_ge(st_y, 16 * (i - 1))
                v.tensor_tensor(yb_[i % 2][:], ffnf[:], gbT[:, :], OP.mult)
                v.tensor_tensor(yb_[i % 2][:], yb_[i % 2][:], x2b_[i % 2][:], OP.add).then_inc(v_y, 1)

        @block.tensor
        def _(t):
            t = _Trk(t)
            t.wait_ge(v_silu, 1)
            for c in range(NCH):
                t.matmul(pd0[:, 480 + c:481 + c], silcb[0:1, c * 128:(c + 1) * 128],
                         ones1b[0:1, 0:1], start=True, stop=True)
            t.fin(p_sT, 1)
            for c in range(NCH):
                t.matmul(pa[:, 1000 + c:1001 + c], xbnd[0:1, c * 128:(c + 1) * 128],
                         ones1[0:1, 0:1], start=True, stop=True)
            t.wait_ge(v_u1b, 1)
            for c in range(NCH):
                t.matmul(pa[:, 1008 + c:1009 + c], u1b[0:1, c * 128:(c + 1) * 128],
                         ones1b[0:1, 0:1], start=True, stop=True)
            t.fin(p_xbf, 1)
            t.wait_ge(v_sT, 1)
            for mv in range(6):
                if mv > 0:
                    t.wait_ge(v_mod, mv)
                for jh in range(2):
                    t.wait_ge(ld_w, LW[MVNAME[mv] + str(jh) + "1"])
                    if jh == 1:
                        t.wait_ge(v_mve, 2 * mv + 1)
                    for ic in range(8):
                        t.matmul(pd1[0:1, 0:512], sT[:, ic:ic + 1],
                                 wmb[ic // 4][:, (ic % 4) * 512:(ic % 4) * 512 + 512],
                                 start=(ic == 0), stop=(ic == 7))
                    t.fin(p_mvh, 1)
                t.wait_ge(v_mve, 2 * mv + 2)
                for c in range(NCH):
                    t.matmul(pd1[:, 384 + c:385 + c], modrow[0:1, c * 128:(c + 1) * 128],
                             ones1[0:1, 0:1], start=True, stop=True)
                t.fin(p_mtr, 1)
            t.wait_ge(v_dga, 2)
            t.matmul(pd1[0:8, 384:512], gbc[:, :], idn[:, :], start=True, stop=True)
            t.fin(p_gb1, 1)
            t.wait_ge(g_gbr, 16 * 8)
            t.matmul(pd0[:, 0:512], ones1[0:1, :], gbrow[0:1, 0:512], start=True, stop=True)
            t.matmul(pd1[:, 0:512], ones1[0:1, :], gbrow[0:1, 512:1024], start=True, stop=True)
            t.fin(p_gbt, 1)
            t.wait_ge(v_gbt, 1)
            def emit_u1t(t, i):
                t.wait_ge(v_u1, i + 1)
                if i == 0:
                    t.wait_ge(v_h1b, 1)
                    t.wait_ge(v_xbf, 1)
                    for c in range(NCH):
                        t.matmul(pa[:, c * 128:(c + 1) * 128], u1[:, c * 128:(c + 1) * 128],
                                 idn[:, :], start=True, stop=True)
                else:
                    t.wait_ge(v_rin, scb(i))
                    t.wait_ge(s_non, scb(i))
                    t.wait_ge(v_vbe, scb(i))
                    for c in range(NCH):
                        t.matmul(pb[:, c * 128:(c + 1) * 128], u1[:, c * 128:(c + 1) * 128],
                                 idn[:, :], start=True, stop=True)
                t.fin(p_u1t, 1)

            def emit_q(t, i):
                t.wait_ge(s_h1, i + 1)
                t.wait_ge(ld_w, LW["wq"])
                for mc in range(NCH):
                    for kc in range(NCH):
                        t.matmul(pb[:, mc * 128:(mc + 1) * 128],
                                 W["wq"][:, kc * D + mc * 128: kc * D + (mc + 1) * 128],
                                 h1f[:, kc * 128:(kc + 1) * 128],
                                 start=(kc == 0), stop=(kc == 7))
                        if i == 0:
                            t.matmul(pd0[:, 0 + mc:1 + mc],
                                     W["wq"][:, kc * D + mc * 128: kc * D + (mc + 1) * 128],
                                     h1bf[:, kc:kc + 1], start=(kc == 0), stop=(kc == 7))
                t.fin(p_q, 1)

            def emit_k(t, i):
                t.wait_ge(ld_w, LW["wk"])
                if i > 0:
                    t.wait_ge(s_krl, i)
                for mc in range(NCH):
                    for kc in range(NCH):
                        t.matmul(pc[:, mc * 128:(mc + 1) * 128],
                                 W["wk"][:, kc * D + mc * 128: kc * D + (mc + 1) * 128],
                                 h1f[:, kc * 128:(kc + 1) * 128],
                                 start=(kc == 0), stop=(kc == 7))
                        if i == 0:
                            t.matmul(pd0[:, 8 + mc:9 + mc],
                                     W["wk"][:, kc * D + mc * 128: kc * D + (mc + 1) * 128],
                                     h1bf[:, kc:kc + 1], start=(kc == 0), stop=(kc == 7))
                t.fin(p_k, 1)

            def emit_vp(t, i):
                t.wait_ge(ld_w, LW["wv"])
                t.wait_ge(v_qe, i + 1)
                for mc in range(NCH):
                    for kc in range(NCH):
                        t.matmul(pb[:, mc * 128:(mc + 1) * 128],
                                 W["wv"][:, kc * D + mc * 128: kc * D + (mc + 1) * 128],
                                 h1f[:, kc * 128:(kc + 1) * 128],
                                 start=(kc == 0), stop=(kc == 7))
                        if i == 0:
                            t.matmul(pd0[:, 16 + mc:17 + mc],
                                     W["wv"][:, kc * D + mc * 128: kc * D + (mc + 1) * 128],
                                     h1bf[:, kc:kc + 1], start=(kc == 0), stop=(kc == 7))
                t.fin(p_v, 1)
                if i == 0:
                    t.fin(p_bqkv, 1)

            emit_u1t(t, 0)
            emit_q(t, 0)
            emit_k(t, 0)
            emit_vp(t, 0)
            for i in range(NT):
                t.wait_ge(v_ve, i + 1)
                t.wait_ge(v_ke, i + 1)
                if i == 0:
                    t.wait_ge(v_bev, 1)
                if i == 6:
                    t.wait_ge(v_mod, 6)
                for b in range(nbatch(i)):
                    gl = batch_groups(i, b)
                    pdx = score_pd(i, b)
                    sb_ = scb(i) + b
                    is_bnd = (len(gl) == 1)
                    pb_ = (b - 1 if i == 0 else b)
                    if b == 0 and i >= 5:
                        t.wait_ge(s_h2, i)
                    if b == 0 and i >= 4:
                        t.wait_ge(s_h1, i + 1)
                    if pdx is pd0:
                        if i == 0 and b == 1:
                            t.wait_ge(v_ext, 1)
                            t.wait_ge(v_rin, 1)
                            t.wait_ge(s_non, 1)
                        if sb_ > 0:
                            t.wait_ge(s_exp, sb_ if i < 6 else sb_ - 1)
                    else:
                        if sb_ >= 2:
                            t.wait_ge(s_exp, sb_ - 1)
                    for kk, gidx in enumerate(gl):
                        t.matmul(pdx[:, kk * 128:(kk + 1) * 128], kg[:, gidx * 128:(gidx + 1) * 128],
                                 qg[:, gidx * 128:(gidx + 1) * 128], start=True, stop=True)
                    t.fin(p_sc, 1)
                    if sb_ >= 2:
                        t.wait_ge(v_vbe, sb_ - 1)
                    if b == 0 and i > 0:
                        t.wait_ge(v_ffn, i)
                    if i == 0 and not is_bnd:
                        t.wait_ge(v_h2b, 1)
                    for kk, gidx in enumerate(gl):
                        if is_bnd:
                            vt_out = pa[:, 256:320]
                        else:
                            vt_out = pa[:, (pb_ % 2) * 512 + 64 * kk:(pb_ % 2) * 512 + 64 * kk + 64]
                        t.matmul(vt_out, vg[0:64, gidx * 128:(gidx + 1) * 128],
                                 idn[0:64, 0:64], start=True, stop=True)
                    t.fin(p_vt, 1)
                    t.wait_ge(s_exp, sb_ + 1)
                    t.wait_ge(v_vbe, sb_ + 1)
                    if sb_ >= 2:
                        t.wait_ge(s_non, sb_ - 1)
                        t.wait_ge(v_rin, sb_ - 1)
                    for kk, gidx in enumerate(gl):
                        if is_bnd:
                            oa_out = pd0[:, 256:321]
                        else:
                            cb_ = (pb_ % 2) * 512 + 65 * kk
                            oa_out = pb[:, cb_:cb_ + 65]
                        t.matmul(oa_out, a2u[:, gidx * 128:(gidx + 1) * 128],
                                 vbs[:, gidx * 72:gidx * 72 + 65], start=True, stop=True)
                    t.fin(p_av, 1)
                    t.wait_ge(s_non, sb_ + 1)
                    if is_bnd:
                        t.matmul(pd0[0:64, 128:256], on_[:, 0:64], idn[:, :], start=True, stop=True)
                    else:
                        if pb_ % 2 == 0:
                            if pb_ == 0:
                                t.wait_ge(v_ke, i + 1)
                                if i > 0:
                                    t.wait_ge(s_kr, i)
                                    t.wait_ge(v_asm, 2 * i)
                            else:
                                t.wait_ge(v_asm, 2 * i + 1)
                        for gidx in gl:
                            s_ = (gidx - 1) % 8
                            t.matmul(pc[0:64, s_ * 128:(s_ + 1) * 128],
                                     on_[:, gidx * 64:(gidx + 1) * 64], idn[:, :],
                                     start=True, stop=True)
                    t.fin(p_ont, 1)
                    if i == 0 and b == 0:
                        t.fin(p_bont, 1)
                        t.wait_ge(v_ext, 1)
                        t.matmul(pa[0:1, 992:993], red_b[:, 0:1], onescol[:, 0:1], start=True, stop=True)
                        t.matmul(pa[0:1, 993:994], red_b[:, 1:2], onescol[:, 0:1], start=True, stop=True)
                        t.fin(p_bst, 1)
                        t.wait_ge(v_ext, 3)
                        t.matmul(pa[:, 994:995], ones1[0:1, :], lb_c[0:1, 1:2], start=True, stop=True)
                        t.matmul(pa[:, 995:996], ones1[0:1, :], lb_c[0:1, 4:5], start=True, stop=True)
                        t.fin(p_bbc, 1)
                t.wait_ge(v_asm, 2 * (i + 1))
                t.wait_ge(v_dga, 1)
                t.wait_ge(s_h1, i + 1)
                if i == 0:
                    t.wait_ge(v_h2b, 1)
                for c in range(NCH):
                    t.matmul(pa[:, c * 128:(c + 1) * 128], fma[:, c * 128:(c + 1) * 128],
                             dga[:, c * 128:(c + 1) * 128], start=True, stop=True)
                t.fin(p_atm, 1)
                if i + 1 < NT:
                    emit_u1t(t, i + 1)
                t.wait_ge(v_u2, i + 1)
                if i >= 4:
                    t.wait_ge(s_h1, i + 1)
                    if i == 4:
                        t.wait_ge(v_mod, 6)
                        t.wait_ge(s_exp, scb(4))
                    else:
                        t.wait_ge(s_h2, i)
                    for c in range(NCH):
                        dstp = pd0 if c < 4 else pd1
                        t.matmul(dstp[:, (c % 4) * 128:(c % 4 + 1) * 128],
                                 u2[:, c * 128:(c + 1) * 128], idn[:, :], start=True, stop=True)
                else:
                    t.wait_ge(v_x2, i + 1)
                    for c in range(NCH):
                        t.matmul(pa[:, c * 128:(c + 1) * 128], u2[:, c * 128:(c + 1) * 128],
                                 idn[:, :], start=True, stop=True)
                t.fin(p_u2t, 1)
                t.wait_ge(v_xrh, i + 1)
                t.wait_ge(ld_w, LW["wr"])
                for hh2 in range(2):
                    for kc in range(NCH):
                        if hh2 == 0 and kc == 4:
                            t.wait_ge(v_xrf, i + 1)
                        t.matmul(pa[:, hh2 * 512:hh2 * 512 + 512],
                                 xr[:, kc * 128:(kc + 1) * 128],
                                 W["wr"][:, kc * D + hh2 * 512: kc * D + hh2 * 512 + 512],
                                 start=(kc == 0), stop=(kc == 7))
                t.fin(p_r, 1)
                t.wait_ge(ld_w, LW["wkf"])
                t.wait_ge(v_xr, i + 1)
                t.wait_ge(v_asm, 2 * (i + 1))
                for mc in range(NCH):
                    for kc in range(NCH):
                        t.matmul(pc[:, mc * 128:(mc + 1) * 128],
                                 W["wkf"][:, kc * D + mc * 128: kc * D + (mc + 1) * 128],
                                 xk[:, kc * 128:(kc + 1) * 128],
                                 start=(kc == 0), stop=(kc == 7))
                t.fin(p_kf, 1)
                if i + 1 < NT:
                    emit_q(t, i + 1)
                    emit_k(t, i + 1)
                t.wait_ge(ld_w, LW["wvf"])
                t.wait_ge(s_krh, i + 1)
                t.wait_ge(s_rth, i + 1)
                for hh2 in range(2):
                    if hh2 == 1:
                        t.wait_ge(s_rt, i + 1)
                    for kc in range(NCH):
                        if hh2 == 0 and kc == 4:
                            t.wait_ge(s_kr, i + 1)
                        t.matmul(pa[:, hh2 * 512:hh2 * 512 + 512],
                                 ksq[:, kc * 128:(kc + 1) * 128],
                                 W["wvf"][:, kc * D + hh2 * 512: kc * D + hh2 * 512 + 512],
                                 start=(kc == 0), stop=(kc == 7))
                t.fin(p_vf, 1)
                if i + 1 < NT:
                    emit_vp(t, i + 1)

        if DEBUG_DUMP:
            dbg_sem = sem("dbg_sem")
            name2sb = dict(vg=vg[0:64, :],
                           h2fv=h2f[:, :].rearrange("p (c w) -> p c w", w=132)[:, :, 0:129],
                           xin0=xin[0], xin1=xin[1], xin2=xin[2], u1=u1, u2=u2,
                           h1f=h1f, dt=dt_, xr=xr, xk=xk, rt=rt,
                           krel=krel, ksq=ksq, ffnf=ffnf, qg=qg, kg=kg,
                           a2u=a2u, vbs=vbs, on_=on_, fma=fma, rinv=rinv,
                           modf=modf, dga=dga, gbT=gbT, sT=sT, h1bf=h1bf,
                           x2b0=x2b_[0], x2b1=x2b_[1], yb0=yb_[0], yb1=yb_[1],
                           rstdb=rstdb, h2bf=h2bf, xbf=xbf, atnb=atnb,
                           x2bf=x2bf, lb_c=lb_c, ln_c=ln_c, ln_s=ln_s)
            dbg_outs = {}
            def _ap(x):
                return x if isinstance(x, bass.AP) else x[:, :]
            for nm in DEBUG_DUMP:
                ap_t = _ap(name2sb[nm])
                dbg_outs[nm] = nc.dram_tensor("dbg_" + nm, list(ap_t.shape), ap_t.dtype,
                                              kind="ExternalOutput").ap()

            @block.sync
            def _(s):
                s.wait_ge(st_y, 16 * NT)
                nd = 0
                for nm in DEBUG_DUMP:
                    s.dma_start(out=dbg_outs[nm], in_=_ap(name2sb[nm])).then_inc(dbg_sem, 16)
                    nd += 1
                s.wait_ge(dbg_sem, 16 * nd)

    return nc


def _host_prep(inputs):
    bf = ml_dtypes.bfloat16
    x = np.ascontiguousarray(np.asarray(inputs["L0_attn_x"], dtype=np.float32))
    B, S, _ = x.shape
    perm = np.concatenate([np.r_[mc * 64:(mc + 1) * 64, (mc + 8) * 64:(mc + 9) * 64]
                           for mc in range(8)])
    wq = np.ascontiguousarray(np.asarray(inputs["L0_attn_Wq"], np.float32)[:, perm])
    wk = np.ascontiguousarray(np.asarray(inputs["L0_attn_Wk"], np.float32)[:, perm])
    wv = np.ascontiguousarray(np.asarray(inputs["L0_attn_Wv"], np.float32)[:, perm])
    bq = np.asarray(inputs["L0_attn_bq"], np.float32)[perm]
    bk = np.asarray(inputs["L0_attn_bk"], np.float32)[perm]
    bv = np.asarray(inputs["L0_attn_bv"], np.float32)[perm]
    wr = np.ascontiguousarray(np.asarray(inputs["L0_ffn_W_r"], np.float32).T)
    wkf = np.ascontiguousarray(np.asarray(inputs["L0_ffn_W_k"], np.float32).T)
    wvf = np.ascontiguousarray(np.asarray(inputs["L0_ffn_W_v"], np.float32).T)
    wmod = np.ascontiguousarray(np.asarray(inputs["L0_wmod"], np.float32))
    cc = np.asarray(inputs["L0_c"], np.float32)
    mur = np.ascontiguousarray(np.asarray(inputs["L0_ffn_mu_r"], np.float32).reshape(8, 128).T)
    muk = np.ascontiguousarray(np.asarray(inputs["L0_ffn_mu_k"], np.float32).reshape(8, 128).T)

    def bias_pat(b):
        p = np.zeros((64, 16), np.float32)
        for mc in range(8):
            p[:, mc] = b[mc * 128: mc * 128 + 64]
            p[:, 8 + mc] = b[mc * 128 + 64: (mc + 1) * 128]
        return p

    consts = np.zeros((128, 16), np.float32)
    for col, val in [(C_EPS, 1e-5), (C_NHALF, -0.5), (C_QSC, 0.125), (C_NSH, -20.0),
                     (C_HALF, 0.5), (C_NEG1, -1.0), (C_RD, 1.0 / 1024.0), (C_ONE, 1.0),
                     (C_QTR, 0.25), (C_NQTR, -0.25), (C_15, 1.5)]:
        consts[:, col] = val
    consts[:, C_QK] = np.full(128, 0x5F3759DF, np.int32).view(np.float32)
    consts[:, C_SH1] = np.full(128, 1, np.int32).view(np.float32)
    idn = np.eye(128, dtype=np.float32).astype(bf)
    qs = np.zeros((64, GSLOT * 128), np.float32)
    ks = np.zeros((64, GSLOT * 128), np.float32)
    col_t = np.arange(GSLOT * 128) % 8
    for tt in range(8):
        qs[tt, col_t == tt] = 1.0
        ks[tt, col_t == tt] = 1024.0
    qs[8, :] = 1.0
    ks[8, :] = -1024.0
    qs = qs.astype(bf)
    ks = ks.astype(bf)
    ones1 = np.ones((1, 128), np.float32)
    ones1b = np.ones((1, 128), np.float32).astype(bf)
    onescol = np.ones((128, 1), np.float32)

    bv_nat = np.asarray(inputs["L0_attn_bv"], np.float32)
    bvt = np.ascontiguousarray(np.repeat(bv_nat.reshape(16, 1, 64), 8, axis=1).reshape(128, 64))
    shared = dict(wq=wq, wk=wk, wv=wv, wr=wr, wkf=wkf, wvf=wvf, wmod=wmod,
                  consts=consts, idn=idn, qstat=qs, kstat=ks, bvt=bvt,
                  bqp=bias_pat(bq), bkp=bias_pat(bk), bvp=bias_pat(bv),
                  mur=mur, muk=muk, ones1=ones1, ones1b=ones1b, onescol=onescol)
    in_maps = []
    for ci in range(8):
        bidx, half = ci // 2, ci % 2
        t0 = half * 2048
        m = dict(shared)
        m["x"] = np.ascontiguousarray(x[bidx, t0:t0 + 2048])
        m["xbnd"] = (np.zeros((1, D), np.float32) if half == 0
                     else np.ascontiguousarray(x[bidx, 2047:2048]))
        m["flag"] = np.full((128, 1), float(half), np.float32)
        m["crow"] = np.ascontiguousarray(cc[bidx:bidx + 1])
        in_maps.append(m)
    return in_maps


def kernel(**inputs):
    in_maps = _host_prep(inputs)
    nc = build_nc()
    res = run_bass_kernel_spmd(nc, in_maps, core_ids=list(range(8)))
    x = np.asarray(inputs["L0_attn_x"])
    B, S, _ = x.shape
    out = np.empty((B, S, D), np.float32)
    for ci in range(8):
        bidx, half = ci // 2, ci % 2
        out[bidx, half * 2048:(half + 1) * 2048] = res.results[ci]["y"]
    return out
